# version 13
# baseline (speedup 1.0000x reference)
import math
import numpy as np
import ml_dtypes
from contextlib import ExitStack

import concourse.bass as bass
import concourse.mybir as mybir
from concourse.bass_utils import run_bass_kernel_spmd

F32 = mybir.dt.float32
BF16 = mybir.dt.bfloat16
I32 = mybir.dt.int32
AF = mybir.ActivationFunctionType
ALU = mybir.AluOpType
AX = mybir.AxisListType.X

B, T, D = 8, 2048, 1024
ND = D // 128
NT = T // 512
TW = 512
EPS = 1e-5

SC_C = 32
SC_NCH = T // SC_C
SC_PB = 512 // SC_C

L1_RFF = 32
RFF_SCALE = math.sqrt(2.0 / (2 * L1_RFF)) * math.sqrt(math.exp(-0.5))
MAGS = [math.sqrt(abs((-1.0) ** m / math.factorial(m)) + 1e-10) for m in range(5)]
NEXP, MOE_HID, SH_HID = 16, 512, 2048

ENG_NAMES = ("t", "v", "s", "g", "sync")


class Buf:
    __slots__ = ("name", "wr", "rd")

    def __init__(self, name):
        self.name = name
        self.wr = None
        self.rd = {}

    def add_rd(self, tok):
        s, v = tok
        if self.rd.get(s, 0) < v:
            self.rd[s] = v


class Ctx:
    def __init__(self, nc):
        self.nc = nc
        self.es = ExitStack()
        self.E = {"t": nc.tensor, "v": nc.vector, "s": nc.scalar, "g": nc.gpsimd,
                  "sync": nc.sync}
        self.sem = {}
        self.cnt = {}
        for s in ("t", "v", "s", "g", "qw", "qb", "qx", "qs", "qA", "qB"):
            self.sem[s] = self.es.enter_context(nc.semaphore(s))
            self.cnt[s] = 0
        self.water = {}
        self.pb = [self.es.enter_context(nc.psum_tensor(f"pb{i}", [128, 512], F32))
                   for i in range(8)]
        self.pbuf = [Buf(f"pb{i}") for i in range(8)]
        self.fresh = [False] * 8
        self.rr = 0
        self.bufs = []

    def sb(self, name, shape, dtype, scope=None):
        h = (scope or self.es).enter_context(self.nc.sbuf_tensor(name, shape, dtype))
        b = Buf(name)
        self.bufs.append(b)
        return h, b

    def wait(self, eng, tok):
        if tok is None:
            return
        s, v = tok
        key = (eng, s)
        if self.water.get(key, 0) >= v:
            return
        self.E[eng].wait_ge(self.sem[s], v)
        self.water[key] = v

    def _wait_rd(self, eng, b):
        self.wait(eng, b.wr)

    def _wait_wr(self, eng, b):
        self.wait(eng, b.wr)
        for s, v in b.rd.items():
            self.wait(eng, (s, v))

    def op(self, eng, emit, reads=(), writes=(), waitw=(), tick=True):
        for b in reads:
            self._wait_rd(eng, b)
        for b in writes:
            self._wait_wr(eng, b)
        for b in waitw:
            self._wait_wr(eng, b)
        instr = emit(self.E[eng])
        tok = None
        if tick:
            self.cnt[eng] += 1
            instr.then_inc(self.sem[eng], 1)
            tok = (eng, self.cnt[eng])
            for b in reads:
                b.add_rd(tok)
            for b in writes:
                b.wr = tok
                b.rd = {}
        return tok

    def dma(self, q, out_ap, in_ap, reads=(), writes=(), **kw):
        eng = {"qw": "sync", "qx": "sync", "qb": "g", "qs": "s",
               "qA": "sync", "qB": "s"}[q]
        for b in reads:
            self._wait_rd(eng, b)
        for b in writes:
            self._wait_wr(eng, b)
        instr = self.E[eng].dma_start(out=out_ap, in_=in_ap, **kw)
        self.cnt[q] += 16
        instr.then_inc(self.sem[q], 16)
        tok = (q, self.cnt[q])
        for b in reads:
            b.add_rd(tok)
        for b in writes:
            b.wr = tok
            b.rd = {}
        return tok

    def bank(self):
        for _ in range(16):
            i = self.rr % 8
            self.rr += 1
            if i not in getattr(self, "reserved", ()):
                self.fresh[i] = True
                return i
        raise RuntimeError("no free psum bank")

    def barrier(self):
        toks = [(s, self.cnt[s]) for s in self.sem]
        for e in ENG_NAMES:
            for tok in toks:
                self.wait(e, tok)
        for b in self.pbuf:
            b.wr = None
            b.rd = {}
        for b in self.bufs:
            b.wr = None
            b.rd = {}


class Module:
    def __init__(self, nc):
        self.nc = nc
        self.ctx = Ctx(nc)
        self.declared = {}

    def dram(self, name, shape, dtype=F32, kind="ExternalInput"):
        ap = self.nc.dram_tensor(name, list(shape), dtype, kind=kind).ap()
        self.declared[name] = (shape, dtype, kind)
        return ap

    def mm(self, bi, out_ap, lhsT, rhs, reads, last=False):
        ctx = self.ctx
        pbuf = ctx.pbuf[bi]
        first = ctx.fresh[bi]
        ctx.fresh[bi] = False
        if not hasattr(ctx, "grp_reads"):
            ctx.grp_reads = [set() for _ in range(8)]
        ctx.grp_reads[bi].update(reads)
        group = list(ctx.grp_reads[bi]) if last else list(reads)
        if last:
            ctx.grp_reads[bi] = set()
        return ctx.op(
            "t",
            lambda E: E.matmul(out_ap, lhsT, rhs, start=first, stop=last,
                               skip_group_check=True),
            reads=group, writes=[pbuf] if last else (),
            waitw=[pbuf] if first else (), tick=last)

    def setup(self):
        ctx = self.ctx
        self.xin = self.dram("xin", (D, T))
        self.xout = self.dram("xout", (D, T), kind="ExternalOutput")
        self.idf_d = self.dram("c_idf", (16, 16))
        self.idb_d = self.dram("c_idb", (128, 128), BF16)
        self.trib_d = self.dram("c_trib", (128, 128), BF16)
        self.ramps_d = self.dram("c_ramps", (2, T))

        self.x, self.xB = ctx.sb("x", [128, ND * T], F32)
        self.bc1, self.bc1B = ctx.sb("bc1", [128, T], F32)
        self.bc2, self.bc2B = ctx.sb("bc2", [128, T], F32)
        self.tmpf, self.tmpfB = ctx.sb("tmpf", [128, T], F32)
        self.stg, self.stgB = ctx.sb("stg", [128, 2048], F32)
        self.stgQB = [self.stgB, Buf("stgq1"), Buf("stgq2"), Buf("stgq3")]
        ctx.bufs += self.stgQB[1:]
        self._stgi = 0
        self.rows, self.rowsB = ctx.sb("rows", [1, 4096], F32)
        self.idf, self.idfB = ctx.sb("idf", [16, 16], F32)
        self.idb, self.idbB = ctx.sb("idb", [128, 128], BF16)
        self.trib, self.tribB = ctx.sb("trib", [128, 128], BF16)
        self.onesf, self.onesfB = ctx.sb("onesf", [1, TW], F32)
        self.onesrb, self.onesrbB = ctx.sb("onesrb", [1, 128], BF16)
        self.onescf, self.onescfB = ctx.sb("onescf", [128, 1], F32)
        self.onescb, self.onescbB = ctx.sb("onescb", [128, 1], BF16)
        self.zcolb, self.zcolbB = ctx.sb("zcolb", [128, 1], BF16)
        self.colp, self.colpB = ctx.sb("colp", [128, 64], F32)
        self.bnf = self.nc.dram_tensor("bounce_f", [24, T], F32, kind="Internal").ap()
        self.bnb = self.nc.dram_tensor("bounce_b", [24, T], BF16, kind="Internal").ap()
        self.bnfB, self.bnbB = Buf("bnf"), Buf("bnb")
        self.ctx.bufs += [self.bnfB, self.bnbB]
        self._slot = 0

        ctx.dma("qw", self.idf[:, :], self.idf_d, writes=[self.idfB])
        ctx.dma("qw", self.idb[:, :], self.idb_d, writes=[self.idbB])
        ctx.dma("qw", self.trib[:, :], self.trib_d, writes=[self.tribB])
        self.cdbt, self.cdbtB = ctx.sb("cdbt", [128, 16], F32)
        for i, v in enumerate((EPS, 2 * math.pi, -math.pi)):
            ctx.op("v", lambda E, i=i, v=v: E.memset(self.cdbt[:, i:i + 1], v),
                   writes=[self.cdbtB])
            self.nc.const_aps.register(F32, v, self.cdbt[:, i:i + 1])
        ctx.op("v", lambda E: E.memset(self.onesf[:, :], 1.0), writes=[self.onesfB])
        ctx.op("v", lambda E: E.memset(self.onesrb[:, :], 1.0), writes=[self.onesrbB])
        ctx.op("v", lambda E: E.memset(self.onescf[:, :], 1.0), writes=[self.onescfB])
        ctx.op("v", lambda E: E.memset(self.onescb[:, :], 1.0), writes=[self.onescbB])
        ctx.op("v", lambda E: E.memset(self.zcolb[:, :], 0.0), writes=[self.zcolbB])
        for dt in range(ND):
            ctx.dma(("qw", "qs", "qb")[dt % 3], self.x[:, dt * T:(dt + 1) * T],
                    self.xin[dt * 128:(dt + 1) * 128, :], writes=[self.xB])

    def finish(self):
        ctx = self.ctx
        qs_used = []
        for dt in range(ND):
            q = ("qx", "qs", "qb")[dt % 3]
            qs_used.append(q)
            ctx.dma(q, self.xout[dt * 128:(dt + 1) * 128, :],
                    self.x[:, dt * T:(dt + 1) * T], reads=[self.xB])
        for q in set(qs_used):
            ctx.wait("sync", (q, ctx.cnt[q]))

    def xs(self, dt, tt=None):
        if tt is None:
            return self.x[:, dt * T:(dt + 1) * T]
        return self.x[:, dt * T + tt * TW: dt * T + (tt + 1) * TW]

    def bcast(self, row_ap, row_buf, dst_ap, dst_buf, nrows=128, width=T, bf16=False):
        ctx = self.ctx
        ones = self.onesrb if bf16 else self.onesf
        onesB = self.onesrbB if bf16 else self.onesfB
        for c0 in range(0, width, 512):
            cw = min(512, width - c0)
            bj = ctx.bank()
            self.mm(bj, ctx.pb[bj][0:nrows, 0:cw], ones[:, 0:nrows], row_ap[:, c0:c0 + cw],
                    [onesB, row_buf], last=True)
            ctx.op("v", lambda E, bj=bj, c0=c0, cw=cw, nrows=nrows: E.tensor_copy(
                out=dst_ap[0:nrows, c0:c0 + cw], in_=ctx.pb[bj][0:nrows, 0:cw]),
                reads=[ctx.pbuf[bj]], writes=[dst_buf])

    def stream_chunks(self, chunks, dst_h, dst_b):
        ctx = self.ctx
        for in_ap, dst_off, cw in chunks:
            si = self._stgi % 4
            self._stgi += 1
            q = ("qw", "qs", "qA", "qB")[si]
            sl = self.stg[:, si * 512: si * 512 + cw]
            ctx.dma(q, sl, in_ap, writes=[self.stgQB[si]])
            ctx.op("s", lambda E, sl=sl, dst_off=dst_off, cw=cw: E.copy(
                out=dst_h[:, dst_off:dst_off + cw], in_=sl),
                reads=[self.stgQB[si]], writes=[dst_b])

    def load_w(self, dram_ap, nk, dout, dst_h, dst_b, dst_off):
        view = dram_ap.rearrange("(k p) d -> p k d", p=128)
        chunks = []
        if dout <= 512:
            kchunk = max(1, 512 // dout)
            k0 = 0
            while k0 < nk:
                kn = min(kchunk, nk - k0)
                chunks.append((view[:, k0:k0 + kn, :], dst_off + k0 * dout, kn * dout))
                k0 += kn
        else:
            nh = (dout + 511) // 512
            for kt in range(nk):
                for ch in range(nh):
                    cw = min(512, dout - ch * 512)
                    chunks.append((view[:, kt, ch * 512:ch * 512 + cw],
                                   dst_off + kt * dout + ch * 512, cw))
        self.stream_chunks(chunks, dst_h, dst_b)

    def emit_ln(self, xn, xnB):
        ctx = self.ctx
        sqv = self.bc1.ap().bitcast(BF16)
        sqvB = self.bc1B

        def xns(dt, tt=None):
            if tt is None:
                return xn[:, dt * T:(dt + 1) * T]
            return xn[:, dt * T + tt * TW: dt * T + (tt + 1) * TW]

        for dt in range(ND):
            ctx.op("s", lambda E, dt=dt: E.copy(out=xns(dt), in_=self.xs(dt)),
                   reads=[self.xB], writes=[xnB])
        for tt in range(NT):
            bs, bq = ctx.bank(), ctx.bank()
            for dt in range(ND):
                ctx.op("s", lambda E, dt=dt, tt=tt: E.activation(
                    out=sqv[:, dt * 512:(dt + 1) * 512], in_=self.xs(dt, tt), func=AF.Square),
                    reads=[self.xB], writes=[sqvB])
            for dt in range(ND):
                self.mm(bs, ctx.pb[bs][0:1, 0:512], self.onescb[:, :], xns(dt, tt),
                        [self.onescbB, xnB], last=(dt == ND - 1))
            for dt in range(ND):
                self.mm(bq, ctx.pb[bq][0:1, 0:512], self.onescb[:, :],
                        sqv[:, dt * 512:(dt + 1) * 512], [self.onescbB, sqvB],
                        last=(dt == ND - 1))
            ctx.op("v", lambda E, tt=tt, bs=bs: E.tensor_copy(
                out=self.rows[:, tt * TW:(tt + 1) * TW], in_=ctx.pb[bs][0:1, 0:512]),
                reads=[ctx.pbuf[bs]], writes=[self.rowsB])
            ctx.op("v", lambda E, tt=tt, bq=bq: E.tensor_copy(
                out=self.rows[:, 2048 + tt * TW:2048 + (tt + 1) * TW],
                in_=ctx.pb[bq][0:1, 0:512]), reads=[ctx.pbuf[bq]], writes=[self.rowsB])
        R0, R1 = 0, 2048
        ctx.op("v", lambda E: E.tensor_scalar(
            out=self.rows[:, R0:R0 + T], in0=self.rows[:, R0:R0 + T], scalar1=1.0 / D,
            scalar2=None, op0=ALU.mult), reads=[self.rowsB], writes=[self.rowsB])
        ctx.op("v", lambda E: E.tensor_scalar(
            out=self.rows[:, R1:R1 + T], in0=self.rows[:, R1:R1 + T], scalar1=1.0 / D,
            scalar2=None, op0=ALU.mult), reads=[self.rowsB], writes=[self.rowsB])
        ctx.op("v", lambda E: E.tensor_tensor(
            out=self.tmpf[0:1, 0:T], in0=self.rows[:, R0:R0 + T], in1=self.rows[:, R0:R0 + T],
            op=ALU.mult), reads=[self.rowsB], writes=[self.tmpfB])
        ctx.op("v", lambda E: E.tensor_tensor(
            out=self.rows[:, R1:R1 + T], in0=self.rows[:, R1:R1 + T], in1=self.tmpf[0:1, 0:T],
            op=ALU.subtract), reads=[self.rowsB, self.tmpfB], writes=[self.rowsB])
        ctx.op("s", lambda E: E.activation(
            out=self.rows[:, R1:R1 + T], in_=self.rows[:, R1:R1 + T], func=AF.Sqrt,
            bias=EPS, scale=1.0), reads=[self.rowsB], writes=[self.rowsB])
        ctx.op("v", lambda E: E.reciprocal(
            out=self.rows[:, R1:R1 + T], in_=self.rows[:, R1:R1 + T]),
            reads=[self.rowsB], writes=[self.rowsB])
        ctx.op("v", lambda E: E.tensor_tensor(
            out=self.rows[:, R0:R0 + T], in0=self.rows[:, R0:R0 + T],
            in1=self.rows[:, R1:R1 + T], op=ALU.mult), reads=[self.rowsB], writes=[self.rowsB])
        self.bcast(self.rows[:, R1:R1 + T], self.rowsB, self.bc1[:, :], self.bc1B)
        self.bcast(self.rows[:, R0:R0 + T], self.rowsB, self.bc2[:, :], self.bc2B)
        for dt in range(ND):
            ctx.op("v", lambda E, dt=dt: E.tensor_tensor(
                out=self.tmpf[:, :], in0=self.xs(dt), in1=self.bc1[:, :], op=ALU.mult),
                reads=[self.xB, self.bc1B], writes=[self.tmpfB])
            ctx.op("v", lambda E, dt=dt: E.tensor_tensor(
                out=xns(dt), in0=self.tmpf[:, :], in1=self.bc2[:, :], op=ALU.subtract),
                reads=[self.tmpfB, self.bc2B], writes=[xnB])

    def emit_l0_attn(self, scope):
        ctx = self.ctx
        w_s1 = self.dram("L0a_s_w1T", (D, 512))
        w_s2 = self.dram("L0a_s_w2T", (512, 128))
        w_B = self.dram("L0a_B_wT", (D, 128))
        w_C = self.dram("L0a_C_wT", (128, D))
        w_D = self.dram("L0a_D_wT", (D, D))
        w_g = self.dram("L0a_g_wT", (D, D))
        a_d = self.dram("L0a_A", (128, 128))
        biases = {}
        for nm, wd in (("s_b1", 512), ("s_b2", 128), ("B_b", 128), ("C_b", D),
                       ("D_b", D), ("g_b", D)):
            biases[nm] = self.dram("L0a_" + nm, (1, wd))

        xn, xnB = ctx.sb("xn0", [128, ND * T], BF16, scope)
        wl, wlB = ctx.sb("wl0a", [128, 17408], BF16, scope)
        O_S1, O_S2, O_B, O_C = 0, 4096, 4608, 16384
        O_D, O_G = 0, 8192
        s1t, s1tB = ctx.sb("s1t", [128, 2048], BF16, scope)
        sgb, sgbB = ctx.sb("sgb", [128, T], BF16, scope)
        ub, ubB = ctx.sb("ub", [128, T], BF16, scope)
        hsb, hsbB = ctx.sb("hsb", [128, T], BF16, scope)
        qstk, qstkB = ctx.sb("qstk", [128, 39 * 128], BF16, scope)
        esb, esbB = ctx.sb("esb", [128, SC_NCH], BF16, scope)
        abm, abmB = ctx.sb("abm", [128, 128], BF16, scope)
        rtm, rtmB = ctx.sb("rtm", [128, 128], BF16, scope)
        gt, gtB = ctx.sb("gt", [128, TW], BF16, scope)
        yt, ytB = ctx.sb("yt", [128, TW], BF16, scope)
        rowb, rowbB = ctx.sb("rowb0", [1, 2816], F32, scope)
        RO = {"s_b1": 0, "s_b2": 512, "B_b": 640, "C_b": 768, "g_b": 1792}

        self.load_w(w_s1, 8, 512, wl, wlB, O_S1)
        self.load_w(w_s2, 4, 128, wl, wlB, O_S2)
        self.load_w(w_B, 8, 128, wl, wlB, O_B)
        self.load_w(w_C, 1, D, wl, wlB, O_C)
        for nm in RO:
            w = {"s_b1": 512, "s_b2": 128, "B_b": 128, "C_b": D, "g_b": D}[nm]
            ctx.dma("qw", rowb[:, RO[nm]:RO[nm] + w], biases[nm], writes=[rowbB])
        ctx.dma("qw", self.tmpf[0:1, 0:D], biases["D_b"], writes=[self.tmpfB])
        ctx.op("v", lambda E: E.tensor_tensor(
            out=rowb[:, RO["C_b"]:RO["C_b"] + D], in0=rowb[:, RO["C_b"]:RO["C_b"] + D],
            in1=self.tmpf[0:1, 0:D], op=ALU.add), reads=[rowbB, self.tmpfB], writes=[rowbB])
        ctx.dma("qw", self.stg[:, 0:128], a_d, writes=[self.stgB])
        ctx.op("g", lambda E: E.tensor_copy(out=abm[:, :], in_=self.stg[:, 0:128]),
               reads=[self.stgB], writes=[abmB])

        self.emit_ln(xn, xnB)

        def xns(dt, tt):
            return xn[:, dt * T + tt * TW: dt * T + (tt + 1) * TW]

        for tt in range(NT):
            for ht in range(4):
                bi = ctx.bank()
                for kt in range(8):
                    self.mm(bi, ctx.pb[bi][:, 0:TW],
                            wl[:, O_S1 + kt * 512 + ht * 128: O_S1 + kt * 512 + ht * 128 + 128],
                            xns(kt, tt), [wlB, xnB])
                self.mm(bi, ctx.pb[bi][:, 0:TW], rowb[:, RO["s_b1"] + ht * 128:RO["s_b1"] + ht * 128 + 128],
                        self.onesf[:, :], [rowbB, self.onesfB], last=True)
                ctx.op("s", lambda E, ht=ht, bi=bi: E.activation(
                    out=s1t[:, ht * 512:(ht + 1) * 512], in_=ctx.pb[bi][:, 0:TW], func=AF.Silu),
                    reads=[ctx.pbuf[bi]], writes=[s1tB])
            bi = ctx.bank()
            for kt in range(4):
                self.mm(bi, ctx.pb[bi][:, 0:TW],
                        wl[:, O_S2 + kt * 128: O_S2 + (kt + 1) * 128],
                        s1t[:, kt * 512:(kt + 1) * 512], [wlB, s1tB])
            self.mm(bi, ctx.pb[bi][:, 0:TW], rowb[:, RO["s_b2"]:RO["s_b2"] + 128],
                    self.onesf[:, :], [rowbB, self.onesfB], last=True)
            ctx.op("s", lambda E, tt=tt, bi=bi: E.activation(
                out=sgb[:, tt * TW:(tt + 1) * TW], in_=ctx.pb[bi][:, 0:TW], func=AF.Sigmoid),
                reads=[ctx.pbuf[bi]], writes=[sgbB])
            bi = ctx.bank()
            for kt in range(8):
                self.mm(bi, ctx.pb[bi][:, 0:TW],
                        wl[:, O_B + kt * 128: O_B + (kt + 1) * 128], xns(kt, tt), [wlB, xnB])
            self.mm(bi, ctx.pb[bi][:, 0:TW], rowb[:, RO["B_b"]:RO["B_b"] + 128],
                    self.onesf[:, :], [rowbB, self.onesfB], last=True)
            ctx.op("v", lambda E, tt=tt, bi=bi: E.tensor_tensor(
                out=ub[:, tt * TW:(tt + 1) * TW], in0=ctx.pb[bi][:, 0:TW],
                in1=sgb[:, tt * TW:(tt + 1) * TW], op=ALU.mult),
                reads=[ctx.pbuf[bi], sgbB], writes=[ubB])

        self.load_w(w_D, 8, D, wl, wlB, O_D)
        self.load_w(w_g, 8, D, wl, wlB, O_G)

        self._emit_scan(abm, abmB, rtm, rtmB, qstk, qstkB, esb, esbB, ub, ubB, hsb, hsbB)
        if DBG:
            dq = self.dram("dbg_q", (128, 1024), kind="ExternalOutput")
            self.ctx.op("v", lambda E: E.tensor_copy(out=self.tmpf[:, 0:1024], in_=qstk[:, 0:1024]),
                        reads=[qstkB], writes=[self.tmpfB])
            self.ctx.dma("qx", dq, self.tmpf[:, 0:1024], reads=[self.tmpfB])
            de = self.dram("dbg_e", (128, 64), kind="ExternalOutput")
            self.ctx.op("v", lambda E: E.tensor_copy(out=self.tmpf[:, 1024:1088], in_=esb[:, :]),
                        reads=[esbB], writes=[self.tmpfB])
            self.ctx.dma("qx", de, self.tmpf[:, 1024:1088], reads=[self.tmpfB])
            for nm, th, tb in (("dbg_u", ub, ubB), ("dbg_hs", hsb, hsbB), ("dbg_s", sgb, sgbB)):
                dap = self.dram(nm, (128, T), kind="ExternalOutput")
                self.ctx.op("v", lambda E, th=th: E.tensor_copy(out=self.tmpf[:, :], in_=th[:, :]),
                            reads=[tb], writes=[self.tmpfB])
                self.ctx.dma("qx", dap, self.tmpf[:, :], reads=[self.tmpfB])

        for dt in range(ND):
            for tt in range(NT):
                bi = ctx.bank()
                self.mm(bi, ctx.pb[bi][:, 0:TW], wl[:, O_C + dt * 128: O_C + dt * 128 + 128],
                        hsb[:, tt * TW:(tt + 1) * TW], [wlB, hsbB])
                for kt in range(8):
                    self.mm(bi, ctx.pb[bi][:, 0:TW],
                            wl[:, O_D + kt * D + dt * 128: O_D + kt * D + dt * 128 + 128],
                            xns(kt, tt), [wlB, xnB])
                self.mm(bi, ctx.pb[bi][:, 0:TW], rowb[:, RO["C_b"] + dt * 128:RO["C_b"] + dt * 128 + 128],
                        self.onesf[:, :], [rowbB, self.onesfB], last=True)
                bg = ctx.bank()
                for kt in range(8):
                    self.mm(bg, ctx.pb[bg][:, 0:TW],
                            wl[:, O_G + kt * D + dt * 128: O_G + kt * D + dt * 128 + 128],
                            xns(kt, tt), [wlB, xnB])
                self.mm(bg, ctx.pb[bg][:, 0:TW], rowb[:, RO["g_b"] + dt * 128:RO["g_b"] + dt * 128 + 128],
                        self.onesf[:, :], [rowbB, self.onesfB], last=True)
                ctx.op("s", lambda E, bg=bg: E.activation(
                    out=gt[:, :], in_=ctx.pb[bg][:, 0:TW], func=AF.Sigmoid),
                    reads=[ctx.pbuf[bg]], writes=[gtB])
                ctx.op("v", lambda E, bi=bi: E.tensor_tensor(
                    out=yt[:, :], in0=ctx.pb[bi][:, 0:TW], in1=gt[:, :], op=ALU.mult),
                    reads=[ctx.pbuf[bi], gtB], writes=[ytB])
                ctx.op("v", lambda E, dt=dt, tt=tt: E.tensor_tensor(
                    out=self.xs(dt, tt), in0=self.xs(dt, tt), in1=yt[:, :], op=ALU.add),
                    reads=[ytB], writes=[self.xB])
        ctx.barrier()

    def _emit_scan(self, abm, abmB, rtm, rtmB, qstk, qstkB, esb, esbB, ub, ubB, hsb, hsbB):
        ctx = self.ctx

        def q(i):
            return qstk[:, i * 128:(i + 1) * 128]

        def pv(bi):
            return ctx.pb[bi].ap().bitcast(BF16)

        ctx.op("v", lambda E: E.tensor_copy(out=q(0), in_=self.idb[:, :]),
               reads=[self.idbB], writes=[qstkB])
        bi = ctx.bank()
        ctx.op("t", lambda E, bi=bi: E.transpose(pv(bi)[:, 0:128], abm[:, :], self.idb[:, :]),
               reads=[abmB, self.idbB], waitw=[ctx.pbuf[bi]], writes=[ctx.pbuf[bi]])
        ctx.op("v", lambda E, bi=bi: E.tensor_copy(out=q(1), in_=pv(bi)[:, 0:128]),
               reads=[ctx.pbuf[bi]], writes=[qstkB])

        def prod_block(src_i, rhs_lo, rhs_n, dst_i):
            bj = ctx.bank()
            ctx.op("t", lambda E, bj=bj: E.transpose(pv(bj)[:, 0:128], q(src_i), self.idb[:, :]),
                   reads=[qstkB, self.idbB], waitw=[ctx.pbuf[bj]], writes=[ctx.pbuf[bj]])
            ctx.op("v", lambda E, bj=bj: E.tensor_copy(out=rtm[:, :], in_=pv(bj)[:, 0:128]),
                   reads=[ctx.pbuf[bj]], writes=[rtmB])
            done = 0
            while done < rhs_n:
                cnt = min(4, rhs_n - done)
                bk = ctx.bank()
                self.mm(bk, ctx.pb[bk][:, 0:cnt * 128], rtm[:, :],
                        qstk[:, (rhs_lo + done) * 128:(rhs_lo + done + cnt) * 128],
                        [rtmB, qstkB], last=True)
                ctx.op("v", lambda E, bk=bk, done=done, cnt=cnt: E.tensor_copy(
                    out=qstk[:, (dst_i + done) * 128:(dst_i + done + cnt) * 128],
                    in_=ctx.pb[bk][:, 0:cnt * 128]), reads=[ctx.pbuf[bk]], writes=[qstkB])
                done += cnt

        for s in (1, 2, 4, 8, 16):
            prod_block(s, 1, min(s, SC_C - s), s + 1)
        prev = 32
        for k in range(1, 7):
            prod_block(prev, prev, 1, 32 + k)
            prev = 32 + k

        sb_banks = [ctx.bank() for _ in range(4)]
        ctx.reserved = set(sb_banks)
        for g, bi in enumerate(sb_banks):
            pb = ctx.pb[bi]
            for m in range(SC_C):
                w = SC_C - m
                out_ap = bass.AP(pb[:, :].tensor, m, [[512, 128], [SC_C, SC_PB], [1, w]])
                in_ap = bass.AP(ub[:, :].tensor, g * 512, [[T, 128], [SC_C, SC_PB], [1, w]])
                self.mm(bi, out_ap, q(m), in_ap, [qstkB, ubB], last=(m == SC_C - 1))
            e_ap = bass.AP(pb[:, :].tensor, SC_C - 1, [[512, 128], [SC_C, SC_PB]])
            ctx.op("v", lambda E, e_ap=e_ap, g=g: E.tensor_copy(
                out=esb[:, g * SC_PB:(g + 1) * SC_PB], in_=e_ap),
                reads=[ctx.pbuf[bi]], writes=[esbB])

        k = 0
        s = 1
        while s < SC_NCH:
            gi = 32 if k == 0 else 32 + k
            bj = ctx.bank()
            ctx.op("t", lambda E, bj=bj, gi=gi: E.transpose(
                pv(bj)[:, 0:128], q(gi), self.idb[:, :]),
                reads=[qstkB, self.idbB], waitw=[ctx.pbuf[bj]], writes=[ctx.pbuf[bj]])
            ctx.op("v", lambda E, bj=bj: E.tensor_copy(out=rtm[:, :], in_=pv(bj)[:, 0:128]),
                   reads=[ctx.pbuf[bj]], writes=[rtmB])
            bk = ctx.bank()
            self.mm(bk, ctx.pb[bk][:, 0:SC_NCH], self.idb[:, :], esb[:, 0:SC_NCH],
                    [self.idbB, esbB])
            self.mm(bk, ctx.pb[bk][:, s:SC_NCH], rtm[:, :], esb[:, 0:SC_NCH - s],
                    [rtmB, esbB], last=True)
            ctx.op("v", lambda E, bk=bk: E.tensor_copy(out=esb[:, :], in_=ctx.pb[bk][:, 0:SC_NCH]),
                   reads=[ctx.pbuf[bk]], writes=[esbB])
            s *= 2
            k += 1

        for g, bi in enumerate(sb_banks if not SCAN_NO_CARRY else []):
            pb = ctx.pb[bi]
            pbuf = ctx.pbuf[bi]
            firstmm = True
            for j in range(SC_C):
                if g == 0:
                    out_ap = bass.AP(pb[:, :].tensor, SC_C + j, [[512, 128], [SC_C, SC_PB - 1]])
                    rhs = esb[:, 0:SC_PB - 1]
                else:
                    out_ap = bass.AP(pb[:, :].tensor, j, [[512, 128], [SC_C, SC_PB]])
                    rhs = esb[:, g * SC_PB - 1: (g + 1) * SC_PB - 1]
                tok = ctx.op(
                    "t", lambda E, out_ap=out_ap, rhs=rhs, j=j: E.matmul(
                        out_ap, q(j + 1), rhs, start=False, stop=(j == SC_C - 1),
                        skip_group_check=True),
                    reads=[qstkB, esbB], writes=[pbuf] if j == SC_C - 1 else (),
                    waitw=[pbuf] if firstmm else (), tick=(j == SC_C - 1))
                firstmm = False
        for g, bi in enumerate(sb_banks):
            ctx.op("v", lambda E, g=g, bi=bi: E.tensor_copy(
                out=hsb[:, g * 512:(g + 1) * 512], in_=ctx.pb[bi][:, 0:512]),
                reads=[ctx.pbuf[bi]], writes=[hsbB])
        ctx.reserved = set()

    def emit_l0_ffn(self, scope):
        nc, ctx = self.nc, self.ctx
        w_rt = self.dram("L0f_WgT", (D, 16))
        w_eg = self.dram("L0f_We_gateT", (NEXP, D, MOE_HID))
        w_eu = self.dram("L0f_We_upT", (NEXP, D, MOE_HID))
        w_ed = self.dram("L0f_We_downT", (NEXP, MOE_HID, D))
        w_sg = self.dram("L0f_Ws_gateT", (D, SH_HID))
        w_su = self.dram("L0f_Ws_upT", (D, SH_HID))
        w_sd = self.dram("L0f_Ws_downT", (SH_HID, D))
        w_x = self.dram("L0f_WsgT", (128, 8))
        mask_d = nc.dram_tensor("moe_mask", [NEXP, T], BF16, kind="Internal").ap()
        mask_dB = Buf("mask_d")
        ctx.bufs.append(mask_dB)

        xn, xnB = ctx.sb("xn1", [128, ND * T], BF16, scope)
        wrtf, wrtfB = ctx.sb("wrtf", [128, 128], F32, scope)
        wxf, wxfB = ctx.sb("wxf", [128, 8], BF16, scope)
        wex, wexB = ctx.sb("wex", [128, 24576], BF16, scope)
        wexHB = [Buf("wexh0"), Buf("wexh1")]
        ctx.bufs += wexHB
        hb, hbB = ctx.sb("hb", [128, 8192], BF16, scope)
        hbTB = [Buf(f"hbt{i}") for i in range(NT)]
        ctx.bufs += hbTB
        ltok, ltokB = ctx.sb("ltok", [128, 16], F32, scope)
        mtok, mtokB = ctx.sb("mtok", [128, 16], BF16, scope)
        mfmx, mfmB = ctx.sb("mfm", [128, T], BF16, scope)
        mfm = mfmx[0:16, 0:T]
        maskb0, maskbB0 = ctx.sb("maskb", [128, T], BF16, scope)
        MASKB = [(maskb0, maskbB0), (mfmx, mfmB)]
        sgt, sgtB = ctx.sb("sgt", [128, TW], BF16, scope)
        hut, hutB = ctx.sb("hut", [128, TW], BF16, scope)
        lgf = self.tmpf[0:16, 0:T]
        lgfB = self.tmpfB

        ctx.dma("qw", wrtf[:, :], w_rt.rearrange("(k p) d -> p k d", p=128)[:, :, :],
                writes=[wrtfB])
        ctx.dma("qw", self.stg[:, 0:8], w_x, writes=[self.stgB])
        ctx.op("g", lambda E: E.tensor_copy(out=wxf[:, :], in_=self.stg[:, 0:8]),
               reads=[self.stgB], writes=[wxfB])

        self.emit_ln(xn, xnB)

        def xns(dt, tt):
            return xn[:, dt * T + tt * TW: dt * T + (tt + 1) * TW]

        bi = ctx.bank()
        for kt in range(8):
            self.mm(bi, ctx.pb[bi][0:16, 0:1], wrtf[:, kt * 16:(kt + 1) * 16], self.onescf[:, :],
                    [self.onescfB, wrtfB], last=(kt == 7))
        ctx.op("v", lambda E, bi=bi: E.tensor_scalar(
            out=self.colp[0:16, 0:1], in0=ctx.pb[bi][0:16, 0:1], scalar1=-1.0, scalar2=None,
            op0=ALU.mult), reads=[ctx.pbuf[bi]], writes=[self.colpB])

        for tt in range(NT):
            bi = ctx.bank()
            for kt in range(8):
                self.mm(bi, ctx.pb[bi][0:16, 0:TW], wrtf[:, kt * 16:(kt + 1) * 16],
                        self.xs(kt, tt), [wrtfB, self.xB], last=(kt == 7))
            ctx.op("v", lambda E, tt=tt, bi=bi: E.tensor_tensor(
                out=lgf[:, tt * TW:(tt + 1) * TW], in0=ctx.pb[bi][0:16, 0:TW],
                in1=self.bc1[0:16, tt * TW:(tt + 1) * TW], op=ALU.mult),
                reads=[ctx.pbuf[bi], self.bc1B], writes=[lgfB])
            ctx.op("v", lambda E, tt=tt: E.scalar_tensor_tensor(
                out=lgf[:, tt * TW:(tt + 1) * TW], in0=self.bc2[0:16, tt * TW:(tt + 1) * TW],
                scalar=self.colp[0:16, 0:1], in1=lgf[:, tt * TW:(tt + 1) * TW],
                op0=ALU.mult, op1=ALU.add),
                reads=[self.bc2B, self.colpB, lgfB], writes=[lgfB])

        for i in range(16):
            bi = ctx.bank()
            pb, pbuf = ctx.pb[bi], ctx.pbuf[bi]
            ctx.op("t", lambda E, i=i, pb=pb: E.transpose(
                pb[:, 0:16], lgf[:, i * 128:(i + 1) * 128], self.idf[0:16, 0:16]),
                reads=[lgfB, self.idfB], waitw=[pbuf], writes=[pbuf])
            ctx.op("v", lambda E, pb=pb: E.tensor_copy(
                out=ltok[:, 0:16], in_=pb[:, 0:16]),
                reads=[pbuf], writes=[ltokB])
            L = ltok[:, 0:16]
            ctx.op("v", lambda E, L=L: E.reduce_max(out=self.colp[:, 1:2], in_=L, axis=AX),
                   reads=[ltokB], writes=[self.colpB])
            ctx.op("v", lambda E, L=L: E.tensor_scalar(
                out=self.colp[:, 8:24], in0=L, scalar1=self.colp[:, 1:2], scalar2=None,
                op0=ALU.is_equal), reads=[ltokB, self.colpB], writes=[self.colpB])
            ctx.op("v", lambda E, L=L: E.scalar_tensor_tensor(
                out=self.colp[:, 24:40], in0=self.colp[:, 8:24], scalar=-1e30, in1=L,
                op0=ALU.mult, op1=ALU.add), reads=[self.colpB, ltokB], writes=[self.colpB])
            ctx.op("v", lambda E: E.reduce_max(out=self.colp[:, 2:3], in_=self.colp[:, 24:40],
                                               axis=AX), reads=[self.colpB], writes=[self.colpB])
            ctx.op("v", lambda E: E.tensor_scalar(
                out=self.colp[:, 40:56], in0=self.colp[:, 24:40], scalar1=self.colp[:, 2:3],
                scalar2=None, op0=ALU.is_equal), reads=[self.colpB], writes=[self.colpB])
            ctx.op("v", lambda E: E.tensor_tensor(
                out=self.colp[:, 5:6], in0=self.colp[:, 1:2], in1=self.colp[:, 2:3],
                op=ALU.subtract), reads=[self.colpB], writes=[self.colpB])
            ctx.op("s", lambda E: E.activation(out=self.colp[:, 3:4], in_=self.colp[:, 5:6],
                                               func=AF.Sigmoid),
                   reads=[self.colpB], writes=[self.colpB])
            ctx.op("v", lambda E: E.tensor_scalar(
                out=self.colp[:, 4:5], in0=self.colp[:, 3:4], scalar1=-1.0, scalar2=1.0,
                op0=ALU.mult, op1=ALU.add), reads=[self.colpB], writes=[self.colpB])
            ctx.op("v", lambda E: E.tensor_scalar(
                out=self.colp[:, 8:24], in0=self.colp[:, 8:24], scalar1=self.colp[:, 3:4],
                scalar2=None, op0=ALU.mult), reads=[self.colpB], writes=[self.colpB])
            ctx.op("v", lambda E: E.scalar_tensor_tensor(
                out=mtok[:, 0:16], in0=self.colp[:, 40:56], scalar=self.colp[:, 4:5],
                in1=self.colp[:, 8:24], op0=ALU.mult, op1=ALU.add),
                reads=[self.colpB], writes=[mtokB])
            bj = ctx.bank()
            pbj, pbjB = ctx.pb[bj], ctx.pbuf[bj]
            ctx.op("t", lambda E, pbj=pbj: E.transpose(
                pbj.ap().bitcast(BF16)[0:16, 0:128], mtok[:, 0:16], self.idb[:, :]),
                reads=[mtokB, self.idbB], waitw=[pbjB], writes=[pbjB])
            ctx.op("v", lambda E, i=i, pbj=pbj: E.tensor_copy(
                out=mfm[:, i * 128:(i + 1) * 128], in_=pbj.ap().bitcast(BF16)[0:16, 0:128]),
                reads=[pbjB], writes=[mfmB])
        ctx.dma("qb", mask_d, mfm[:, :], reads=[mfmB], writes=[mask_dB])
        if DBG:
            dm = self.dram("dbg_mask", (16, T), kind="ExternalOutput")
            self.ctx.op("v", lambda E: E.tensor_copy(out=self.tmpf[32:48, 0:T], in_=mfm[:, :]),
                        reads=[mfmB], writes=[self.tmpfB])
            self.ctx.dma("qx", dm, self.tmpf[32:48, 0:T], reads=[self.tmpfB])
            dl = self.dram("dbg_lg", (16, T), kind="ExternalOutput")
            self.ctx.dma("qx", dl, lgf, reads=[lgfB])

        for tt in range(NT):
            bi = ctx.bank()
            for kt in range(8):
                self.mm(bi, ctx.pb[bi][0:1, 0:TW], wxf[:, kt:kt + 1], xns(kt, tt),
                        [wxfB, xnB], last=(kt == 7))
            ctx.op("s", lambda E, tt=tt, bi=bi: E.activation(
                out=self.rows[:, tt * TW:(tt + 1) * TW], in_=ctx.pb[bi][0:1, 0:TW],
                func=AF.Sigmoid), reads=[ctx.pbuf[bi]], writes=[self.rowsB])

        def load_expert(e, half):
            off = half * 12288
            hbuf = wexHB[half]
            chunks = []
            for j, wsrc in enumerate((w_eg, w_eu, w_ed)):
                view = wsrc[e].rearrange("(k p) d -> p k d", p=128)
                if j < 2:
                    for kt in range(8):
                        chunks.append((view[:, kt, :], off + j * 4096 + kt * 512, 512))
                else:
                    for kt in range(4):
                        for ch in range(2):
                            chunks.append((view[:, kt, ch * 512:(ch + 1) * 512],
                                           off + 8192 + kt * 1024 + ch * 512, 512))
            self.stream_chunks(chunks, wex, hbuf)

        def mrow_ap(e):
            return self.rows.ap().bitcast(BF16)[0:1, 4096 + (e % 2) * 2048: 4096 + (e % 2) * 2048 + T]

        mrowB = [Buf("mrow0"), Buf("mrow1")]
        ctx.bufs += mrowB
        _elist = ([] if MOE_NO_EXPERTS else ([MOE_ONE_EXPERT] if MOE_ONE_EXPERT is not None else list(range(MOE_NUM))))
        if _elist:
            ctx.dma("qb", mrow_ap(_elist[0]), mask_d[_elist[0]:_elist[0] + 1, :],
                    reads=[mask_dB], writes=[mrowB[_elist[0] % 2]])
        if _elist and not MOE_SERIAL:
            load_expert(_elist[0], _elist[0] % 2)
        for e in _elist:
            half = e % 2
            off = half * 12288
            hbuf = wexHB[half]
            maskb, maskbB = MASKB[half]
            if e + 1 < NEXP and len(_elist) > 1:
                ctx.dma("qb", mrow_ap(e + 1), mask_d[e + 1:e + 2, :], reads=[mask_dB],
                        writes=[mrowB[(e + 1) % 2]])
            self.bcast(mrow_ap(e), mrowB[e % 2], maskb[:, :], maskbB, bf16=True)
            if MOE_SERIAL:
                load_expert(e, half)
            elif e + 1 < NEXP and len(_elist) > 1:
                load_expert(e + 1, 1 - half)
            for tt in range(NT):
                for ht in range(4):
                    bg_, bu_ = ctx.bank(), ctx.bank()
                    for kt in range(8):
                        self.mm(bg_, ctx.pb[bg_][:, 0:TW],
                                wex[:, off + kt * MOE_HID + ht * 128: off + kt * MOE_HID + ht * 128 + 128],
                                xns(kt, tt), [hbuf, xnB], last=(kt == 7))
                    for kt in range(8):
                        self.mm(bu_, ctx.pb[bu_][:, 0:TW],
                                wex[:, off + 4096 + kt * MOE_HID + ht * 128: off + 4096 + kt * MOE_HID + ht * 128 + 128],
                                xns(kt, tt), [hbuf, xnB], last=(kt == 7))
                    ctx.op("s", lambda E, bg_=bg_: E.activation(
                        out=sgt[:, :], in_=ctx.pb[bg_][:, 0:TW], func=AF.Silu),
                        reads=[ctx.pbuf[bg_]], writes=[sgtB])
                    ctx.op("v", lambda E, tt=tt, bu_=bu_: E.tensor_tensor(
                        out=hut[:, :], in0=ctx.pb[bu_][:, 0:TW],
                        in1=maskb[:, tt * TW:(tt + 1) * TW], op=ALU.mult),
                        reads=[ctx.pbuf[bu_], maskbB], writes=[hutB])
                    ctx.op("v", lambda E, tt=tt, ht=ht: E.tensor_tensor(
                        out=hb[:, ht * T + tt * TW: ht * T + (tt + 1) * TW], in0=sgt[:, :],
                        in1=hut[:, :], op=ALU.mult), reads=[sgtB, hutB], writes=[hbTB[tt]])
                for dto in range(ND):
                    by = ctx.bank()
                    for kt in range(4):
                        self.mm(by, ctx.pb[by][:, 0:TW],
                                wex[:, off + 8192 + kt * D + dto * 128: off + 8192 + kt * D + dto * 128 + 128],
                                hb[:, kt * T + tt * TW: kt * T + (tt + 1) * TW],
                                [hbuf, hbTB[tt]], last=(kt == 3))
                    ctx.op("v", lambda E, dto=dto, tt=tt, by=by: E.tensor_tensor(
                        out=self.xs(dto, tt), in0=self.xs(dto, tt), in1=ctx.pb[by][:, 0:TW],
                        op=ALU.add), reads=[ctx.pbuf[by]], writes=[self.xB])

        self.bcast(self.rows[:, 0:T], self.rowsB, self.bc1[:, :], self.bc1B)
        def load_shared(qq, shalf):
            soff = shalf * 12288
            chunks = []
            for j, wsrc in enumerate((w_sg, w_su)):
                view = wsrc.rearrange("(k p) d -> p k d", p=128)
                for kt in range(8):
                    chunks.append((view[:, kt, qq * 512: qq * 512 + 512],
                                   soff + j * 4096 + kt * 512, 512))
            viewd = w_sd.rearrange("(k p) d -> p k d", p=128)
            for kt in range(4):
                for ch in range(2):
                    chunks.append((viewd[:, qq * 4 + kt, ch * 512:(ch + 1) * 512],
                                   soff + 8192 + kt * 1024 + ch * 512, 512))
            self.stream_chunks(chunks, wex, wexHB[shalf])

        if not MOE_NO_SHARED:
            load_shared(0, 0)
        for qq in (range(4) if not MOE_NO_SHARED else []):
            shalf = qq % 2
            soff = shalf * 12288
            sbuf_w = wexHB[shalf]
            if qq + 1 < 4:
                load_shared(qq + 1, 1 - shalf)
            for tt in range(NT):
                for ht in range(4):
                    bg_, bu_ = ctx.bank(), ctx.bank()
                    for kt in range(8):
                        self.mm(bg_, ctx.pb[bg_][:, 0:TW],
                                wex[:, soff + kt * 512 + ht * 128: soff + kt * 512 + ht * 128 + 128],
                                xns(kt, tt), [sbuf_w, xnB], last=(kt == 7))
                    for kt in range(8):
                        self.mm(bu_, ctx.pb[bu_][:, 0:TW],
                                wex[:, soff + 4096 + kt * 512 + ht * 128: soff + 4096 + kt * 512 + ht * 128 + 128],
                                xns(kt, tt), [sbuf_w, xnB], last=(kt == 7))
                    ctx.op("s", lambda E, bg_=bg_: E.activation(
                        out=sgt[:, :], in_=ctx.pb[bg_][:, 0:TW], func=AF.Silu),
                        reads=[ctx.pbuf[bg_]], writes=[sgtB])
                    ctx.op("v", lambda E, tt=tt, bu_=bu_: E.tensor_tensor(
                        out=hut[:, :], in0=ctx.pb[bu_][:, 0:TW],
                        in1=self.bc1[:, tt * TW:(tt + 1) * TW], op=ALU.mult),
                        reads=[ctx.pbuf[bu_], self.bc1B], writes=[hutB])
                    ctx.op("v", lambda E, tt=tt, ht=ht: E.tensor_tensor(
                        out=hb[:, ht * T + tt * TW: ht * T + (tt + 1) * TW], in0=sgt[:, :],
                        in1=hut[:, :], op=ALU.mult), reads=[sgtB, hutB], writes=[hbTB[tt]])
                for dto in range(ND):
                    by = ctx.bank()
                    for kt in range(4):
                        self.mm(by, ctx.pb[by][:, 0:TW],
                                wex[:, soff + 8192 + kt * D + dto * 128: soff + 8192 + kt * D + dto * 128 + 128],
                                hb[:, kt * T + tt * TW: kt * T + (tt + 1) * TW],
                                [sbuf_w, hbTB[tt]], last=(kt == 3))
                    ctx.op("v", lambda E, dto=dto, tt=tt, by=by: E.tensor_tensor(
                        out=self.xs(dto, tt), in0=self.xs(dto, tt), in1=ctx.pb[by][:, 0:TW],
                        op=ALU.add), reads=[ctx.pbuf[by]], writes=[self.xB])
        ctx.barrier()

    def emit_l1_attn(self, scope):
        ctx = self.ctx
        w_ch = self.dram("L1a_W_chargeT", (128, 8))
        w_sh = self.dram("L1a_W_shellT", (D, 64))
        w_om = self.dram("L1a_omegaT", (64, 32))
        w_cmb = self.dram("L1a_W_combineT", (320, 64))
        w_v = self.dram("L1a_W_vT", (D, D))
        w_o = self.dram("L1a_W_outT", (D, D))
        bv_c = self.dram("L1a_b_v_col", (128, 9))
        r_bch = self.dram("L1a_b_charge", (1, 1))
        r_bsh = self.dram("L1a_b_shell", (1, 64))
        r_rff = self.dram("L1a_rff_bias", (1, 32))
        r_bo = self.dram("L1a_b_out", (1, D))
        r_lt = self.dram("L1a_log_temp", (1, 1))

        vsb, vsbB = ctx.sb("vsb", [128, 16 * 1024], BF16, scope)
        phq, phqB = ctx.sb("phq", [64, T], BF16, scope)
        phk, phkB = ctx.sb("phk", [64, T], BF16, scope)
        phis, phisB = ctx.sb("phis", [64, T], BF16, scope)
        rowb, rowbB = ctx.sb("rowb2", [1, 2176], F32, scope)
        RO_BSH, RO_RFF, RO_BO, RO_BIAS, RO_M = 0, 64, 96, 1120, 2144
        bvcol, bvcolB = ctx.sb("bvcol", [128, 8], BF16, scope)
        RQ, RS = 0, 2048

        with ExitStack() as p1:
            xn, xnB = ctx.sb("xn2", [128, ND * T], BF16, p1)
            wv, wvB = ctx.sb("wv", [128, 8192], BF16, p1)
            wsm, wsmB = ctx.sb("wsm", [128, 768], BF16, p1)
            O_CH, O_SH, O_OM, O_CMB = 0, 8, 520, 552
            shl, shlB = ctx.sb("shl", [64, T], BF16, p1)

            self.load_w(w_v, 8, D, wv, wvB, 0)
            ctx.dma("qw", self.stg[:, 0:8], w_ch, writes=[self.stgB])
            ctx.op("g", lambda E: E.tensor_copy(out=wsm[:, O_CH:O_CH + 8], in_=self.stg[:, 0:8]),
                   reads=[self.stgB], writes=[wsmB])
            self.load_w(w_sh, 8, 64, wsm, wsmB, O_SH)
            ctx.dma("qw", self.stg[0:64, 0:32], w_om, writes=[self.stgB])
            ctx.op("g", lambda E: E.tensor_copy(out=wsm[0:64, O_OM:O_OM + 32],
                                                in_=self.stg[0:64, 0:32]),
                   reads=[self.stgB], writes=[wsmB])
            for kt in range(3):
                kk = 128 if kt < 2 else 64
                ctx.dma("qw", self.stg[0:kk, 64:128], w_cmb[kt * 128:kt * 128 + kk, :],
                        writes=[self.stgB])
                ctx.op("g", lambda E, kt=kt, kk=kk: E.tensor_copy(
                    out=wsm[0:kk, O_CMB + kt * 64:O_CMB + (kt + 1) * 64],
                    in_=self.stg[0:kk, 64:128]), reads=[self.stgB], writes=[wsmB])
            ctx.dma("qw", self.stg[:, 0:9], bv_c, writes=[self.stgB])
            ctx.op("g", lambda E: E.tensor_copy(out=bvcol[:, :], in_=self.stg[:, 0:8]),
                   reads=[self.stgB], writes=[bvcolB])
            ctx.op("g", lambda E: E.tensor_copy(out=self.colp[0:32, 6:7], in_=self.stg[0:32, 8:9]),
                   reads=[self.stgB], writes=[self.colpB])
            for r_d, off, w in ((r_bsh, RO_BSH, 64), (r_rff, RO_RFF, 32), (r_bo, RO_BO, D),
                                (r_bch, RO_M, 1), (r_lt, RO_M + 1, 1)):
                ctx.dma("qw", rowb[:, off:off + w], r_d, writes=[rowbB])

            self.emit_ln(xn, xnB)

            def xns(dt, tt):
                return xn[:, dt * T + tt * TW: dt * T + (tt + 1) * TW]

            for i in range(16):
                for dh in range(2):
                    bi = ctx.bank()
                    for kt in range(8):
                        self.mm(bi, ctx.pb[bi][:, 0:TW],
                                xn[:, kt * T + i * 128: kt * T + (i + 1) * 128],
                                wv[:, kt * D + dh * TW: kt * D + (dh + 1) * TW],
                                [xnB, wvB], last=(kt == 7))
                    ctx.op("s", lambda E, i=i, dh=dh, bi=bi: E.copy(
                        out=vsb[:, i * 1024 + dh * TW: i * 1024 + (dh + 1) * TW],
                        in_=ctx.pb[bi][:, 0:TW]), reads=[ctx.pbuf[bi]], writes=[vsbB])

            for tt in range(NT):
                bi = ctx.bank()
                for kt in range(8):
                    self.mm(bi, ctx.pb[bi][0:1, 0:TW], wsm[:, O_CH + kt:O_CH + kt + 1],
                            xns(kt, tt), [wsmB, xnB])
                self.mm(bi, ctx.pb[bi][0:1, 0:TW], rowb[:, RO_M:RO_M + 1], self.onesf[:, :],
                        [rowbB, self.onesfB], last=True)
                ctx.op("s", lambda E, tt=tt, bi=bi: E.activation(
                    out=self.rows[:, RQ + tt * TW:RQ + (tt + 1) * TW], in_=ctx.pb[bi][0:1, 0:TW],
                    func=AF.Tanh), reads=[ctx.pbuf[bi]], writes=[self.rowsB])

            sqv = self.bc1.ap().bitcast(BF16)
            for tt in range(NT):
                bi = ctx.bank()
                for kt in range(8):
                    self.mm(bi, ctx.pb[bi][0:64, 0:TW], wsm[:, O_SH + kt * 64:O_SH + (kt + 1) * 64],
                            xns(kt, tt), [wsmB, xnB])
                self.mm(bi, ctx.pb[bi][0:64, 0:TW], rowb[:, RO_BSH:RO_BSH + 64], self.onesf[:, :],
                        [rowbB, self.onesfB], last=True)
                ctx.op("s", lambda E, tt=tt, bi=bi: E.copy(
                    out=shl[:, tt * TW:(tt + 1) * TW], in_=ctx.pb[bi][0:64, 0:TW]),
                    reads=[ctx.pbuf[bi]], writes=[shlB])
                ctx.op("s", lambda E, tt=tt, bi=bi: E.activation(
                    out=sqv[0:64, tt * TW:(tt + 1) * TW], in_=ctx.pb[bi][0:64, 0:TW],
                    func=AF.Square), reads=[ctx.pbuf[bi]], writes=[self.bc1B])
                bj = ctx.bank()
                self.mm(bj, ctx.pb[bj][0:1, 0:TW], self.onescb[0:64, :],
                        sqv[0:64, tt * TW:(tt + 1) * TW], [self.onescbB, self.bc1B], last=True)
                ctx.op("s", lambda E, tt=tt, bj=bj: E.activation(
                    out=self.rows[:, RS + tt * TW:RS + (tt + 1) * TW], in_=ctx.pb[bj][0:1, 0:TW],
                    func=AF.Sqrt), reads=[ctx.pbuf[bj]], writes=[self.rowsB])
            ctx.op("v", lambda E: E.tensor_scalar(
                out=self.rows[:, RS:RS + T], in0=self.rows[:, RS:RS + T], scalar1=1e-12,
                scalar2=None, op0=ALU.max), reads=[self.rowsB], writes=[self.rowsB])
            ctx.op("v", lambda E: E.reciprocal(
                out=self.rows[:, RS:RS + T], in_=self.rows[:, RS:RS + T]),
                reads=[self.rowsB], writes=[self.rowsB])
            self.bcast(self.rows[:, RS:RS + T], self.rowsB, self.bc1[0:32, 0:T], self.bc1B,
                       nrows=32)
            prj = self.tmpf[0:32, 0:T]
            for tt in range(NT):
                bi = ctx.bank()
                self.mm(bi, ctx.pb[bi][0:32, 0:TW], wsm[0:64, O_OM:O_OM + 32],
                        shl[:, tt * TW:(tt + 1) * TW], [wsmB, shlB], last=True)
                ctx.op("v", lambda E, tt=tt, bi=bi: E.tensor_tensor(
                    out=prj[:, tt * TW:(tt + 1) * TW], in0=ctx.pb[bi][0:32, 0:TW],
                    in1=self.bc1[0:32, tt * TW:(tt + 1) * TW], op=ALU.mult),
                    reads=[ctx.pbuf[bi], self.bc1B], writes=[self.tmpfB])
            ctx.op("v", lambda E: E.tensor_scalar(
                out=prj[:, :], in0=prj[:, :], scalar1=self.colp[0:32, 6:7], scalar2=None,
                op0=ALU.add), reads=[self.tmpfB, self.colpB], writes=[self.tmpfB])
            prji = self.bc2.ap().bitcast(I32)[32:64, 0:T]
            tred = self.tmpf[32:64, 0:T]
            inv2pi = 1.0 / (2 * math.pi)
            for rowlo, shift in ((32, 0.0), (0, 0.25)):
                ctx.op("v", lambda E, shift=shift, tred=tred: E.tensor_scalar(
                    out=tred, in0=prj[:, :], scalar1=inv2pi, scalar2=shift,
                    op0=ALU.mult, op1=ALU.add), reads=[self.tmpfB], writes=[self.tmpfB])
                ctx.op("v", lambda E, tred=tred: E.tensor_scalar(
                    out=prji[:, :], in0=tred, scalar1=-0.5, scalar2=None,
                    op0=ALU.add), reads=[self.tmpfB], writes=[self.bc2B])
                ctx.op("v", lambda E, tred=tred: E.tensor_tensor(
                    out=tred, in0=tred, in1=prji[:, :],
                    op=ALU.subtract), reads=[self.tmpfB, self.bc2B], writes=[self.tmpfB])
                ctx.op("s", lambda E, rowlo=rowlo, tred=tred: E.activation(
                    out=phis[rowlo:rowlo + 32, 0:T], in_=tred, func=AF.Sin,
                    scale=2 * math.pi, bias=-math.pi), reads=[self.tmpfB], writes=[phisB])

        ctx.barrier()
        with ExitStack() as p1b:
            tp1, tp1B = ctx.sb("tp1", [128, T], BF16, p1b)
            tp2, tp2B = ctx.sb("tp2", [128, T], BF16, p1b)
            tp3, tp3B = ctx.sb("tp3", [64, T], BF16, p1b)
            wcm, wcmB = ctx.sb("wcm", [128, 192], BF16, p1b)
            for kt in range(3):
                kk = 128 if kt < 2 else 64
                ctx.dma("qw", self.stg[0:kk, 0:64], w_cmb[kt * 128:kt * 128 + kk, :],
                        writes=[self.stgB])
                ctx.op("g", lambda E, kt=kt, kk=kk: E.tensor_copy(
                    out=wcm[0:kk, kt * 64:(kt + 1) * 64], in_=self.stg[0:kk, 0:64]),
                    reads=[self.stgB], writes=[wcmB])
            scl = [-RFF_SCALE * m for m in MAGS]
            targets = [(tp1, tp1B, 0), (tp1, tp1B, 64), (tp2, tp2B, 0), (tp2, tp2B, 64),
                       (tp3, tp3B, 0)]
            for m in range(5):
                tph, tphB, prow = targets[m]
                if m == 0:
                    ctx.op("v", lambda E: E.memset(self.rows[:, RS:RS + T], scl[0]),
                           writes=[self.rowsB])
                else:
                    if m == 1:
                        ctx.op("v", lambda E: E.tensor_copy(
                            out=self.tmpf[0:1, 0:T], in_=self.rows[:, RQ:RQ + T]),
                            reads=[self.rowsB], writes=[self.tmpfB])
                    else:
                        ctx.op("v", lambda E: E.tensor_tensor(
                            out=self.tmpf[0:1, 0:T], in0=self.tmpf[0:1, 0:T],
                            in1=self.rows[:, RQ:RQ + T], op=ALU.mult),
                            reads=[self.tmpfB, self.rowsB], writes=[self.tmpfB])
                    ctx.op("v", lambda E, m=m: E.tensor_scalar(
                        out=self.rows[:, RS:RS + T], in0=self.tmpf[0:1, 0:T], scalar1=scl[m],
                        scalar2=None, op0=ALU.mult), reads=[self.tmpfB], writes=[self.rowsB])
                self.bcast(self.rows[:, RS:RS + T], self.rowsB, self.bc2[0:64, 0:T],
                           self.bc2B, nrows=64)
                ctx.op("v", lambda E, tph=tph, prow=prow: E.tensor_tensor(
                    out=tph[prow:prow + 64, 0:T], in0=phis[:, :], in1=self.bc2[0:64, 0:T],
                    op=ALU.mult), reads=[phisB, self.bc2B], writes=[tphB])
            if DBG:
                dtp = self.dram("dbg_tp1", (128, T), BF16, kind="ExternalOutput")
                self.ctx.dma("qx", dtp, tp1[:, :], reads=[tp1B])
                dtp2 = self.dram("dbg_tp2", (128, T), BF16, kind="ExternalOutput")
                self.ctx.dma("qx", dtp2, tp2[:, :], reads=[tp2B])
                dtp3 = self.dram("dbg_tp3", (64, T), BF16, kind="ExternalOutput")
                self.ctx.dma("qx", dtp3, tp3[:, :], reads=[tp3B])
                dwcm = self.dram("dbg_wcm", (128, 192), BF16, kind="ExternalOutput")
                self.ctx.dma("qx", dwcm, wcm[:, :], reads=[wcmB])
            phiT = self.tmpf[64:128, 0:T]
            for tt in range(NT):
                bi = ctx.bank()
                srcs = [(tp1, tp1B, 128, 0), (tp2, tp2B, 128, 1), (tp3, tp3B, 64, 2)]
                for j, (tph, tphB, kk, kt) in enumerate(srcs):
                    self.mm(bi, ctx.pb[bi][0:64, 0:TW], wcm[0:kk, kt * 64:(kt + 1) * 64],
                            tph[0:kk, tt * TW:(tt + 1) * TW], [tphB, wcmB], last=(j == 2))
                ctx.op("v", lambda E, tt=tt, bi=bi: E.tensor_copy(
                    out=phiT[:, tt * TW:(tt + 1) * TW], in_=ctx.pb[bi][0:64, 0:TW]),
                    reads=[ctx.pbuf[bi]], writes=[self.tmpfB])

            M = RO_M
            ctx.op("s", lambda E: E.activation(
                out=rowb[:, M + 2:M + 3], in_=rowb[:, M + 1:M + 2], func=AF.Abs),
                reads=[rowbB], writes=[rowbB])
            ctx.op("s", lambda E: E.activation(
                out=rowb[:, M + 3:M + 4], in_=rowb[:, M + 2:M + 3], func=AF.Exp, scale=-1.0),
                reads=[rowbB], writes=[rowbB])
            ctx.op("s", lambda E: E.activation(
                out=rowb[:, M + 3:M + 4], in_=rowb[:, M + 3:M + 4], func=AF.Ln, bias=1.0,
                scale=1.0), reads=[rowbB], writes=[rowbB])
            ctx.op("v", lambda E: E.tensor_scalar(
                out=rowb[:, M + 4:M + 5], in0=rowb[:, M + 1:M + 2], scalar1=0.0, scalar2=None,
                op0=ALU.max), reads=[rowbB], writes=[rowbB])
            ctx.op("v", lambda E: E.tensor_tensor(
                out=rowb[:, M + 4:M + 5], in0=rowb[:, M + 4:M + 5], in1=rowb[:, M + 3:M + 4],
                op=ALU.add), reads=[rowbB], writes=[rowbB])
            ctx.op("v", lambda E: E.tensor_scalar(
                out=rowb[:, M + 4:M + 5], in0=rowb[:, M + 4:M + 5], scalar1=0.1, scalar2=None,
                op0=ALU.add), reads=[rowbB], writes=[rowbB])
            ctx.op("v", lambda E: E.reciprocal(out=rowb[:, M + 5:M + 6], in_=rowb[:, M + 4:M + 5]),
                   reads=[rowbB], writes=[rowbB])
            ctx.dma("qw", self.rows[:, RS:RS + T], self.ramps_d[0:1, :], writes=[self.rowsB])
            ctx.op("v", lambda E: E.tensor_scalar(
                out=self.rows[:, RS:RS + T], in0=self.rows[:, RS:RS + T],
                scalar1=rowb[:, M + 5:M + 6], scalar2=None, op0=ALU.mult),
                reads=[self.rowsB, rowbB], writes=[self.rowsB])
            self.bcast(self.rows[:, RS:RS + T], self.rowsB, self.bc1[64:128, 0:T], self.bc1B,
                       nrows=64)
            ctx.dma("qw", self.rows[:, RS:RS + T], self.ramps_d[1:2, :], writes=[self.rowsB])
            self.bcast(self.rows[:, RS:RS + T], self.rowsB, self.bc2[64:128, 0:T], self.bc2B,
                       nrows=64)
            ctx.op("v", lambda E: E.tensor_tensor(
                out=phq[:, :], in0=phiT[:, :], in1=self.bc1[64:128, 0:T], op=ALU.mult),
                reads=[self.tmpfB, self.bc1B], writes=[phqB])
            ctx.op("v", lambda E: E.tensor_tensor(
                out=phk[:, :], in0=phiT[:, :], in1=self.bc2[64:128, 0:T], op=ALU.mult),
                reads=[self.tmpfB, self.bc2B], writes=[phkB])

        if DBG:
            dprj = self.dram("dbg_prj", (32, T), kind="ExternalOutput")
            self.ctx.dma("qx", dprj, self.tmpf[0:32, 0:T], reads=[self.tmpfB])
            dphiT = self.dram("dbg_phiT", (64, T), kind="ExternalOutput")
            self.ctx.dma("qx", dphiT, self.tmpf[64:128, 0:T], reads=[self.tmpfB])
            dbc1 = self.dram("dbg_rampq", (64, T), kind="ExternalOutput")
            self.ctx.dma("qx", dbc1, self.bc1[64:128, 0:T], reads=[self.bc1B])
            for nm, th, tb, pp in (("dbg_phq", phq, phqB, 64), ("dbg_phk", phk, phkB, 64),
                                   ("dbg_phis", phis, phisB, 64)):
                dap = self.dram(nm, (pp, T), BF16, kind="ExternalOutput")
                self.ctx.dma("qx", dap, th[0:pp, 0:T], reads=[tb])
            dq = self.dram("dbg_qrow", (1, T), kind="ExternalOutput")
            self.ctx.dma("qx", dq, self.rows[:, RQ:RQ + T], reads=[self.rowsB])

        ctx.barrier()
        with ExitStack() as p2:
            wo, woB = ctx.sb("wo", [128, 8192], BF16, p2)
            expt, exptB = ctx.sb("expt", [128, 16 * 128], BF16, p2)
            otq, otqB = ctx.sb("otq", [128, 1024], BF16, p2)
            self.load_w(w_o, 8, D, wo, woB, 0)
            for half in range(2):
                bi = ctx.bank()
                for kt in range(8):
                    self.mm(bi, ctx.pb[bi][0:1, 0:TW], bvcol[:, kt:kt + 1],
                            wo[:, kt * D + half * TW: kt * D + (half + 1) * TW],
                            [bvcolB, woB], last=(kt == 7))
                ctx.op("v", lambda E, half=half, bi=bi: E.tensor_copy(
                    out=rowb[:, RO_BIAS + half * TW:RO_BIAS + (half + 1) * TW],
                    in_=ctx.pb[bi][0:1, 0:TW]), reads=[ctx.pbuf[bi]], writes=[rowbB])
            ctx.op("v", lambda E: E.tensor_tensor(
                out=rowb[:, RO_BIAS:RO_BIAS + D], in0=rowb[:, RO_BIAS:RO_BIAS + D],
                in1=rowb[:, RO_BO:RO_BO + D], op=ALU.add), reads=[rowbB], writes=[rowbB])

            for qt in range(16):
                for kt in range(qt + 1):
                    bl = ctx.bank()
                    self.mm(bl, ctx.pb[bl][:, 0:128], phk[:, kt * 128:(kt + 1) * 128],
                            phq[:, qt * 128:(qt + 1) * 128], [phkB, phqB], last=True)
                    ctx.op("s", lambda E, kt=kt, bl=bl: E.activation(
                        out=expt[:, kt * 128:(kt + 1) * 128], in_=ctx.pb[bl][:, 0:128],
                        func=AF.Exp), reads=[ctx.pbuf[bl]], writes=[exptB])
                    if kt == qt:
                        ctx.op("v", lambda E, kt=kt: E.tensor_tensor(
                            out=expt[:, kt * 128:(kt + 1) * 128],
                            in0=expt[:, kt * 128:(kt + 1) * 128], in1=self.trib[:, :],
                            op=ALU.mult), reads=[exptB, self.tribB], writes=[exptB])
                bs = ctx.bank()
                for kt in range(qt + 1):
                    self.mm(bs, ctx.pb[bs][0:1, 0:128], self.onescb[:, :],
                            expt[:, kt * 128:(kt + 1) * 128], [self.onescbB, exptB],
                            last=(kt == qt))
                ctx.op("v", lambda E, qt=qt, bs=bs: E.tensor_copy(
                    out=self.rows[:, RS + qt * 128:RS + (qt + 1) * 128],
                    in_=ctx.pb[bs][0:1, 0:128]), reads=[ctx.pbuf[bs]], writes=[self.rowsB])
                ctx.op("v", lambda E, qt=qt: E.reciprocal(
                    out=self.rows[:, RS + qt * 128:RS + (qt + 1) * 128],
                    in_=self.rows[:, RS + qt * 128:RS + (qt + 1) * 128]),
                    reads=[self.rowsB], writes=[self.rowsB])
                self.bcast(self.rows[:, RS + qt * 128:RS + (qt + 1) * 128], self.rowsB,
                           self.bc1[:, qt * 128:(qt + 1) * 128], self.bc1B, width=128)
                for bq in range(2):
                    bo = ctx.bank()
                    for kt in range(qt + 1):
                        for dv in range(4):
                            dvg = bq * 4 + dv
                            self.mm(bo, ctx.pb[bo][:, dv * 128:(dv + 1) * 128],
                                    vsb[:, kt * 1024 + dvg * 128: kt * 1024 + (dvg + 1) * 128],
                                    expt[:, kt * 128:(kt + 1) * 128], [vsbB, exptB],
                                    last=(kt == qt and dv == 3))
                    for dv in range(4):
                        dvg = bq * 4 + dv
                        ctx.op("v", lambda E, dv=dv, dvg=dvg, bo=bo, qt=qt: E.tensor_tensor(
                            out=otq[:, dvg * 128:(dvg + 1) * 128],
                            in0=ctx.pb[bo][:, dv * 128:(dv + 1) * 128],
                            in1=self.bc1[:, qt * 128:(qt + 1) * 128], op=ALU.mult),
                            reads=[ctx.pbuf[bo], self.bc1B], writes=[otqB])
                for bq in range(2):
                    bz = ctx.bank()
                    for dtl in range(4):
                        dt = bq * 4 + dtl
                        for kv in range(8):
                            self.mm(bz, ctx.pb[bz][:, dtl * 128:(dtl + 1) * 128],
                                    wo[:, kv * D + dt * 128: kv * D + dt * 128 + 128],
                                    otq[:, kv * 128:(kv + 1) * 128], [woB, otqB])
                        self.mm(bz, ctx.pb[bz][:, dtl * 128:(dtl + 1) * 128],
                                rowb[:, RO_BIAS + dt * 128:RO_BIAS + dt * 128 + 128],
                                self.onesf[:, 0:128], [rowbB, self.onesfB],
                                last=(dtl == 3))
                    for dtl in range(4):
                        dt = bq * 4 + dtl
                        ctx.op("v", lambda E, dt=dt, dtl=dtl, qt=qt, bz=bz: E.tensor_tensor(
                            out=self.x[:, dt * T + qt * 128: dt * T + (qt + 1) * 128],
                            in0=self.x[:, dt * T + qt * 128: dt * T + (qt + 1) * 128],
                            in1=ctx.pb[bz][:, dtl * 128:(dtl + 1) * 128], op=ALU.add),
                            reads=[ctx.pbuf[bz]], writes=[self.xB])
        ctx.barrier()

    def emit_l1_ffn(self, scope):
        ctx = self.ctx
        w_f = self.dram("L1f_weightT", (D, D))
        r_bf = self.dram("L1f_bias", (1, D))
        r_al = self.dram("L1f_alpha", (1, 1))

        xn, xnB = ctx.sb("xn3", [128, ND * T], BF16, scope)
        wstg, wstgB = ctx.sb("wf32", [128, 8192], F32, scope)
        wt, wtB = ctx.sb("wtern", [128, 8192], BF16, scope)
        sgn, sgnB = ctx.sb("sgn1f", [128, 8192], BF16, scope)
        rowb, rowbB = ctx.sb("rowb3", [1, 2048], F32, scope)

        view = w_f.rearrange("(k p) d -> p k d", p=128)
        for c in range(4):
            ctx.dma("qw", wstg[:, c * 2048:(c + 1) * 2048], view[:, c * 2:(c + 1) * 2, :],
                    writes=[wstgB])
        ctx.dma("qw", rowb[:, 0:D], r_bf, writes=[rowbB])
        ctx.dma("qw", rowb[:, D:D + 1], r_al, writes=[rowbB])

        def xns(dt, tt):
            return xn[:, dt * T + tt * TW: dt * T + (tt + 1) * TW]

        ctx.op("v", lambda E: E.tensor_reduce(
            out=self.colp[:, 0:1], in_=wstg[:, :], axis=AX, op=ALU.add,
            apply_absolute_value=True), reads=[wstgB], writes=[self.colpB])
        bi = ctx.bank()
        self.mm(bi, ctx.pb[bi][0:1, 0:1], self.colp[:, 0:1], self.onescf[:, :],
                [self.colpB, self.onescfB], last=True)
        ctx.op("v", lambda E, bi=bi: E.tensor_scalar(
            out=rowb[:, D + 2:D + 3], in0=ctx.pb[bi][0:1, 0:1], scalar1=0.7 / (D * D),
            scalar2=None, op0=ALU.mult), reads=[ctx.pbuf[bi]], writes=[rowbB])
        ctx.op("v", lambda E: E.reciprocal(out=rowb[:, D + 3:D + 4], in_=rowb[:, D:D + 1]),
               reads=[rowbB], writes=[rowbB])
        bj = ctx.bank()
        self.mm(bj, ctx.pb[bj][:, 0:1], self.onesf[:, 0:128], rowb[:, D + 2:D + 3],
                [self.onesfB, rowbB], last=True)
        ctx.op("v", lambda E, bj=bj: E.tensor_copy(out=self.colp[:, 1:2], in_=ctx.pb[bj][:, 0:1]),
               reads=[ctx.pbuf[bj]], writes=[self.colpB])
        bj = ctx.bank()
        self.mm(bj, ctx.pb[bj][:, 0:1], self.onesf[:, 0:128], rowb[:, D:D + 1],
                [self.onesfB, rowbB], last=True)
        ctx.op("v", lambda E, bj=bj: E.tensor_copy(out=self.colp[:, 2:3], in_=ctx.pb[bj][:, 0:1]),
               reads=[ctx.pbuf[bj]], writes=[self.colpB])
        ctx.op("s", lambda E: E.activation(out=sgn[:, :], in_=wstg[:, :], func=AF.Sign),
               reads=[wstgB], writes=[sgnB])
        ctx.op("s", lambda E: E.activation(out=wstg[:, :], in_=wstg[:, :], func=AF.Abs),
               reads=[wstgB], writes=[wstgB])
        ctx.op("v", lambda E: E.tensor_scalar(
            out=wstg[:, :], in0=wstg[:, :], scalar1=self.colp[:, 1:2], scalar2=None,
            op0=ALU.is_gt), reads=[wstgB, self.colpB], writes=[wstgB])
        ctx.op("v", lambda E: E.tensor_tensor(
            out=wt[:, :], in0=sgn[:, :], in1=wstg[:, :], op=ALU.mult),
            reads=[sgnB, wstgB], writes=[wtB])
        ctx.op("v", lambda E: E.tensor_scalar(
            out=rowb[:, 0:D], in0=rowb[:, 0:D], scalar1=rowb[:, D + 3:D + 4], scalar2=None,
            op0=ALU.mult), reads=[rowbB], writes=[rowbB])
        self.emit_ln(xn, xnB)
        for dt in range(ND):
            for tt in range(NT):
                bi = ctx.bank()
                for kt in range(8):
                    self.mm(bi, ctx.pb[bi][:, 0:TW],
                            wt[:, kt * D + dt * 128: kt * D + dt * 128 + 128], xns(kt, tt),
                            [wtB, xnB])
                self.mm(bi, ctx.pb[bi][:, 0:TW], rowb[:, dt * 128:dt * 128 + 128],
                        self.onesf[:, :], [rowbB, self.onesfB], last=True)
                ctx.op("v", lambda E, dt=dt, tt=tt, bi=bi: E.scalar_tensor_tensor(
                    out=self.xs(dt, tt), in0=ctx.pb[bi][:, 0:TW], scalar=self.colp[:, 2:3],
                    in1=self.xs(dt, tt), op0=ALU.mult, op1=ALU.add),
                    reads=[ctx.pbuf[bi], self.colpB], writes=[self.xB])
        ctx.barrier()


BLOCKS = {
    0: lambda m, sc: m.emit_l0_attn(sc),
    1: lambda m, sc: m.emit_l0_ffn(sc),
    2: lambda m, sc: m.emit_l1_attn(sc),
    3: lambda m, sc: m.emit_l1_ffn(sc),
}

LAUNCH_PLAN = [[0, 1, 2, 3]]
DBG = False
SCAN_NO_CARRY = False
MOE_NO_EXPERTS = False
MOE_ONE_EXPERT = None
MOE_SERIAL = False
MOE_NUM = NEXP
MOE_NO_SHARED = False


def build_program(block_ids):
    nc = bass.Bass("TRN2", target_bir_lowering=False, dynamic_dma_scratch_size=2048)
    m = Module(nc)
    with m.ctx.es:
        m.setup()
        m.ctx.barrier()
        for b in block_ids:
            with ExitStack() as scope:
                BLOCKS[b](m, scope)
        m.finish()
    return nc, m.declared


def _tr(a):
    return np.ascontiguousarray(np.asarray(a, np.float32).T)


def _row(a):
    return np.ascontiguousarray(np.asarray(a, np.float32).reshape(1, -1))


def prep_weights(inputs):
    g = {}
    g["L0a_s_w1T"] = _tr(inputs["L0_attn_s_w1"])
    g["L0a_s_w2T"] = _tr(inputs["L0_attn_s_w2"])
    g["L0a_B_wT"] = _tr(inputs["L0_attn_B_w"])
    g["L0a_C_wT"] = _tr(inputs["L0_attn_C_w"])
    g["L0a_D_wT"] = _tr(inputs["L0_attn_D_w"])
    g["L0a_g_wT"] = _tr(inputs["L0_attn_g_w"])
    g["L0a_A"] = np.ascontiguousarray(np.asarray(inputs["L0_attn_A"], np.float32))
    for nm in ("s_b1", "s_b2", "B_b", "C_b", "D_b", "g_b"):
        g["L0a_" + nm] = _row(inputs["L0_attn_" + nm])
    g["L0f_WgT"] = _tr(inputs["L0_ffn_Wg"])
    for nm, src in (("We_gateT", "We_gate"), ("We_upT", "We_up"), ("We_downT", "We_down")):
        g["L0f_" + nm] = np.ascontiguousarray(
            np.swapaxes(np.asarray(inputs["L0_ffn_" + src], np.float32), 1, 2))
    g["L0f_Ws_gateT"] = _tr(inputs["L0_ffn_Ws_gate"])
    g["L0f_Ws_upT"] = _tr(inputs["L0_ffn_Ws_up"])
    g["L0f_Ws_downT"] = _tr(inputs["L0_ffn_Ws_down"])
    g["L0f_WsgT"] = np.ascontiguousarray(
        np.asarray(inputs["L0_ffn_Wsg"], np.float32).reshape(8, 128).T)
    g["L1a_W_chargeT"] = np.ascontiguousarray(
        np.asarray(inputs["L1_attn_W_charge"], np.float32).reshape(8, 128).T)
    g["L1a_W_shellT"] = _tr(inputs["L1_attn_W_shell"])
    g["L1a_omegaT"] = _tr(inputs["L1_attn_omega"])
    g["L1a_W_combineT"] = _tr(inputs["L1_attn_W_combine"])
    g["L1a_W_vT"] = _tr(inputs["L1_attn_W_v"])
    g["L1a_W_outT"] = _tr(inputs["L1_attn_W_out"])
    bvc = np.zeros((128, 9), np.float32)
    bvc[:, 0:8] = np.asarray(inputs["L1_attn_b_v"], np.float32).reshape(8, 128).T
    bvc[0:32, 8] = np.asarray(inputs["L1_attn_rff_bias"], np.float32).ravel()
    g["L1a_b_v_col"] = np.ascontiguousarray(bvc)
    g["L1a_b_charge"] = _row(inputs["L1_attn_b_charge"])
    g["L1a_b_shell"] = _row(inputs["L1_attn_b_shell"])
    g["L1a_rff_bias"] = _row(inputs["L1_attn_rff_bias"])
    g["L1a_b_out"] = _row(inputs["L1_attn_b_out"])
    g["L1a_log_temp"] = _row(inputs["L1_attn_log_temp"])
    g["L1f_weightT"] = _tr(inputs["L1_ffn_weight"])
    g["L1f_bias"] = _row(inputs["L1_ffn_bias"])
    g["L1f_alpha"] = _row(inputs["L1_ffn_alpha"])
    g["c_idf"] = np.eye(16, dtype=np.float32)
    g["c_idb"] = np.eye(128, dtype=ml_dtypes.bfloat16)
    g["c_trib"] = (np.arange(128)[:, None] <= np.arange(128)[None, :]).astype(ml_dtypes.bfloat16)
    t_idx = np.arange(T, dtype=np.float64)
    c = 0.3 / 10.0
    g["c_ramps"] = np.stack([np.exp(-c * t_idx), np.exp(c * t_idx)]).astype(np.float32)
    return g


def kernel(**inputs):
    x = np.asarray(inputs["L0_attn_x"], np.float32)
    g = prep_weights(inputs)
    xcur = [np.ascontiguousarray(x[b].T) for b in range(B)]
    for launch in LAUNCH_PLAN:
        nc, declared = build_program(launch)
        in_maps = []
        for b in range(B):
            mp = {"xin": xcur[b]}
            for name, (shape, dtype, kind) in declared.items():
                if kind == "ExternalInput" and name != "xin":
                    mp[name] = g[name]
            in_maps.append(mp)
        res = run_bass_kernel_spmd(nc, in_maps, core_ids=list(range(B)))
        xcur = [res.results[b]["xout"] for b in range(B)]
    out = np.stack([xc.T for xc in xcur], axis=0).astype(np.float32)
    return np.ascontiguousarray(out)


# version 15
# speedup vs baseline: 1.0472x; 1.0472x over previous
import math
import numpy as np
import ml_dtypes
from contextlib import ExitStack

import concourse.bass as bass
import concourse.mybir as mybir
from concourse.bass_utils import run_bass_kernel_spmd

F32 = mybir.dt.float32
BF16 = mybir.dt.bfloat16
I32 = mybir.dt.int32
AF = mybir.ActivationFunctionType
ALU = mybir.AluOpType
AX = mybir.AxisListType.X

B, T, D = 8, 2048, 1024
ND = D // 128
NT = T // 512
TW = 512
EPS = 1e-5

SC_C = 32
SC_NCH = T // SC_C
SC_PB = 512 // SC_C

L1_RFF = 32
RFF_SCALE = math.sqrt(2.0 / (2 * L1_RFF)) * math.sqrt(math.exp(-0.5))
MAGS = [math.sqrt(abs((-1.0) ** m / math.factorial(m)) + 1e-10) for m in range(5)]
NEXP, MOE_HID, SH_HID = 16, 512, 2048

ENG_NAMES = ("t", "v", "s", "g", "sync")


class Buf:
    __slots__ = ("name", "wr", "rd")

    def __init__(self, name):
        self.name = name
        self.wr = None
        self.rd = {}

    def add_rd(self, tok):
        s, v = tok
        if self.rd.get(s, 0) < v:
            self.rd[s] = v


class Ctx:
    def __init__(self, nc):
        self.nc = nc
        self.es = ExitStack()
        self.E = {"t": nc.tensor, "v": nc.vector, "s": nc.scalar, "g": nc.gpsimd,
                  "sync": nc.sync}
        self.sem = {}
        self.cnt = {}
        for s in ("t", "v", "s", "g", "qw", "qb", "qx", "qs", "qA", "qB"):
            self.sem[s] = self.es.enter_context(nc.semaphore(s))
            self.cnt[s] = 0
        self.water = {}
        self.pb = [self.es.enter_context(nc.psum_tensor(f"pb{i}", [128, 512], F32))
                   for i in range(8)]
        self.pbuf = [Buf(f"pb{i}") for i in range(8)]
        self.fresh = [False] * 8
        self.rr = 0
        self.bufs = []

    def sb(self, name, shape, dtype, scope=None):
        h = (scope or self.es).enter_context(self.nc.sbuf_tensor(name, shape, dtype))
        b = Buf(name)
        self.bufs.append(b)
        return h, b

    def wait(self, eng, tok):
        if tok is None:
            return
        s, v = tok
        key = (eng, s)
        if self.water.get(key, 0) >= v:
            return
        self.E[eng].wait_ge(self.sem[s], v)
        self.water[key] = v

    def _wait_rd(self, eng, b):
        self.wait(eng, b.wr)

    def _wait_wr(self, eng, b):
        self.wait(eng, b.wr)
        for s, v in b.rd.items():
            self.wait(eng, (s, v))

    def op(self, eng, emit, reads=(), writes=(), waitw=(), tick=True):
        for b in reads:
            self._wait_rd(eng, b)
        for b in writes:
            self._wait_wr(eng, b)
        for b in waitw:
            self._wait_wr(eng, b)
        instr = emit(self.E[eng])
        tok = None
        if tick:
            self.cnt[eng] += 1
            instr.then_inc(self.sem[eng], 1)
            tok = (eng, self.cnt[eng])
            for b in reads:
                b.add_rd(tok)
            for b in writes:
                b.wr = tok
                b.rd = {}
        return tok

    def dma(self, q, out_ap, in_ap, reads=(), writes=(), **kw):
        eng = {"qw": "sync", "qx": "sync", "qb": "g", "qs": "s", "qA": "sync",
               "qB": "s"}[q]
        for b in reads:
            self._wait_rd(eng, b)
        for b in writes:
            self._wait_wr(eng, b)
        instr = self.E[eng].dma_start(out=out_ap, in_=in_ap, **kw)
        self.cnt[q] += 16
        instr.then_inc(self.sem[q], 16)
        tok = (q, self.cnt[q])
        for b in reads:
            b.add_rd(tok)
        for b in writes:
            b.wr = tok
            b.rd = {}
        return tok

    def bank(self):
        for _ in range(16):
            i = self.rr % 8
            self.rr += 1
            if i not in getattr(self, "reserved", ()):
                self.fresh[i] = True
                return i
        raise RuntimeError("no free psum bank")

    def barrier(self):
        toks = [(s, self.cnt[s]) for s in self.sem]
        for e in ENG_NAMES:
            for tok in toks:
                self.wait(e, tok)
        for b in self.pbuf:
            b.wr = None
            b.rd = {}
        for b in self.bufs:
            b.wr = None
            b.rd = {}


class Module:
    def __init__(self, nc):
        self.nc = nc
        self.ctx = Ctx(nc)
        self.declared = {}

    def dram(self, name, shape, dtype=F32, kind="ExternalInput"):
        ap = self.nc.dram_tensor(name, list(shape), dtype, kind=kind).ap()
        self.declared[name] = (shape, dtype, kind)
        return ap

    def mm(self, bi, out_ap, lhsT, rhs, reads, last=False):
        ctx = self.ctx
        pbuf = ctx.pbuf[bi]
        first = ctx.fresh[bi]
        ctx.fresh[bi] = False
        if not hasattr(ctx, "grp_reads"):
            ctx.grp_reads = [set() for _ in range(8)]
        ctx.grp_reads[bi].update(reads)
        group = list(ctx.grp_reads[bi]) if last else list(reads)
        if last:
            ctx.grp_reads[bi] = set()
        return ctx.op(
            "t",
            lambda E: E.matmul(out_ap, lhsT, rhs, start=first, stop=last,
                               skip_group_check=True),
            reads=group, writes=[pbuf] if last else (),
            waitw=[pbuf] if first else (), tick=last)

    def setup(self):
        ctx = self.ctx
        self.xin = self.dram("xin", (D, T))
        self.xout = self.dram("xout", (D, T), kind="ExternalOutput")
        self.idf_d = self.dram("c_idf", (16, 16))
        self.idb_d = self.dram("c_idb", (128, 128), BF16)
        self.trib_d = self.dram("c_trib", (128, 128), BF16)
        self.ramps_d = self.dram("c_ramps", (2, T))

        self.x, self.xB = ctx.sb("x", [128, ND * T], F32)
        self.bc1, self.bc1B = ctx.sb("bc1", [128, T], F32)
        self.bc2, self.bc2B = ctx.sb("bc2", [128, T], F32)
        self.tmpf, self.tmpfB = ctx.sb("tmpf", [128, T], F32)
        self.stg, self.stgB = ctx.sb("stg", [128, 2048], F32)
        self.stgHB = [self.stgB, Buf("stgh1")]
        ctx.bufs.append(self.stgHB[1])
        self.rows, self.rowsB = ctx.sb("rows", [1, 4096], F32)
        self.idf, self.idfB = ctx.sb("idf", [16, 16], F32)
        self.idb, self.idbB = ctx.sb("idb", [128, 128], BF16)
        self.trib, self.tribB = ctx.sb("trib", [128, 128], BF16)
        self.onesf, self.onesfB = ctx.sb("onesf", [1, TW], F32)
        self.onesrb, self.onesrbB = ctx.sb("onesrb", [1, 128], BF16)
        self.onescf, self.onescfB = ctx.sb("onescf", [128, 1], F32)
        self.onescb, self.onescbB = ctx.sb("onescb", [128, 1], BF16)
        self.zcolb, self.zcolbB = ctx.sb("zcolb", [128, 1], BF16)
        self.colp, self.colpB = ctx.sb("colp", [128, 64], F32)
        self.bnf = self.nc.dram_tensor("bounce_f", [24, T], F32, kind="Internal").ap()
        self.bnb = self.nc.dram_tensor("bounce_b", [24, T], BF16, kind="Internal").ap()
        self.bnfB, self.bnbB = Buf("bnf"), Buf("bnb")
        self.ctx.bufs += [self.bnfB, self.bnbB]
        self._slot = 0

        ctx.dma("qw", self.idf[:, :], self.idf_d, writes=[self.idfB])
        ctx.dma("qw", self.idb[:, :], self.idb_d, writes=[self.idbB])
        ctx.dma("qw", self.trib[:, :], self.trib_d, writes=[self.tribB])
        self.cdbt, self.cdbtB = ctx.sb("cdbt", [128, 16], F32)
        for i, v in enumerate((EPS, 2 * math.pi, -math.pi)):
            ctx.op("v", lambda E, i=i, v=v: E.memset(self.cdbt[:, i:i + 1], v),
                   writes=[self.cdbtB])
            self.nc.const_aps.register(F32, v, self.cdbt[:, i:i + 1])
        ctx.op("v", lambda E: E.memset(self.onesf[:, :], 1.0), writes=[self.onesfB])
        ctx.op("v", lambda E: E.memset(self.onesrb[:, :], 1.0), writes=[self.onesrbB])
        ctx.op("v", lambda E: E.memset(self.onescf[:, :], 1.0), writes=[self.onescfB])
        ctx.op("v", lambda E: E.memset(self.onescb[:, :], 1.0), writes=[self.onescbB])
        ctx.op("v", lambda E: E.memset(self.zcolb[:, :], 0.0), writes=[self.zcolbB])
        for dt in range(ND):
            ctx.dma(("qw", "qs", "qb")[dt % 3], self.x[:, dt * T:(dt + 1) * T],
                    self.xin[dt * 128:(dt + 1) * 128, :], writes=[self.xB])

    def finish(self):
        ctx = self.ctx
        qs_used = []
        for dt in range(ND):
            q = ("qx", "qs", "qb")[dt % 3]
            qs_used.append(q)
            ctx.dma(q, self.xout[dt * 128:(dt + 1) * 128, :],
                    self.x[:, dt * T:(dt + 1) * T], reads=[self.xB])
        for q in set(qs_used):
            ctx.wait("sync", (q, ctx.cnt[q]))

    def xs(self, dt, tt=None):
        if tt is None:
            return self.x[:, dt * T:(dt + 1) * T]
        return self.x[:, dt * T + tt * TW: dt * T + (tt + 1) * TW]

    def bcast(self, row_ap, row_buf, dst_ap, dst_buf, nrows=128, width=T, bf16=False):
        ctx = self.ctx
        ones = self.onesrb if bf16 else self.onesf
        onesB = self.onesrbB if bf16 else self.onesfB
        for c0 in range(0, width, 512):
            cw = min(512, width - c0)
            bj = ctx.bank()
            self.mm(bj, ctx.pb[bj][0:nrows, 0:cw], ones[:, 0:nrows], row_ap[:, c0:c0 + cw],
                    [onesB, row_buf], last=True)
            ctx.op("v", lambda E, bj=bj, c0=c0, cw=cw, nrows=nrows: E.tensor_copy(
                out=dst_ap[0:nrows, c0:c0 + cw], in_=ctx.pb[bj][0:nrows, 0:cw]),
                reads=[ctx.pbuf[bj]], writes=[dst_buf])

    def load_w(self, dram_ap, nk, dout, dst_h, dst_b, dst_off):
        ctx = self.ctx
        view = dram_ap.rearrange("(k p) d -> p k d", p=128)
        kchunk = max(1, 1024 // dout)
        k0 = 0
        ci = 0
        while k0 < nk:
            kn = min(kchunk, nk - k0)
            cw = kn * dout
            done = k0 * dout
            hs = ci % 2
            sl = self.stg[:, hs * 1024: hs * 1024 + cw]
            ctx.dma("qw" if hs == 0 else "qs", sl, view[:, k0:k0 + kn, :],
                    writes=[self.stgHB[hs]], allow_slow_non_contiguous=(dout < 16))
            ctx.op("s", lambda E, cw=cw, done=done, sl=sl, hs=hs: E.copy(
                out=dst_h[:, dst_off + done:dst_off + done + cw], in_=sl),
                reads=[self.stgHB[hs]], writes=[dst_b])
            k0 += kn
            ci += 1

    def emit_ln(self, xn, xnB):
        ctx = self.ctx
        sqv = self.bc1.ap().bitcast(BF16)
        sqvB = self.bc1B

        def xns(dt, tt=None):
            if tt is None:
                return xn[:, dt * T:(dt + 1) * T]
            return xn[:, dt * T + tt * TW: dt * T + (tt + 1) * TW]

        for dt in range(ND):
            ctx.op("s", lambda E, dt=dt: E.copy(out=xns(dt), in_=self.xs(dt)),
                   reads=[self.xB], writes=[xnB])
        for tt in range(NT):
            bs, bq = ctx.bank(), ctx.bank()
            for dt in range(ND):
                ctx.op("s", lambda E, dt=dt, tt=tt: E.activation(
                    out=sqv[:, dt * 512:(dt + 1) * 512], in_=self.xs(dt, tt), func=AF.Square),
                    reads=[self.xB], writes=[sqvB])
            for dt in range(ND):
                self.mm(bs, ctx.pb[bs][0:1, 0:512], self.onescb[:, :], xns(dt, tt),
                        [self.onescbB, xnB], last=(dt == ND - 1))
            for dt in range(ND):
                self.mm(bq, ctx.pb[bq][0:1, 0:512], self.onescb[:, :],
                        sqv[:, dt * 512:(dt + 1) * 512], [self.onescbB, sqvB],
                        last=(dt == ND - 1))
            ctx.op("v", lambda E, tt=tt, bs=bs: E.tensor_copy(
                out=self.rows[:, tt * TW:(tt + 1) * TW], in_=ctx.pb[bs][0:1, 0:512]),
                reads=[ctx.pbuf[bs]], writes=[self.rowsB])
            ctx.op("v", lambda E, tt=tt, bq=bq: E.tensor_copy(
                out=self.rows[:, 2048 + tt * TW:2048 + (tt + 1) * TW],
                in_=ctx.pb[bq][0:1, 0:512]), reads=[ctx.pbuf[bq]], writes=[self.rowsB])
        R0, R1 = 0, 2048
        ctx.op("v", lambda E: E.tensor_scalar(
            out=self.rows[:, R0:R0 + T], in0=self.rows[:, R0:R0 + T], scalar1=1.0 / D,
            scalar2=None, op0=ALU.mult), reads=[self.rowsB], writes=[self.rowsB])
        ctx.op("v", lambda E: E.tensor_scalar(
            out=self.rows[:, R1:R1 + T], in0=self.rows[:, R1:R1 + T], scalar1=1.0 / D,
            scalar2=None, op0=ALU.mult), reads=[self.rowsB], writes=[self.rowsB])
        ctx.op("v", lambda E: E.tensor_tensor(
            out=self.tmpf[0:1, 0:T], in0=self.rows[:, R0:R0 + T], in1=self.rows[:, R0:R0 + T],
            op=ALU.mult), reads=[self.rowsB], writes=[self.tmpfB])
        ctx.op("v", lambda E: E.tensor_tensor(
            out=self.rows[:, R1:R1 + T], in0=self.rows[:, R1:R1 + T], in1=self.tmpf[0:1, 0:T],
            op=ALU.subtract), reads=[self.rowsB, self.tmpfB], writes=[self.rowsB])
        ctx.op("s", lambda E: E.activation(
            out=self.rows[:, R1:R1 + T], in_=self.rows[:, R1:R1 + T], func=AF.Sqrt,
            bias=EPS, scale=1.0), reads=[self.rowsB], writes=[self.rowsB])
        ctx.op("v", lambda E: E.reciprocal(
            out=self.rows[:, R1:R1 + T], in_=self.rows[:, R1:R1 + T]),
            reads=[self.rowsB], writes=[self.rowsB])
        ctx.op("v", lambda E: E.tensor_tensor(
            out=self.rows[:, R0:R0 + T], in0=self.rows[:, R0:R0 + T],
            in1=self.rows[:, R1:R1 + T], op=ALU.mult), reads=[self.rowsB], writes=[self.rowsB])
        self.bcast(self.rows[:, R1:R1 + T], self.rowsB, self.bc1[:, :], self.bc1B)
        self.bcast(self.rows[:, R0:R0 + T], self.rowsB, self.bc2[:, :], self.bc2B)
        for dt in range(ND):
            ctx.op("v", lambda E, dt=dt: E.tensor_tensor(
                out=self.tmpf[:, :], in0=self.xs(dt), in1=self.bc1[:, :], op=ALU.mult),
                reads=[self.xB, self.bc1B], writes=[self.tmpfB])
            ctx.op("v", lambda E, dt=dt: E.tensor_tensor(
                out=xns(dt), in0=self.tmpf[:, :], in1=self.bc2[:, :], op=ALU.subtract),
                reads=[self.tmpfB, self.bc2B], writes=[xnB])

    def emit_l0_attn(self, scope):
        ctx = self.ctx
        w_s1 = self.dram("L0a_s_w1T", (D, 512))
        w_s2 = self.dram("L0a_s_w2T", (512, 128))
        w_B = self.dram("L0a_B_wT", (D, 128))
        w_C = self.dram("L0a_C_wT", (128, D))
        w_D = self.dram("L0a_D_wT", (D, D))
        w_g = self.dram("L0a_g_wT", (D, D))
        a_d = self.dram("L0a_A", (128, 128))
        biases = {}
        for nm, wd in (("s_b1", 512), ("s_b2", 128), ("B_b", 128), ("C_b", D),
                       ("D_b", D), ("g_b", D)):
            biases[nm] = self.dram("L0a_" + nm, (1, wd))

        xn, xnB = ctx.sb("xn0", [128, ND * T], BF16, scope)
        wl, wlB = ctx.sb("wl0a", [128, 17408], BF16, scope)
        O_S1, O_S2, O_B, O_C = 0, 4096, 4608, 16384
        O_D, O_G = 0, 8192
        s1t, s1tB = ctx.sb("s1t", [128, 2048], BF16, scope)
        sgb, sgbB = ctx.sb("sgb", [128, T], BF16, scope)
        ub, ubB = ctx.sb("ub", [128, T], BF16, scope)
        hsb, hsbB = ctx.sb("hsb", [128, T], BF16, scope)
        qstk, qstkB = ctx.sb("qstk", [128, 39 * 128], BF16, scope)
        esb, esbB = ctx.sb("esb", [128, SC_NCH], BF16, scope)
        abm, abmB = ctx.sb("abm", [128, 128], BF16, scope)
        rtm, rtmB = ctx.sb("rtm", [128, 128], BF16, scope)
        gt, gtB = ctx.sb("gt", [128, TW], BF16, scope)
        yt, ytB = ctx.sb("yt", [128, TW], BF16, scope)
        rowb, rowbB = ctx.sb("rowb0", [1, 2816], F32, scope)
        RO = {"s_b1": 0, "s_b2": 512, "B_b": 640, "C_b": 768, "g_b": 1792}

        self.load_w(w_s1, 8, 512, wl, wlB, O_S1)
        self.load_w(w_s2, 4, 128, wl, wlB, O_S2)
        self.load_w(w_B, 8, 128, wl, wlB, O_B)
        self.load_w(w_C, 1, D, wl, wlB, O_C)
        for nm in RO:
            w = {"s_b1": 512, "s_b2": 128, "B_b": 128, "C_b": D, "g_b": D}[nm]
            ctx.dma("qw", rowb[:, RO[nm]:RO[nm] + w], biases[nm], writes=[rowbB])
        ctx.dma("qw", self.tmpf[0:1, 0:D], biases["D_b"], writes=[self.tmpfB])
        ctx.op("v", lambda E: E.tensor_tensor(
            out=rowb[:, RO["C_b"]:RO["C_b"] + D], in0=rowb[:, RO["C_b"]:RO["C_b"] + D],
            in1=self.tmpf[0:1, 0:D], op=ALU.add), reads=[rowbB, self.tmpfB], writes=[rowbB])
        ctx.dma("qw", self.stg[:, 0:128], a_d, writes=[self.stgB])
        ctx.op("g", lambda E: E.tensor_copy(out=abm[:, :], in_=self.stg[:, 0:128]),
               reads=[self.stgB], writes=[abmB])

        self.emit_ln(xn, xnB)

        def xns(dt, tt):
            return xn[:, dt * T + tt * TW: dt * T + (tt + 1) * TW]

        for tt in range(NT):
            for ht in range(4):
                bi = ctx.bank()
                for kt in range(8):
                    self.mm(bi, ctx.pb[bi][:, 0:TW],
                            wl[:, O_S1 + kt * 512 + ht * 128: O_S1 + kt * 512 + ht * 128 + 128],
                            xns(kt, tt), [wlB, xnB])
                self.mm(bi, ctx.pb[bi][:, 0:TW], rowb[:, RO["s_b1"] + ht * 128:RO["s_b1"] + ht * 128 + 128],
                        self.onesf[:, :], [rowbB, self.onesfB], last=True)
                ctx.op("s", lambda E, ht=ht, bi=bi: E.activation(
                    out=s1t[:, ht * 512:(ht + 1) * 512], in_=ctx.pb[bi][:, 0:TW], func=AF.Silu),
                    reads=[ctx.pbuf[bi]], writes=[s1tB])
            bi = ctx.bank()
            for kt in range(4):
                self.mm(bi, ctx.pb[bi][:, 0:TW],
                        wl[:, O_S2 + kt * 128: O_S2 + (kt + 1) * 128],
                        s1t[:, kt * 512:(kt + 1) * 512], [wlB, s1tB])
            self.mm(bi, ctx.pb[bi][:, 0:TW], rowb[:, RO["s_b2"]:RO["s_b2"] + 128],
                    self.onesf[:, :], [rowbB, self.onesfB], last=True)
            ctx.op("s", lambda E, tt=tt, bi=bi: E.activation(
                out=sgb[:, tt * TW:(tt + 1) * TW], in_=ctx.pb[bi][:, 0:TW], func=AF.Sigmoid),
                reads=[ctx.pbuf[bi]], writes=[sgbB])
            bi = ctx.bank()
            for kt in range(8):
                self.mm(bi, ctx.pb[bi][:, 0:TW],
                        wl[:, O_B + kt * 128: O_B + (kt + 1) * 128], xns(kt, tt), [wlB, xnB])
            self.mm(bi, ctx.pb[bi][:, 0:TW], rowb[:, RO["B_b"]:RO["B_b"] + 128],
                    self.onesf[:, :], [rowbB, self.onesfB], last=True)
            ctx.op("v", lambda E, tt=tt, bi=bi: E.tensor_tensor(
                out=ub[:, tt * TW:(tt + 1) * TW], in0=ctx.pb[bi][:, 0:TW],
                in1=sgb[:, tt * TW:(tt + 1) * TW], op=ALU.mult),
                reads=[ctx.pbuf[bi], sgbB], writes=[ubB])

        self.load_w(w_D, 8, D, wl, wlB, O_D)
        self.load_w(w_g, 8, D, wl, wlB, O_G)

        self._emit_scan(abm, abmB, rtm, rtmB, qstk, qstkB, esb, esbB, ub, ubB, hsb, hsbB)
        if DBG:
            dq = self.dram("dbg_q", (128, 1024), kind="ExternalOutput")
            self.ctx.op("v", lambda E: E.tensor_copy(out=self.tmpf[:, 0:1024], in_=qstk[:, 0:1024]),
                        reads=[qstkB], writes=[self.tmpfB])
            self.ctx.dma("qx", dq, self.tmpf[:, 0:1024], reads=[self.tmpfB])
            de = self.dram("dbg_e", (128, 64), kind="ExternalOutput")
            self.ctx.op("v", lambda E: E.tensor_copy(out=self.tmpf[:, 1024:1088], in_=esb[:, :]),
                        reads=[esbB], writes=[self.tmpfB])
            self.ctx.dma("qx", de, self.tmpf[:, 1024:1088], reads=[self.tmpfB])
            for nm, th, tb in (("dbg_u", ub, ubB), ("dbg_hs", hsb, hsbB), ("dbg_s", sgb, sgbB)):
                dap = self.dram(nm, (128, T), kind="ExternalOutput")
                self.ctx.op("v", lambda E, th=th: E.tensor_copy(out=self.tmpf[:, :], in_=th[:, :]),
                            reads=[tb], writes=[self.tmpfB])
                self.ctx.dma("qx", dap, self.tmpf[:, :], reads=[self.tmpfB])

        for dt in range(ND):
            for tt in range(NT):
                bi = ctx.bank()
                self.mm(bi, ctx.pb[bi][:, 0:TW], wl[:, O_C + dt * 128: O_C + dt * 128 + 128],
                        hsb[:, tt * TW:(tt + 1) * TW], [wlB, hsbB])
                for kt in range(8):
                    self.mm(bi, ctx.pb[bi][:, 0:TW],
                            wl[:, O_D + kt * D + dt * 128: O_D + kt * D + dt * 128 + 128],
                            xns(kt, tt), [wlB, xnB])
                self.mm(bi, ctx.pb[bi][:, 0:TW], rowb[:, RO["C_b"] + dt * 128:RO["C_b"] + dt * 128 + 128],
                        self.onesf[:, :], [rowbB, self.onesfB], last=True)
                bg = ctx.bank()
                for kt in range(8):
                    self.mm(bg, ctx.pb[bg][:, 0:TW],
                            wl[:, O_G + kt * D + dt * 128: O_G + kt * D + dt * 128 + 128],
                            xns(kt, tt), [wlB, xnB])
                self.mm(bg, ctx.pb[bg][:, 0:TW], rowb[:, RO["g_b"] + dt * 128:RO["g_b"] + dt * 128 + 128],
                        self.onesf[:, :], [rowbB, self.onesfB], last=True)
                ctx.op("s", lambda E, bg=bg: E.activation(
                    out=gt[:, :], in_=ctx.pb[bg][:, 0:TW], func=AF.Sigmoid),
                    reads=[ctx.pbuf[bg]], writes=[gtB])
                ctx.op("v", lambda E, bi=bi: E.tensor_tensor(
                    out=yt[:, :], in0=ctx.pb[bi][:, 0:TW], in1=gt[:, :], op=ALU.mult),
                    reads=[ctx.pbuf[bi], gtB], writes=[ytB])
                ctx.op("v", lambda E, dt=dt, tt=tt: E.tensor_tensor(
                    out=self.xs(dt, tt), in0=self.xs(dt, tt), in1=yt[:, :], op=ALU.add),
                    reads=[ytB], writes=[self.xB])
        ctx.barrier()

    def _emit_scan(self, abm, abmB, rtm, rtmB, qstk, qstkB, esb, esbB, ub, ubB, hsb, hsbB):
        ctx = self.ctx

        def q(i):
            return qstk[:, i * 128:(i + 1) * 128]

        def pv(bi):
            return ctx.pb[bi].ap().bitcast(BF16)

        ctx.op("v", lambda E: E.tensor_copy(out=q(0), in_=self.idb[:, :]),
               reads=[self.idbB], writes=[qstkB])
        bi = ctx.bank()
        ctx.op("t", lambda E, bi=bi: E.transpose(pv(bi)[:, 0:128], abm[:, :], self.idb[:, :]),
               reads=[abmB, self.idbB], waitw=[ctx.pbuf[bi]], writes=[ctx.pbuf[bi]])
        ctx.op("v", lambda E, bi=bi: E.tensor_copy(out=q(1), in_=pv(bi)[:, 0:128]),
               reads=[ctx.pbuf[bi]], writes=[qstkB])

        def prod_block(src_i, rhs_lo, rhs_n, dst_i):
            bj = ctx.bank()
            ctx.op("t", lambda E, bj=bj: E.transpose(pv(bj)[:, 0:128], q(src_i), self.idb[:, :]),
                   reads=[qstkB, self.idbB], waitw=[ctx.pbuf[bj]], writes=[ctx.pbuf[bj]])
            ctx.op("v", lambda E, bj=bj: E.tensor_copy(out=rtm[:, :], in_=pv(bj)[:, 0:128]),
                   reads=[ctx.pbuf[bj]], writes=[rtmB])
            done = 0
            while done < rhs_n:
                cnt = min(4, rhs_n - done)
                bk = ctx.bank()
                self.mm(bk, ctx.pb[bk][:, 0:cnt * 128], rtm[:, :],
                        qstk[:, (rhs_lo + done) * 128:(rhs_lo + done + cnt) * 128],
                        [rtmB, qstkB], last=True)
                ctx.op("v", lambda E, bk=bk, done=done, cnt=cnt: E.tensor_copy(
                    out=qstk[:, (dst_i + done) * 128:(dst_i + done + cnt) * 128],
                    in_=ctx.pb[bk][:, 0:cnt * 128]), reads=[ctx.pbuf[bk]], writes=[qstkB])
                done += cnt

        for s in (1, 2, 4, 8, 16):
            prod_block(s, 1, min(s, SC_C - s), s + 1)
        prev = 32
        for k in range(1, 7):
            prod_block(prev, prev, 1, 32 + k)
            prev = 32 + k

        sb_banks = [ctx.bank() for _ in range(4)]
        ctx.reserved = set(sb_banks)
        for g, bi in enumerate(sb_banks):
            pb = ctx.pb[bi]
            for m in range(SC_C):
                w = SC_C - m
                out_ap = bass.AP(pb[:, :].tensor, m, [[512, 128], [SC_C, SC_PB], [1, w]])
                in_ap = bass.AP(ub[:, :].tensor, g * 512, [[T, 128], [SC_C, SC_PB], [1, w]])
                self.mm(bi, out_ap, q(m), in_ap, [qstkB, ubB], last=(m == SC_C - 1))
            e_ap = bass.AP(pb[:, :].tensor, SC_C - 1, [[512, 128], [SC_C, SC_PB]])
            ctx.op("v", lambda E, e_ap=e_ap, g=g: E.tensor_copy(
                out=esb[:, g * SC_PB:(g + 1) * SC_PB], in_=e_ap),
                reads=[ctx.pbuf[bi]], writes=[esbB])

        k = 0
        s = 1
        while s < SC_NCH:
            gi = 32 if k == 0 else 32 + k
            bj = ctx.bank()
            ctx.op("t", lambda E, bj=bj, gi=gi: E.transpose(
                pv(bj)[:, 0:128], q(gi), self.idb[:, :]),
                reads=[qstkB, self.idbB], waitw=[ctx.pbuf[bj]], writes=[ctx.pbuf[bj]])
            ctx.op("v", lambda E, bj=bj: E.tensor_copy(out=rtm[:, :], in_=pv(bj)[:, 0:128]),
                   reads=[ctx.pbuf[bj]], writes=[rtmB])
            bk = ctx.bank()
            self.mm(bk, ctx.pb[bk][:, 0:SC_NCH], self.idb[:, :], esb[:, 0:SC_NCH],
                    [self.idbB, esbB])
            self.mm(bk, ctx.pb[bk][:, s:SC_NCH], rtm[:, :], esb[:, 0:SC_NCH - s],
                    [rtmB, esbB], last=True)
            ctx.op("v", lambda E, bk=bk: E.tensor_copy(out=esb[:, :], in_=ctx.pb[bk][:, 0:SC_NCH]),
                   reads=[ctx.pbuf[bk]], writes=[esbB])
            s *= 2
            k += 1

        for g, bi in enumerate(sb_banks if not SCAN_NO_CARRY else []):
            pb = ctx.pb[bi]
            pbuf = ctx.pbuf[bi]
            firstmm = True
            for j in range(SC_C):
                if g == 0:
                    out_ap = bass.AP(pb[:, :].tensor, SC_C + j, [[512, 128], [SC_C, SC_PB - 1]])
                    rhs = esb[:, 0:SC_PB - 1]
                else:
                    out_ap = bass.AP(pb[:, :].tensor, j, [[512, 128], [SC_C, SC_PB]])
                    rhs = esb[:, g * SC_PB - 1: (g + 1) * SC_PB - 1]
                tok = ctx.op(
                    "t", lambda E, out_ap=out_ap, rhs=rhs, j=j: E.matmul(
                        out_ap, q(j + 1), rhs, start=False, stop=(j == SC_C - 1),
                        skip_group_check=True),
                    reads=[qstkB, esbB], writes=[pbuf] if j == SC_C - 1 else (),
                    waitw=[pbuf] if firstmm else (), tick=(j == SC_C - 1))
                firstmm = False
        for g, bi in enumerate(sb_banks):
            ctx.op("v", lambda E, g=g, bi=bi: E.tensor_copy(
                out=hsb[:, g * 512:(g + 1) * 512], in_=ctx.pb[bi][:, 0:512]),
                reads=[ctx.pbuf[bi]], writes=[hsbB])
        ctx.reserved = set()

    def emit_l0_ffn(self, scope):
        nc, ctx = self.nc, self.ctx
        w_rt = self.dram("L0f_WgT", (D, 16))
        w_eg = self.dram("L0f_We_gateT", (NEXP, D, MOE_HID))
        w_eu = self.dram("L0f_We_upT", (NEXP, D, MOE_HID))
        w_ed = self.dram("L0f_We_downT", (NEXP, MOE_HID, D))
        w_sg = self.dram("L0f_Ws_gateT", (D, SH_HID))
        w_su = self.dram("L0f_Ws_upT", (D, SH_HID))
        w_sd = self.dram("L0f_Ws_downT", (SH_HID, D))
        w_x = self.dram("L0f_WsgT", (128, 8))
        mask_d = nc.dram_tensor("moe_mask", [NEXP, T], BF16, kind="Internal").ap()
        mask_dB = Buf("mask_d")
        ctx.bufs.append(mask_dB)

        xn, xnB = ctx.sb("xn1", [128, ND * T], BF16, scope)
        wrtf, wrtfB = ctx.sb("wrtf", [128, 128], F32, scope)
        wxf, wxfB = ctx.sb("wxf", [128, 8], BF16, scope)
        wex, wexB = ctx.sb("wex", [128, 24576], BF16, scope)
        wexHB = [Buf("wexh0"), Buf("wexh1")]
        ctx.bufs += wexHB
        hb, hbB = ctx.sb("hb", [128, 8192], BF16, scope)
        hbTB = [Buf(f"hbt{i}") for i in range(NT)]
        ctx.bufs += hbTB
        ltok, ltokB = ctx.sb("ltok", [128, 16], F32, scope)
        mtok, mtokB = ctx.sb("mtok", [128, 16], BF16, scope)
        mfmx, mfmB = ctx.sb("mfm", [128, T], BF16, scope)
        mfm = mfmx[0:16, 0:T]
        maskb0, maskbB0 = ctx.sb("maskb", [128, T], BF16, scope)
        MASKB = [(maskb0, maskbB0), (mfmx, mfmB)]
        sgt, sgtB = ctx.sb("sgt", [128, TW], BF16, scope)
        hut, hutB = ctx.sb("hut", [128, TW], BF16, scope)
        lgf = self.tmpf[0:16, 0:T]
        lgfB = self.tmpfB

        ctx.dma("qw", wrtf[:, :], w_rt.rearrange("(k p) d -> p k d", p=128)[:, :, :],
                writes=[wrtfB])
        ctx.dma("qw", self.stg[:, 0:8], w_x, writes=[self.stgB])
        ctx.op("g", lambda E: E.tensor_copy(out=wxf[:, :], in_=self.stg[:, 0:8]),
               reads=[self.stgB], writes=[wxfB])

        self.emit_ln(xn, xnB)

        def xns(dt, tt):
            return xn[:, dt * T + tt * TW: dt * T + (tt + 1) * TW]

        bi = ctx.bank()
        for kt in range(8):
            self.mm(bi, ctx.pb[bi][0:16, 0:1], wrtf[:, kt * 16:(kt + 1) * 16], self.onescf[:, :],
                    [self.onescfB, wrtfB], last=(kt == 7))
        ctx.op("v", lambda E, bi=bi: E.tensor_scalar(
            out=self.colp[0:16, 0:1], in0=ctx.pb[bi][0:16, 0:1], scalar1=-1.0, scalar2=None,
            op0=ALU.mult), reads=[ctx.pbuf[bi]], writes=[self.colpB])

        for tt in range(NT):
            bi = ctx.bank()
            for kt in range(8):
                self.mm(bi, ctx.pb[bi][0:16, 0:TW], wrtf[:, kt * 16:(kt + 1) * 16],
                        self.xs(kt, tt), [wrtfB, self.xB], last=(kt == 7))
            ctx.op("v", lambda E, tt=tt, bi=bi: E.tensor_tensor(
                out=lgf[:, tt * TW:(tt + 1) * TW], in0=ctx.pb[bi][0:16, 0:TW],
                in1=self.bc1[0:16, tt * TW:(tt + 1) * TW], op=ALU.mult),
                reads=[ctx.pbuf[bi], self.bc1B], writes=[lgfB])
            ctx.op("v", lambda E, tt=tt: E.scalar_tensor_tensor(
                out=lgf[:, tt * TW:(tt + 1) * TW], in0=self.bc2[0:16, tt * TW:(tt + 1) * TW],
                scalar=self.colp[0:16, 0:1], in1=lgf[:, tt * TW:(tt + 1) * TW],
                op0=ALU.mult, op1=ALU.add),
                reads=[self.bc2B, self.colpB, lgfB], writes=[lgfB])

        for i in range(16):
            bi = ctx.bank()
            pb, pbuf = ctx.pb[bi], ctx.pbuf[bi]
            ctx.op("t", lambda E, i=i, pb=pb: E.transpose(
                pb[:, 0:16], lgf[:, i * 128:(i + 1) * 128], self.idf[0:16, 0:16]),
                reads=[lgfB, self.idfB], waitw=[pbuf], writes=[pbuf])
            ctx.op("v", lambda E, pb=pb: E.tensor_copy(
                out=ltok[:, 0:16], in_=pb[:, 0:16]),
                reads=[pbuf], writes=[ltokB])
            L = ltok[:, 0:16]
            ctx.op("v", lambda E, L=L: E.reduce_max(out=self.colp[:, 1:2], in_=L, axis=AX),
                   reads=[ltokB], writes=[self.colpB])
            ctx.op("v", lambda E, L=L: E.tensor_scalar(
                out=self.colp[:, 8:24], in0=L, scalar1=self.colp[:, 1:2], scalar2=None,
                op0=ALU.is_equal), reads=[ltokB, self.colpB], writes=[self.colpB])
            ctx.op("v", lambda E, L=L: E.scalar_tensor_tensor(
                out=self.colp[:, 24:40], in0=self.colp[:, 8:24], scalar=-1e30, in1=L,
                op0=ALU.mult, op1=ALU.add), reads=[self.colpB, ltokB], writes=[self.colpB])
            ctx.op("v", lambda E: E.reduce_max(out=self.colp[:, 2:3], in_=self.colp[:, 24:40],
                                               axis=AX), reads=[self.colpB], writes=[self.colpB])
            ctx.op("v", lambda E: E.tensor_scalar(
                out=self.colp[:, 40:56], in0=self.colp[:, 24:40], scalar1=self.colp[:, 2:3],
                scalar2=None, op0=ALU.is_equal), reads=[self.colpB], writes=[self.colpB])
            ctx.op("v", lambda E: E.tensor_tensor(
                out=self.colp[:, 5:6], in0=self.colp[:, 1:2], in1=self.colp[:, 2:3],
                op=ALU.subtract), reads=[self.colpB], writes=[self.colpB])
            ctx.op("s", lambda E: E.activation(out=self.colp[:, 3:4], in_=self.colp[:, 5:6],
                                               func=AF.Sigmoid),
                   reads=[self.colpB], writes=[self.colpB])
            ctx.op("v", lambda E: E.tensor_scalar(
                out=self.colp[:, 4:5], in0=self.colp[:, 3:4], scalar1=-1.0, scalar2=1.0,
                op0=ALU.mult, op1=ALU.add), reads=[self.colpB], writes=[self.colpB])
            ctx.op("v", lambda E: E.tensor_scalar(
                out=self.colp[:, 8:24], in0=self.colp[:, 8:24], scalar1=self.colp[:, 3:4],
                scalar2=None, op0=ALU.mult), reads=[self.colpB], writes=[self.colpB])
            ctx.op("v", lambda E: E.scalar_tensor_tensor(
                out=mtok[:, 0:16], in0=self.colp[:, 40:56], scalar=self.colp[:, 4:5],
                in1=self.colp[:, 8:24], op0=ALU.mult, op1=ALU.add),
                reads=[self.colpB], writes=[mtokB])
            bj = ctx.bank()
            pbj, pbjB = ctx.pb[bj], ctx.pbuf[bj]
            ctx.op("t", lambda E, pbj=pbj: E.transpose(
                pbj.ap().bitcast(BF16)[0:16, 0:128], mtok[:, 0:16], self.idb[:, :]),
                reads=[mtokB, self.idbB], waitw=[pbjB], writes=[pbjB])
            ctx.op("v", lambda E, i=i, pbj=pbj: E.tensor_copy(
                out=mfm[:, i * 128:(i + 1) * 128], in_=pbj.ap().bitcast(BF16)[0:16, 0:128]),
                reads=[pbjB], writes=[mfmB])
        ctx.dma("qb", mask_d, mfm[:, :], reads=[mfmB], writes=[mask_dB])
        if DBG:
            dm = self.dram("dbg_mask", (16, T), kind="ExternalOutput")
            self.ctx.op("v", lambda E: E.tensor_copy(out=self.tmpf[32:48, 0:T], in_=mfm[:, :]),
                        reads=[mfmB], writes=[self.tmpfB])
            self.ctx.dma("qx", dm, self.tmpf[32:48, 0:T], reads=[self.tmpfB])
            dl = self.dram("dbg_lg", (16, T), kind="ExternalOutput")
            self.ctx.dma("qx", dl, lgf, reads=[lgfB])

        for tt in range(NT):
            bi = ctx.bank()
            for kt in range(8):
                self.mm(bi, ctx.pb[bi][0:1, 0:TW], wxf[:, kt:kt + 1], xns(kt, tt),
                        [wxfB, xnB], last=(kt == 7))
            ctx.op("s", lambda E, tt=tt, bi=bi: E.activation(
                out=self.rows[:, tt * TW:(tt + 1) * TW], in_=ctx.pb[bi][0:1, 0:TW],
                func=AF.Sigmoid), reads=[ctx.pbuf[bi]], writes=[self.rowsB])

        def load_expert(e, half):
            off = half * 12288
            hbuf = wexHB[half]
            ci = 0
            slots = [(self.stg[:, 0:1024], self.stgHB[0], "qw"),
                     (self.stg[:, 1024:2048], self.stgHB[1], "qs"),
                     (self.tmpf[:, 0:1024], self.tmpfB, "qA"),
                     (self.bc2[:, 0:1024], self.bc2B, "qB")]
            for j, wsrc in enumerate((w_eg, w_eu, w_ed)):
                view = wsrc[e].rearrange("(k p) d -> p k d", p=128)
                kn = 2 if j < 2 else 1
                for c in range(4):
                    sl, slb, q = slots[ci % 4]
                    ctx.dma(q, sl, view[:, c * kn:(c + 1) * kn, :], writes=[slb])
                    ctx.op("s", lambda E, j=j, c=c, off=off, sl=sl, slb=slb: E.copy(
                        out=wex[:, off + j * 4096 + c * 1024: off + j * 4096 + (c + 1) * 1024],
                        in_=sl), reads=[slb], writes=[hbuf])
                    ci += 1

        def mrow_ap(e):
            return self.rows.ap().bitcast(BF16)[0:1, 4096 + (e % 2) * 2048: 4096 + (e % 2) * 2048 + T]

        mrowB = [Buf("mrow0"), Buf("mrow1")]
        ctx.bufs += mrowB
        _elist = ([] if MOE_NO_EXPERTS else ([MOE_ONE_EXPERT] if MOE_ONE_EXPERT is not None else list(range(MOE_NUM))))
        if _elist:
            ctx.dma("qb", mrow_ap(_elist[0]), mask_d[_elist[0]:_elist[0] + 1, :],
                    reads=[mask_dB], writes=[mrowB[_elist[0] % 2]])
        if _elist and not MOE_SERIAL:
            load_expert(_elist[0], _elist[0] % 2)
        for e in _elist:
            half = e % 2
            off = half * 12288
            hbuf = wexHB[half]
            maskb, maskbB = MASKB[half]
            if e + 1 < NEXP and len(_elist) > 1:
                ctx.dma("qb", mrow_ap(e + 1), mask_d[e + 1:e + 2, :], reads=[mask_dB],
                        writes=[mrowB[(e + 1) % 2]])
            self.bcast(mrow_ap(e), mrowB[e % 2], maskb[:, :], maskbB, bf16=True)
            if MOE_SERIAL:
                load_expert(e, half)
            elif e + 1 < NEXP and len(_elist) > 1:
                load_expert(e + 1, 1 - half)
            for tt in range(NT):
                for ht in range(4):
                    bg_, bu_ = ctx.bank(), ctx.bank()
                    for kt in range(8):
                        self.mm(bg_, ctx.pb[bg_][:, 0:TW],
                                wex[:, off + kt * MOE_HID + ht * 128: off + kt * MOE_HID + ht * 128 + 128],
                                xns(kt, tt), [hbuf, xnB], last=(kt == 7))
                    for kt in range(8):
                        self.mm(bu_, ctx.pb[bu_][:, 0:TW],
                                wex[:, off + 4096 + kt * MOE_HID + ht * 128: off + 4096 + kt * MOE_HID + ht * 128 + 128],
                                xns(kt, tt), [hbuf, xnB], last=(kt == 7))
                    ctx.op("s", lambda E, bg_=bg_: E.activation(
                        out=sgt[:, :], in_=ctx.pb[bg_][:, 0:TW], func=AF.Silu),
                        reads=[ctx.pbuf[bg_]], writes=[sgtB])
                    ctx.op("v", lambda E, tt=tt, bu_=bu_: E.tensor_tensor(
                        out=hut[:, :], in0=ctx.pb[bu_][:, 0:TW],
                        in1=maskb[:, tt * TW:(tt + 1) * TW], op=ALU.mult),
                        reads=[ctx.pbuf[bu_], maskbB], writes=[hutB])
                    ctx.op("v", lambda E, tt=tt, ht=ht: E.tensor_tensor(
                        out=hb[:, ht * T + tt * TW: ht * T + (tt + 1) * TW], in0=sgt[:, :],
                        in1=hut[:, :], op=ALU.mult), reads=[sgtB, hutB], writes=[hbTB[tt]])
                for dto in range(ND):
                    by = ctx.bank()
                    for kt in range(4):
                        self.mm(by, ctx.pb[by][:, 0:TW],
                                wex[:, off + 8192 + kt * D + dto * 128: off + 8192 + kt * D + dto * 128 + 128],
                                hb[:, kt * T + tt * TW: kt * T + (tt + 1) * TW],
                                [hbuf, hbTB[tt]], last=(kt == 3))
                    ctx.op("v", lambda E, dto=dto, tt=tt, by=by: E.tensor_tensor(
                        out=self.xs(dto, tt), in0=self.xs(dto, tt), in1=ctx.pb[by][:, 0:TW],
                        op=ALU.add), reads=[ctx.pbuf[by]], writes=[self.xB])

        self.bcast(self.rows[:, 0:T], self.rowsB, self.bc1[:, :], self.bc1B)
        def load_shared(qq, shalf):
            soff = shalf * 12288
            ci = 0
            slots = [(self.stg[:, 0:1024], self.stgHB[0], "qw", 0),
                     (self.stg[:, 1024:2048], self.stgHB[1], "qs", 1024),
                     (self.tmpf[:, 0:1024], self.tmpfB, "qA", -1),
                     (self.bc2[:, 0:1024], self.bc2B, "qB", -1)]
            for j, wsrc in enumerate((w_sg, w_su)):
                view = wsrc.rearrange("(k p) d -> p k d", p=128)
                for c in range(4):
                    sl, slb, q, _ = slots[ci % 4]
                    for kt in range(c * 2, c * 2 + 2):
                        ctx.dma(q, sl[:, (kt % 2) * 512:(kt % 2 + 1) * 512],
                                view[:, kt, qq * 512: qq * 512 + 512], writes=[slb])
                    ctx.op("s", lambda E, sl=sl, j=j, c=c, slb=slb, soff=soff: E.copy(
                        out=wex[:, soff + j * 4096 + c * 1024: soff + j * 4096 + (c + 1) * 1024],
                        in_=sl), reads=[slb], writes=[wexHB[shalf]])
                    ci += 1
            viewd = w_sd.rearrange("(k p) d -> p k d", p=128)
            for c in range(4):
                sl, slb, q, _ = slots[ci % 4]
                ctx.dma(q, sl, viewd[:, qq * 4 + c: qq * 4 + c + 1, :], writes=[slb])
                ctx.op("s", lambda E, c=c, sl=sl, slb=slb, soff=soff: E.copy(
                    out=wex[:, soff + 8192 + c * 1024: soff + 8192 + (c + 1) * 1024], in_=sl),
                    reads=[slb], writes=[wexHB[shalf]])
                ci += 1

        if not MOE_NO_SHARED:
            load_shared(0, 0)
        for qq in (range(4) if not MOE_NO_SHARED else []):
            shalf = qq % 2
            soff = shalf * 12288
            sbuf_w = wexHB[shalf]
            if qq + 1 < 4:
                load_shared(qq + 1, 1 - shalf)
            for tt in range(NT):
                for ht in range(4):
                    bg_, bu_ = ctx.bank(), ctx.bank()
                    for kt in range(8):
                        self.mm(bg_, ctx.pb[bg_][:, 0:TW],
                                wex[:, soff + kt * 512 + ht * 128: soff + kt * 512 + ht * 128 + 128],
                                xns(kt, tt), [sbuf_w, xnB], last=(kt == 7))
                    for kt in range(8):
                        self.mm(bu_, ctx.pb[bu_][:, 0:TW],
                                wex[:, soff + 4096 + kt * 512 + ht * 128: soff + 4096 + kt * 512 + ht * 128 + 128],
                                xns(kt, tt), [sbuf_w, xnB], last=(kt == 7))
                    ctx.op("s", lambda E, bg_=bg_: E.activation(
                        out=sgt[:, :], in_=ctx.pb[bg_][:, 0:TW], func=AF.Silu),
                        reads=[ctx.pbuf[bg_]], writes=[sgtB])
                    ctx.op("v", lambda E, tt=tt, bu_=bu_: E.tensor_tensor(
                        out=hut[:, :], in0=ctx.pb[bu_][:, 0:TW],
                        in1=self.bc1[:, tt * TW:(tt + 1) * TW], op=ALU.mult),
                        reads=[ctx.pbuf[bu_], self.bc1B], writes=[hutB])
                    ctx.op("v", lambda E, tt=tt, ht=ht: E.tensor_tensor(
                        out=hb[:, ht * T + tt * TW: ht * T + (tt + 1) * TW], in0=sgt[:, :],
                        in1=hut[:, :], op=ALU.mult), reads=[sgtB, hutB], writes=[hbTB[tt]])
                for dto in range(ND):
                    by = ctx.bank()
                    for kt in range(4):
                        self.mm(by, ctx.pb[by][:, 0:TW],
                                wex[:, soff + 8192 + kt * D + dto * 128: soff + 8192 + kt * D + dto * 128 + 128],
                                hb[:, kt * T + tt * TW: kt * T + (tt + 1) * TW],
                                [sbuf_w, hbTB[tt]], last=(kt == 3))
                    ctx.op("v", lambda E, dto=dto, tt=tt, by=by: E.tensor_tensor(
                        out=self.xs(dto, tt), in0=self.xs(dto, tt), in1=ctx.pb[by][:, 0:TW],
                        op=ALU.add), reads=[ctx.pbuf[by]], writes=[self.xB])
        ctx.barrier()

    def emit_l1_attn(self, scope):
        ctx = self.ctx
        w_ch = self.dram("L1a_W_chargeT", (128, 8))
        w_sh = self.dram("L1a_W_shellT", (D, 64))
        w_om = self.dram("L1a_omegaT", (64, 32))
        w_cmb = self.dram("L1a_W_combineT", (320, 64))
        w_v = self.dram("L1a_W_vT", (D, D))
        w_o = self.dram("L1a_W_outT", (D, D))
        bv_c = self.dram("L1a_b_v_col", (128, 9))
        r_bch = self.dram("L1a_b_charge", (1, 1))
        r_bsh = self.dram("L1a_b_shell", (1, 64))
        r_rff = self.dram("L1a_rff_bias", (1, 32))
        r_bo = self.dram("L1a_b_out", (1, D))
        r_lt = self.dram("L1a_log_temp", (1, 1))

        vsb, vsbB = ctx.sb("vsb", [128, 16 * 1024], BF16, scope)
        phq, phqB = ctx.sb("phq", [64, T], BF16, scope)
        phk, phkB = ctx.sb("phk", [64, T], BF16, scope)
        phis, phisB = ctx.sb("phis", [64, T], BF16, scope)
        rowb, rowbB = ctx.sb("rowb2", [1, 2176], F32, scope)
        RO_BSH, RO_RFF, RO_BO, RO_BIAS, RO_M = 0, 64, 96, 1120, 2144
        bvcol, bvcolB = ctx.sb("bvcol", [128, 8], BF16, scope)
        RQ, RS = 0, 2048

        with ExitStack() as p1:
            xn, xnB = ctx.sb("xn2", [128, ND * T], BF16, p1)
            wv, wvB = ctx.sb("wv", [128, 8192], BF16, p1)
            wsm, wsmB = ctx.sb("wsm", [128, 768], BF16, p1)
            O_CH, O_SH, O_OM, O_CMB = 0, 8, 520, 552
            shl, shlB = ctx.sb("shl", [64, T], BF16, p1)

            self.load_w(w_v, 8, D, wv, wvB, 0)
            ctx.dma("qw", self.stg[:, 0:8], w_ch, writes=[self.stgB])
            ctx.op("g", lambda E: E.tensor_copy(out=wsm[:, O_CH:O_CH + 8], in_=self.stg[:, 0:8]),
                   reads=[self.stgB], writes=[wsmB])
            self.load_w(w_sh, 8, 64, wsm, wsmB, O_SH)
            ctx.dma("qw", self.stg[0:64, 0:32], w_om, writes=[self.stgB])
            ctx.op("g", lambda E: E.tensor_copy(out=wsm[0:64, O_OM:O_OM + 32],
                                                in_=self.stg[0:64, 0:32]),
                   reads=[self.stgB], writes=[wsmB])
            for kt in range(3):
                kk = 128 if kt < 2 else 64
                ctx.dma("qw", self.stg[0:kk, 64:128], w_cmb[kt * 128:kt * 128 + kk, :],
                        writes=[self.stgB])
                ctx.op("g", lambda E, kt=kt, kk=kk: E.tensor_copy(
                    out=wsm[0:kk, O_CMB + kt * 64:O_CMB + (kt + 1) * 64],
                    in_=self.stg[0:kk, 64:128]), reads=[self.stgB], writes=[wsmB])
            ctx.dma("qw", self.stg[:, 0:9], bv_c, writes=[self.stgB])
            ctx.op("g", lambda E: E.tensor_copy(out=bvcol[:, :], in_=self.stg[:, 0:8]),
                   reads=[self.stgB], writes=[bvcolB])
            ctx.op("g", lambda E: E.tensor_copy(out=self.colp[0:32, 6:7], in_=self.stg[0:32, 8:9]),
                   reads=[self.stgB], writes=[self.colpB])
            for r_d, off, w in ((r_bsh, RO_BSH, 64), (r_rff, RO_RFF, 32), (r_bo, RO_BO, D),
                                (r_bch, RO_M, 1), (r_lt, RO_M + 1, 1)):
                ctx.dma("qw", rowb[:, off:off + w], r_d, writes=[rowbB])

            self.emit_ln(xn, xnB)

            def xns(dt, tt):
                return xn[:, dt * T + tt * TW: dt * T + (tt + 1) * TW]

            for i in range(16):
                for dh in range(2):
                    bi = ctx.bank()
                    for kt in range(8):
                        self.mm(bi, ctx.pb[bi][:, 0:TW],
                                xn[:, kt * T + i * 128: kt * T + (i + 1) * 128],
                                wv[:, kt * D + dh * TW: kt * D + (dh + 1) * TW],
                                [xnB, wvB], last=(kt == 7))
                    ctx.op("s", lambda E, i=i, dh=dh, bi=bi: E.copy(
                        out=vsb[:, i * 1024 + dh * TW: i * 1024 + (dh + 1) * TW],
                        in_=ctx.pb[bi][:, 0:TW]), reads=[ctx.pbuf[bi]], writes=[vsbB])

            for tt in range(NT):
                bi = ctx.bank()
                for kt in range(8):
                    self.mm(bi, ctx.pb[bi][0:1, 0:TW], wsm[:, O_CH + kt:O_CH + kt + 1],
                            xns(kt, tt), [wsmB, xnB])
                self.mm(bi, ctx.pb[bi][0:1, 0:TW], rowb[:, RO_M:RO_M + 1], self.onesf[:, :],
                        [rowbB, self.onesfB], last=True)
                ctx.op("s", lambda E, tt=tt, bi=bi: E.activation(
                    out=self.rows[:, RQ + tt * TW:RQ + (tt + 1) * TW], in_=ctx.pb[bi][0:1, 0:TW],
                    func=AF.Tanh), reads=[ctx.pbuf[bi]], writes=[self.rowsB])

            sqv = self.bc1.ap().bitcast(BF16)
            for tt in range(NT):
                bi = ctx.bank()
                for kt in range(8):
                    self.mm(bi, ctx.pb[bi][0:64, 0:TW], wsm[:, O_SH + kt * 64:O_SH + (kt + 1) * 64],
                            xns(kt, tt), [wsmB, xnB])
                self.mm(bi, ctx.pb[bi][0:64, 0:TW], rowb[:, RO_BSH:RO_BSH + 64], self.onesf[:, :],
                        [rowbB, self.onesfB], last=True)
                ctx.op("s", lambda E, tt=tt, bi=bi: E.copy(
                    out=shl[:, tt * TW:(tt + 1) * TW], in_=ctx.pb[bi][0:64, 0:TW]),
                    reads=[ctx.pbuf[bi]], writes=[shlB])
                ctx.op("s", lambda E, tt=tt, bi=bi: E.activation(
                    out=sqv[0:64, tt * TW:(tt + 1) * TW], in_=ctx.pb[bi][0:64, 0:TW],
                    func=AF.Square), reads=[ctx.pbuf[bi]], writes=[self.bc1B])
                bj = ctx.bank()
                self.mm(bj, ctx.pb[bj][0:1, 0:TW], self.onescb[0:64, :],
                        sqv[0:64, tt * TW:(tt + 1) * TW], [self.onescbB, self.bc1B], last=True)
                ctx.op("s", lambda E, tt=tt, bj=bj: E.activation(
                    out=self.rows[:, RS + tt * TW:RS + (tt + 1) * TW], in_=ctx.pb[bj][0:1, 0:TW],
                    func=AF.Sqrt), reads=[ctx.pbuf[bj]], writes=[self.rowsB])
            ctx.op("v", lambda E: E.tensor_scalar(
                out=self.rows[:, RS:RS + T], in0=self.rows[:, RS:RS + T], scalar1=1e-12,
                scalar2=None, op0=ALU.max), reads=[self.rowsB], writes=[self.rowsB])
            ctx.op("v", lambda E: E.reciprocal(
                out=self.rows[:, RS:RS + T], in_=self.rows[:, RS:RS + T]),
                reads=[self.rowsB], writes=[self.rowsB])
            self.bcast(self.rows[:, RS:RS + T], self.rowsB, self.bc1[0:32, 0:T], self.bc1B,
                       nrows=32)
            prj = self.tmpf[0:32, 0:T]
            for tt in range(NT):
                bi = ctx.bank()
                self.mm(bi, ctx.pb[bi][0:32, 0:TW], wsm[0:64, O_OM:O_OM + 32],
                        shl[:, tt * TW:(tt + 1) * TW], [wsmB, shlB], last=True)
                ctx.op("v", lambda E, tt=tt, bi=bi: E.tensor_tensor(
                    out=prj[:, tt * TW:(tt + 1) * TW], in0=ctx.pb[bi][0:32, 0:TW],
                    in1=self.bc1[0:32, tt * TW:(tt + 1) * TW], op=ALU.mult),
                    reads=[ctx.pbuf[bi], self.bc1B], writes=[self.tmpfB])
            ctx.op("v", lambda E: E.tensor_scalar(
                out=prj[:, :], in0=prj[:, :], scalar1=self.colp[0:32, 6:7], scalar2=None,
                op0=ALU.add), reads=[self.tmpfB, self.colpB], writes=[self.tmpfB])
            prji = self.bc2.ap().bitcast(I32)[32:64, 0:T]
            tred = self.tmpf[32:64, 0:T]
            inv2pi = 1.0 / (2 * math.pi)
            for rowlo, shift in ((32, 0.0), (0, 0.25)):
                ctx.op("v", lambda E, shift=shift, tred=tred: E.tensor_scalar(
                    out=tred, in0=prj[:, :], scalar1=inv2pi, scalar2=shift,
                    op0=ALU.mult, op1=ALU.add), reads=[self.tmpfB], writes=[self.tmpfB])
                ctx.op("v", lambda E, tred=tred: E.tensor_scalar(
                    out=prji[:, :], in0=tred, scalar1=-0.5, scalar2=None,
                    op0=ALU.add), reads=[self.tmpfB], writes=[self.bc2B])
                ctx.op("v", lambda E, tred=tred: E.tensor_tensor(
                    out=tred, in0=tred, in1=prji[:, :],
                    op=ALU.subtract), reads=[self.tmpfB, self.bc2B], writes=[self.tmpfB])
                ctx.op("s", lambda E, rowlo=rowlo, tred=tred: E.activation(
                    out=phis[rowlo:rowlo + 32, 0:T], in_=tred, func=AF.Sin,
                    scale=2 * math.pi, bias=-math.pi), reads=[self.tmpfB], writes=[phisB])

        ctx.barrier()
        with ExitStack() as p1b:
            tp1, tp1B = ctx.sb("tp1", [128, T], BF16, p1b)
            tp2, tp2B = ctx.sb("tp2", [128, T], BF16, p1b)
            tp3, tp3B = ctx.sb("tp3", [64, T], BF16, p1b)
            wcm, wcmB = ctx.sb("wcm", [128, 192], BF16, p1b)
            for kt in range(3):
                kk = 128 if kt < 2 else 64
                ctx.dma("qw", self.stg[0:kk, 0:64], w_cmb[kt * 128:kt * 128 + kk, :],
                        writes=[self.stgB])
                ctx.op("g", lambda E, kt=kt, kk=kk: E.tensor_copy(
                    out=wcm[0:kk, kt * 64:(kt + 1) * 64], in_=self.stg[0:kk, 0:64]),
                    reads=[self.stgB], writes=[wcmB])
            scl = [-RFF_SCALE * m for m in MAGS]
            targets = [(tp1, tp1B, 0), (tp1, tp1B, 64), (tp2, tp2B, 0), (tp2, tp2B, 64),
                       (tp3, tp3B, 0)]
            for m in range(5):
                tph, tphB, prow = targets[m]
                if m == 0:
                    ctx.op("v", lambda E: E.memset(self.rows[:, RS:RS + T], scl[0]),
                           writes=[self.rowsB])
                else:
                    if m == 1:
                        ctx.op("v", lambda E: E.tensor_copy(
                            out=self.tmpf[0:1, 0:T], in_=self.rows[:, RQ:RQ + T]),
                            reads=[self.rowsB], writes=[self.tmpfB])
                    else:
                        ctx.op("v", lambda E: E.tensor_tensor(
                            out=self.tmpf[0:1, 0:T], in0=self.tmpf[0:1, 0:T],
                            in1=self.rows[:, RQ:RQ + T], op=ALU.mult),
                            reads=[self.tmpfB, self.rowsB], writes=[self.tmpfB])
                    ctx.op("v", lambda E, m=m: E.tensor_scalar(
                        out=self.rows[:, RS:RS + T], in0=self.tmpf[0:1, 0:T], scalar1=scl[m],
                        scalar2=None, op0=ALU.mult), reads=[self.tmpfB], writes=[self.rowsB])
                self.bcast(self.rows[:, RS:RS + T], self.rowsB, self.bc2[0:64, 0:T],
                           self.bc2B, nrows=64)
                ctx.op("v", lambda E, tph=tph, prow=prow: E.tensor_tensor(
                    out=tph[prow:prow + 64, 0:T], in0=phis[:, :], in1=self.bc2[0:64, 0:T],
                    op=ALU.mult), reads=[phisB, self.bc2B], writes=[tphB])
            if DBG:
                dtp = self.dram("dbg_tp1", (128, T), BF16, kind="ExternalOutput")
                self.ctx.dma("qx", dtp, tp1[:, :], reads=[tp1B])
                dtp2 = self.dram("dbg_tp2", (128, T), BF16, kind="ExternalOutput")
                self.ctx.dma("qx", dtp2, tp2[:, :], reads=[tp2B])
                dtp3 = self.dram("dbg_tp3", (64, T), BF16, kind="ExternalOutput")
                self.ctx.dma("qx", dtp3, tp3[:, :], reads=[tp3B])
                dwcm = self.dram("dbg_wcm", (128, 192), BF16, kind="ExternalOutput")
                self.ctx.dma("qx", dwcm, wcm[:, :], reads=[wcmB])
            phiT = self.tmpf[64:128, 0:T]
            for tt in range(NT):
                bi = ctx.bank()
                srcs = [(tp1, tp1B, 128, 0), (tp2, tp2B, 128, 1), (tp3, tp3B, 64, 2)]
                for j, (tph, tphB, kk, kt) in enumerate(srcs):
                    self.mm(bi, ctx.pb[bi][0:64, 0:TW], wcm[0:kk, kt * 64:(kt + 1) * 64],
                            tph[0:kk, tt * TW:(tt + 1) * TW], [tphB, wcmB], last=(j == 2))
                ctx.op("v", lambda E, tt=tt, bi=bi: E.tensor_copy(
                    out=phiT[:, tt * TW:(tt + 1) * TW], in_=ctx.pb[bi][0:64, 0:TW]),
                    reads=[ctx.pbuf[bi]], writes=[self.tmpfB])

            M = RO_M
            ctx.op("s", lambda E: E.activation(
                out=rowb[:, M + 2:M + 3], in_=rowb[:, M + 1:M + 2], func=AF.Abs),
                reads=[rowbB], writes=[rowbB])
            ctx.op("s", lambda E: E.activation(
                out=rowb[:, M + 3:M + 4], in_=rowb[:, M + 2:M + 3], func=AF.Exp, scale=-1.0),
                reads=[rowbB], writes=[rowbB])
            ctx.op("s", lambda E: E.activation(
                out=rowb[:, M + 3:M + 4], in_=rowb[:, M + 3:M + 4], func=AF.Ln, bias=1.0,
                scale=1.0), reads=[rowbB], writes=[rowbB])
            ctx.op("v", lambda E: E.tensor_scalar(
                out=rowb[:, M + 4:M + 5], in0=rowb[:, M + 1:M + 2], scalar1=0.0, scalar2=None,
                op0=ALU.max), reads=[rowbB], writes=[rowbB])
            ctx.op("v", lambda E: E.tensor_tensor(
                out=rowb[:, M + 4:M + 5], in0=rowb[:, M + 4:M + 5], in1=rowb[:, M + 3:M + 4],
                op=ALU.add), reads=[rowbB], writes=[rowbB])
            ctx.op("v", lambda E: E.tensor_scalar(
                out=rowb[:, M + 4:M + 5], in0=rowb[:, M + 4:M + 5], scalar1=0.1, scalar2=None,
                op0=ALU.add), reads=[rowbB], writes=[rowbB])
            ctx.op("v", lambda E: E.reciprocal(out=rowb[:, M + 5:M + 6], in_=rowb[:, M + 4:M + 5]),
                   reads=[rowbB], writes=[rowbB])
            ctx.dma("qw", self.rows[:, RS:RS + T], self.ramps_d[0:1, :], writes=[self.rowsB])
            ctx.op("v", lambda E: E.tensor_scalar(
                out=self.rows[:, RS:RS + T], in0=self.rows[:, RS:RS + T],
                scalar1=rowb[:, M + 5:M + 6], scalar2=None, op0=ALU.mult),
                reads=[self.rowsB, rowbB], writes=[self.rowsB])
            self.bcast(self.rows[:, RS:RS + T], self.rowsB, self.bc1[64:128, 0:T], self.bc1B,
                       nrows=64)
            ctx.dma("qw", self.rows[:, RS:RS + T], self.ramps_d[1:2, :], writes=[self.rowsB])
            self.bcast(self.rows[:, RS:RS + T], self.rowsB, self.bc2[64:128, 0:T], self.bc2B,
                       nrows=64)
            ctx.op("v", lambda E: E.tensor_tensor(
                out=phq[:, :], in0=phiT[:, :], in1=self.bc1[64:128, 0:T], op=ALU.mult),
                reads=[self.tmpfB, self.bc1B], writes=[phqB])
            ctx.op("v", lambda E: E.tensor_tensor(
                out=phk[:, :], in0=phiT[:, :], in1=self.bc2[64:128, 0:T], op=ALU.mult),
                reads=[self.tmpfB, self.bc2B], writes=[phkB])

        if DBG:
            dprj = self.dram("dbg_prj", (32, T), kind="ExternalOutput")
            self.ctx.dma("qx", dprj, self.tmpf[0:32, 0:T], reads=[self.tmpfB])
            dphiT = self.dram("dbg_phiT", (64, T), kind="ExternalOutput")
            self.ctx.dma("qx", dphiT, self.tmpf[64:128, 0:T], reads=[self.tmpfB])
            dbc1 = self.dram("dbg_rampq", (64, T), kind="ExternalOutput")
            self.ctx.dma("qx", dbc1, self.bc1[64:128, 0:T], reads=[self.bc1B])
            for nm, th, tb, pp in (("dbg_phq", phq, phqB, 64), ("dbg_phk", phk, phkB, 64),
                                   ("dbg_phis", phis, phisB, 64)):
                dap = self.dram(nm, (pp, T), BF16, kind="ExternalOutput")
                self.ctx.dma("qx", dap, th[0:pp, 0:T], reads=[tb])
            dq = self.dram("dbg_qrow", (1, T), kind="ExternalOutput")
            self.ctx.dma("qx", dq, self.rows[:, RQ:RQ + T], reads=[self.rowsB])

        ctx.barrier()
        with ExitStack() as p2:
            wo, woB = ctx.sb("wo", [128, 8192], BF16, p2)
            expt, exptB = ctx.sb("expt", [128, 16 * 128], BF16, p2)
            otq, otqB = ctx.sb("otq", [128, 1024], BF16, p2)
            self.load_w(w_o, 8, D, wo, woB, 0)
            for half in range(2):
                bi = ctx.bank()
                for kt in range(8):
                    self.mm(bi, ctx.pb[bi][0:1, 0:TW], bvcol[:, kt:kt + 1],
                            wo[:, kt * D + half * TW: kt * D + (half + 1) * TW],
                            [bvcolB, woB], last=(kt == 7))
                ctx.op("v", lambda E, half=half, bi=bi: E.tensor_copy(
                    out=rowb[:, RO_BIAS + half * TW:RO_BIAS + (half + 1) * TW],
                    in_=ctx.pb[bi][0:1, 0:TW]), reads=[ctx.pbuf[bi]], writes=[rowbB])
            ctx.op("v", lambda E: E.tensor_tensor(
                out=rowb[:, RO_BIAS:RO_BIAS + D], in0=rowb[:, RO_BIAS:RO_BIAS + D],
                in1=rowb[:, RO_BO:RO_BO + D], op=ALU.add), reads=[rowbB], writes=[rowbB])

            for qt in range(16):
                for kt in range(qt + 1):
                    bl = ctx.bank()
                    self.mm(bl, ctx.pb[bl][:, 0:128], phk[:, kt * 128:(kt + 1) * 128],
                            phq[:, qt * 128:(qt + 1) * 128], [phkB, phqB], last=True)
                    ctx.op("s", lambda E, kt=kt, bl=bl: E.activation(
                        out=expt[:, kt * 128:(kt + 1) * 128], in_=ctx.pb[bl][:, 0:128],
                        func=AF.Exp), reads=[ctx.pbuf[bl]], writes=[exptB])
                    if kt == qt:
                        ctx.op("v", lambda E, kt=kt: E.tensor_tensor(
                            out=expt[:, kt * 128:(kt + 1) * 128],
                            in0=expt[:, kt * 128:(kt + 1) * 128], in1=self.trib[:, :],
                            op=ALU.mult), reads=[exptB, self.tribB], writes=[exptB])
                bs = ctx.bank()
                for kt in range(qt + 1):
                    self.mm(bs, ctx.pb[bs][0:1, 0:128], self.onescb[:, :],
                            expt[:, kt * 128:(kt + 1) * 128], [self.onescbB, exptB],
                            last=(kt == qt))
                ctx.op("v", lambda E, qt=qt, bs=bs: E.tensor_copy(
                    out=self.rows[:, RS + qt * 128:RS + (qt + 1) * 128],
                    in_=ctx.pb[bs][0:1, 0:128]), reads=[ctx.pbuf[bs]], writes=[self.rowsB])
                ctx.op("v", lambda E, qt=qt: E.reciprocal(
                    out=self.rows[:, RS + qt * 128:RS + (qt + 1) * 128],
                    in_=self.rows[:, RS + qt * 128:RS + (qt + 1) * 128]),
                    reads=[self.rowsB], writes=[self.rowsB])
                self.bcast(self.rows[:, RS + qt * 128:RS + (qt + 1) * 128], self.rowsB,
                           self.bc1[:, qt * 128:(qt + 1) * 128], self.bc1B, width=128)
                for bq in range(2):
                    bo = ctx.bank()
                    for kt in range(qt + 1):
                        for dv in range(4):
                            dvg = bq * 4 + dv
                            self.mm(bo, ctx.pb[bo][:, dv * 128:(dv + 1) * 128],
                                    vsb[:, kt * 1024 + dvg * 128: kt * 1024 + (dvg + 1) * 128],
                                    expt[:, kt * 128:(kt + 1) * 128], [vsbB, exptB],
                                    last=(kt == qt and dv == 3))
                    for dv in range(4):
                        dvg = bq * 4 + dv
                        ctx.op("v", lambda E, dv=dv, dvg=dvg, bo=bo, qt=qt: E.tensor_tensor(
                            out=otq[:, dvg * 128:(dvg + 1) * 128],
                            in0=ctx.pb[bo][:, dv * 128:(dv + 1) * 128],
                            in1=self.bc1[:, qt * 128:(qt + 1) * 128], op=ALU.mult),
                            reads=[ctx.pbuf[bo], self.bc1B], writes=[otqB])
                for bq in range(2):
                    bz = ctx.bank()
                    for dtl in range(4):
                        dt = bq * 4 + dtl
                        for kv in range(8):
                            self.mm(bz, ctx.pb[bz][:, dtl * 128:(dtl + 1) * 128],
                                    wo[:, kv * D + dt * 128: kv * D + dt * 128 + 128],
                                    otq[:, kv * 128:(kv + 1) * 128], [woB, otqB])
                        self.mm(bz, ctx.pb[bz][:, dtl * 128:(dtl + 1) * 128],
                                rowb[:, RO_BIAS + dt * 128:RO_BIAS + dt * 128 + 128],
                                self.onesf[:, 0:128], [rowbB, self.onesfB],
                                last=(dtl == 3))
                    for dtl in range(4):
                        dt = bq * 4 + dtl
                        ctx.op("v", lambda E, dt=dt, dtl=dtl, qt=qt, bz=bz: E.tensor_tensor(
                            out=self.x[:, dt * T + qt * 128: dt * T + (qt + 1) * 128],
                            in0=self.x[:, dt * T + qt * 128: dt * T + (qt + 1) * 128],
                            in1=ctx.pb[bz][:, dtl * 128:(dtl + 1) * 128], op=ALU.add),
                            reads=[ctx.pbuf[bz]], writes=[self.xB])
        ctx.barrier()

    def emit_l1_ffn(self, scope):
        ctx = self.ctx
        w_f = self.dram("L1f_weightT", (D, D))
        r_bf = self.dram("L1f_bias", (1, D))
        r_al = self.dram("L1f_alpha", (1, 1))

        xn, xnB = ctx.sb("xn3", [128, ND * T], BF16, scope)
        wstg, wstgB = ctx.sb("wf32", [128, 8192], F32, scope)
        wt, wtB = ctx.sb("wtern", [128, 8192], BF16, scope)
        sgn, sgnB = ctx.sb("sgn1f", [128, 8192], BF16, scope)
        rowb, rowbB = ctx.sb("rowb3", [1, 2048], F32, scope)

        view = w_f.rearrange("(k p) d -> p k d", p=128)
        for c in range(4):
            ctx.dma("qw", wstg[:, c * 2048:(c + 1) * 2048], view[:, c * 2:(c + 1) * 2, :],
                    writes=[wstgB])
        ctx.dma("qw", rowb[:, 0:D], r_bf, writes=[rowbB])
        ctx.dma("qw", rowb[:, D:D + 1], r_al, writes=[rowbB])

        def xns(dt, tt):
            return xn[:, dt * T + tt * TW: dt * T + (tt + 1) * TW]

        ctx.op("v", lambda E: E.tensor_reduce(
            out=self.colp[:, 0:1], in_=wstg[:, :], axis=AX, op=ALU.add,
            apply_absolute_value=True), reads=[wstgB], writes=[self.colpB])
        bi = ctx.bank()
        self.mm(bi, ctx.pb[bi][0:1, 0:1], self.colp[:, 0:1], self.onescf[:, :],
                [self.colpB, self.onescfB], last=True)
        ctx.op("v", lambda E, bi=bi: E.tensor_scalar(
            out=rowb[:, D + 2:D + 3], in0=ctx.pb[bi][0:1, 0:1], scalar1=0.7 / (D * D),
            scalar2=None, op0=ALU.mult), reads=[ctx.pbuf[bi]], writes=[rowbB])
        ctx.op("v", lambda E: E.reciprocal(out=rowb[:, D + 3:D + 4], in_=rowb[:, D:D + 1]),
               reads=[rowbB], writes=[rowbB])
        bj = ctx.bank()
        self.mm(bj, ctx.pb[bj][:, 0:1], self.onesf[:, 0:128], rowb[:, D + 2:D + 3],
                [self.onesfB, rowbB], last=True)
        ctx.op("v", lambda E, bj=bj: E.tensor_copy(out=self.colp[:, 1:2], in_=ctx.pb[bj][:, 0:1]),
               reads=[ctx.pbuf[bj]], writes=[self.colpB])
        bj = ctx.bank()
        self.mm(bj, ctx.pb[bj][:, 0:1], self.onesf[:, 0:128], rowb[:, D:D + 1],
                [self.onesfB, rowbB], last=True)
        ctx.op("v", lambda E, bj=bj: E.tensor_copy(out=self.colp[:, 2:3], in_=ctx.pb[bj][:, 0:1]),
               reads=[ctx.pbuf[bj]], writes=[self.colpB])
        ctx.op("s", lambda E: E.activation(out=sgn[:, :], in_=wstg[:, :], func=AF.Sign),
               reads=[wstgB], writes=[sgnB])
        ctx.op("s", lambda E: E.activation(out=wstg[:, :], in_=wstg[:, :], func=AF.Abs),
               reads=[wstgB], writes=[wstgB])
        ctx.op("v", lambda E: E.tensor_scalar(
            out=wstg[:, :], in0=wstg[:, :], scalar1=self.colp[:, 1:2], scalar2=None,
            op0=ALU.is_gt), reads=[wstgB, self.colpB], writes=[wstgB])
        ctx.op("v", lambda E: E.tensor_tensor(
            out=wt[:, :], in0=sgn[:, :], in1=wstg[:, :], op=ALU.mult),
            reads=[sgnB, wstgB], writes=[wtB])
        ctx.op("v", lambda E: E.tensor_scalar(
            out=rowb[:, 0:D], in0=rowb[:, 0:D], scalar1=rowb[:, D + 3:D + 4], scalar2=None,
            op0=ALU.mult), reads=[rowbB], writes=[rowbB])
        self.emit_ln(xn, xnB)
        for dt in range(ND):
            for tt in range(NT):
                bi = ctx.bank()
                for kt in range(8):
                    self.mm(bi, ctx.pb[bi][:, 0:TW],
                            wt[:, kt * D + dt * 128: kt * D + dt * 128 + 128], xns(kt, tt),
                            [wtB, xnB])
                self.mm(bi, ctx.pb[bi][:, 0:TW], rowb[:, dt * 128:dt * 128 + 128],
                        self.onesf[:, :], [rowbB, self.onesfB], last=True)
                ctx.op("v", lambda E, dt=dt, tt=tt, bi=bi: E.scalar_tensor_tensor(
                    out=self.xs(dt, tt), in0=ctx.pb[bi][:, 0:TW], scalar=self.colp[:, 2:3],
                    in1=self.xs(dt, tt), op0=ALU.mult, op1=ALU.add),
                    reads=[ctx.pbuf[bi], self.colpB], writes=[self.xB])
        ctx.barrier()


BLOCKS = {
    0: lambda m, sc: m.emit_l0_attn(sc),
    1: lambda m, sc: m.emit_l0_ffn(sc),
    2: lambda m, sc: m.emit_l1_attn(sc),
    3: lambda m, sc: m.emit_l1_ffn(sc),
}

LAUNCH_PLAN = [[0, 1, 2, 3]]
DBG = False
SCAN_NO_CARRY = False
MOE_NO_EXPERTS = False
MOE_ONE_EXPERT = None
MOE_SERIAL = False
MOE_NUM = NEXP
MOE_NO_SHARED = False


def build_program(block_ids):
    nc = bass.Bass("TRN2", target_bir_lowering=False, dynamic_dma_scratch_size=2048)
    m = Module(nc)
    with m.ctx.es:
        m.setup()
        m.ctx.barrier()
        for b in block_ids:
            with ExitStack() as scope:
                BLOCKS[b](m, scope)
        m.finish()
    return nc, m.declared


def _tr(a):
    return np.ascontiguousarray(np.asarray(a, np.float32).T)


def _row(a):
    return np.ascontiguousarray(np.asarray(a, np.float32).reshape(1, -1))


def prep_weights(inputs):
    g = {}
    g["L0a_s_w1T"] = _tr(inputs["L0_attn_s_w1"])
    g["L0a_s_w2T"] = _tr(inputs["L0_attn_s_w2"])
    g["L0a_B_wT"] = _tr(inputs["L0_attn_B_w"])
    g["L0a_C_wT"] = _tr(inputs["L0_attn_C_w"])
    g["L0a_D_wT"] = _tr(inputs["L0_attn_D_w"])
    g["L0a_g_wT"] = _tr(inputs["L0_attn_g_w"])
    g["L0a_A"] = np.ascontiguousarray(np.asarray(inputs["L0_attn_A"], np.float32))
    for nm in ("s_b1", "s_b2", "B_b", "C_b", "D_b", "g_b"):
        g["L0a_" + nm] = _row(inputs["L0_attn_" + nm])
    g["L0f_WgT"] = _tr(inputs["L0_ffn_Wg"])
    for nm, src in (("We_gateT", "We_gate"), ("We_upT", "We_up"), ("We_downT", "We_down")):
        g["L0f_" + nm] = np.ascontiguousarray(
            np.swapaxes(np.asarray(inputs["L0_ffn_" + src], np.float32), 1, 2))
    g["L0f_Ws_gateT"] = _tr(inputs["L0_ffn_Ws_gate"])
    g["L0f_Ws_upT"] = _tr(inputs["L0_ffn_Ws_up"])
    g["L0f_Ws_downT"] = _tr(inputs["L0_ffn_Ws_down"])
    g["L0f_WsgT"] = np.ascontiguousarray(
        np.asarray(inputs["L0_ffn_Wsg"], np.float32).reshape(8, 128).T)
    g["L1a_W_chargeT"] = np.ascontiguousarray(
        np.asarray(inputs["L1_attn_W_charge"], np.float32).reshape(8, 128).T)
    g["L1a_W_shellT"] = _tr(inputs["L1_attn_W_shell"])
    g["L1a_omegaT"] = _tr(inputs["L1_attn_omega"])
    g["L1a_W_combineT"] = _tr(inputs["L1_attn_W_combine"])
    g["L1a_W_vT"] = _tr(inputs["L1_attn_W_v"])
    g["L1a_W_outT"] = _tr(inputs["L1_attn_W_out"])
    bvc = np.zeros((128, 9), np.float32)
    bvc[:, 0:8] = np.asarray(inputs["L1_attn_b_v"], np.float32).reshape(8, 128).T
    bvc[0:32, 8] = np.asarray(inputs["L1_attn_rff_bias"], np.float32).ravel()
    g["L1a_b_v_col"] = np.ascontiguousarray(bvc)
    g["L1a_b_charge"] = _row(inputs["L1_attn_b_charge"])
    g["L1a_b_shell"] = _row(inputs["L1_attn_b_shell"])
    g["L1a_rff_bias"] = _row(inputs["L1_attn_rff_bias"])
    g["L1a_b_out"] = _row(inputs["L1_attn_b_out"])
    g["L1a_log_temp"] = _row(inputs["L1_attn_log_temp"])
    g["L1f_weightT"] = _tr(inputs["L1_ffn_weight"])
    g["L1f_bias"] = _row(inputs["L1_ffn_bias"])
    g["L1f_alpha"] = _row(inputs["L1_ffn_alpha"])
    g["c_idf"] = np.eye(16, dtype=np.float32)
    g["c_idb"] = np.eye(128, dtype=ml_dtypes.bfloat16)
    g["c_trib"] = (np.arange(128)[:, None] <= np.arange(128)[None, :]).astype(ml_dtypes.bfloat16)
    t_idx = np.arange(T, dtype=np.float64)
    c = 0.3 / 10.0
    g["c_ramps"] = np.stack([np.exp(-c * t_idx), np.exp(c * t_idx)]).astype(np.float32)
    return g


def kernel(**inputs):
    x = np.asarray(inputs["L0_attn_x"], np.float32)
    g = prep_weights(inputs)
    xcur = [np.ascontiguousarray(x[b].T) for b in range(B)]
    for launch in LAUNCH_PLAN:
        nc, declared = build_program(launch)
        in_maps = []
        for b in range(B):
            mp = {"xin": xcur[b]}
            for name, (shape, dtype, kind) in declared.items():
                if kind == "ExternalInput" and name != "xin":
                    mp[name] = g[name]
            in_maps.append(mp)
        res = run_bass_kernel_spmd(nc, in_maps, core_ids=list(range(B)))
        xcur = [res.results[b]["xout"] for b in range(B)]
    out = np.stack([xc.T for xc in xcur], axis=0).astype(np.float32)
    return np.ascontiguousarray(out)


# version 17 (final)
# speedup vs baseline: 1.0835x; 1.0347x over previous
import math
import numpy as np
import ml_dtypes
from contextlib import ExitStack

import concourse.bass as bass
import concourse.mybir as mybir
from concourse.bass_utils import run_bass_kernel_spmd

F32 = mybir.dt.float32
BF16 = mybir.dt.bfloat16
I32 = mybir.dt.int32
AF = mybir.ActivationFunctionType
ALU = mybir.AluOpType
AX = mybir.AxisListType.X

B, T, D = 8, 2048, 1024
ND = D // 128
NT = T // 512
TW = 512
EPS = 1e-5

SC_C = 32
SC_NCH = T // SC_C
SC_PB = 512 // SC_C

L1_RFF = 32
RFF_SCALE = math.sqrt(2.0 / (2 * L1_RFF)) * math.sqrt(math.exp(-0.5))
MAGS = [math.sqrt(abs((-1.0) ** m / math.factorial(m)) + 1e-10) for m in range(5)]
NEXP, MOE_HID, SH_HID = 16, 512, 2048

ENG_NAMES = ("t", "v", "s", "g", "sync")


class Buf:
    __slots__ = ("name", "wr", "rd")

    def __init__(self, name):
        self.name = name
        self.wr = None
        self.rd = {}

    def add_rd(self, tok):
        s, v = tok
        if self.rd.get(s, 0) < v:
            self.rd[s] = v


class Ctx:
    def __init__(self, nc):
        self.nc = nc
        self.es = ExitStack()
        self.E = {"t": nc.tensor, "v": nc.vector, "s": nc.scalar, "g": nc.gpsimd,
                  "sync": nc.sync}
        self.sem = {}
        self.cnt = {}
        for s in ("t", "v", "s", "g", "qw", "qb", "qx", "qs", "qA"):
            self.sem[s] = self.es.enter_context(nc.semaphore(s))
            self.cnt[s] = 0
        self.water = {}
        self.pb = [self.es.enter_context(nc.psum_tensor(f"pb{i}", [128, 512], F32))
                   for i in range(8)]
        self.pbuf = [Buf(f"pb{i}") for i in range(8)]
        self.fresh = [False] * 8
        self.rr = 0
        self.bufs = []

    def sb(self, name, shape, dtype, scope=None):
        h = (scope or self.es).enter_context(self.nc.sbuf_tensor(name, shape, dtype))
        b = Buf(name)
        self.bufs.append(b)
        return h, b

    def wait(self, eng, tok):
        if tok is None:
            return
        s, v = tok
        key = (eng, s)
        if self.water.get(key, 0) >= v:
            return
        self.E[eng].wait_ge(self.sem[s], v)
        self.water[key] = v

    def _wait_rd(self, eng, b):
        self.wait(eng, b.wr)

    def _wait_wr(self, eng, b):
        self.wait(eng, b.wr)
        for s, v in b.rd.items():
            self.wait(eng, (s, v))

    def op(self, eng, emit, reads=(), writes=(), waitw=(), tick=True):
        for b in reads:
            self._wait_rd(eng, b)
        for b in writes:
            self._wait_wr(eng, b)
        for b in waitw:
            self._wait_wr(eng, b)
        instr = emit(self.E[eng])
        tok = None
        if tick:
            self.cnt[eng] += 1
            instr.then_inc(self.sem[eng], 1)
            tok = (eng, self.cnt[eng])
            for b in reads:
                b.add_rd(tok)
            for b in writes:
                b.wr = tok
                b.rd = {}
        return tok

    def dma(self, q, out_ap, in_ap, reads=(), writes=(), **kw):
        eng = {"qw": "sync", "qx": "sync", "qb": "g", "qs": "s", "qA": "sync"}[q]
        for b in reads:
            self._wait_rd(eng, b)
        for b in writes:
            self._wait_wr(eng, b)
        instr = self.E[eng].dma_start(out=out_ap, in_=in_ap, **kw)
        self.cnt[q] += 16
        instr.then_inc(self.sem[q], 16)
        tok = (q, self.cnt[q])
        for b in reads:
            b.add_rd(tok)
        for b in writes:
            b.wr = tok
            b.rd = {}
        return tok

    def bank(self):
        for _ in range(16):
            i = self.rr % 8
            self.rr += 1
            if i not in getattr(self, "reserved", ()):
                self.fresh[i] = True
                return i
        raise RuntimeError("no free psum bank")

    def barrier(self):
        toks = [(s, self.cnt[s]) for s in self.sem]
        for e in ENG_NAMES:
            for tok in toks:
                self.wait(e, tok)
        for b in self.pbuf:
            b.wr = None
            b.rd = {}
        for b in self.bufs:
            b.wr = None
            b.rd = {}


class Module:
    def __init__(self, nc):
        self.nc = nc
        self.ctx = Ctx(nc)
        self.declared = {}

    def dram(self, name, shape, dtype=F32, kind="ExternalInput"):
        ap = self.nc.dram_tensor(name, list(shape), dtype, kind=kind).ap()
        self.declared[name] = (shape, dtype, kind)
        return ap

    def mm(self, bi, out_ap, lhsT, rhs, reads, last=False):
        ctx = self.ctx
        pbuf = ctx.pbuf[bi]
        first = ctx.fresh[bi]
        ctx.fresh[bi] = False
        if not hasattr(ctx, "grp_reads"):
            ctx.grp_reads = [set() for _ in range(8)]
        ctx.grp_reads[bi].update(reads)
        group = list(ctx.grp_reads[bi]) if last else list(reads)
        if last:
            ctx.grp_reads[bi] = set()
        return ctx.op(
            "t",
            lambda E: E.matmul(out_ap, lhsT, rhs, start=first, stop=last,
                               skip_group_check=True),
            reads=group, writes=[pbuf] if last else (),
            waitw=[pbuf] if first else (), tick=last)

    def setup(self):
        ctx = self.ctx
        self.xin = self.dram("xin", (D, T))
        self.xout = self.dram("xout", (D, T), kind="ExternalOutput")
        self.idf_d = self.dram("c_idf", (16, 16))
        self.idb_d = self.dram("c_idb", (128, 128), BF16)
        self.trib_d = self.dram("c_trib", (128, 128), BF16)
        self.ramps_d = self.dram("c_ramps", (2, T))

        self.x, self.xB = ctx.sb("x", [128, ND * T], F32)
        self.bc1, self.bc1B = ctx.sb("bc1", [128, T], F32)
        self.bc2, self.bc2B = ctx.sb("bc2", [128, T], F32)
        self.tmpf, self.tmpfB = ctx.sb("tmpf", [128, T], F32)
        self.stg, self.stgB = ctx.sb("stg", [128, 2048], F32)
        self.stgHB = [self.stgB, Buf("stgh1")]
        ctx.bufs.append(self.stgHB[1])
        self.rows, self.rowsB = ctx.sb("rows", [1, 4096], F32)
        self.idf, self.idfB = ctx.sb("idf", [16, 16], F32)
        self.idb, self.idbB = ctx.sb("idb", [128, 128], BF16)
        self.trib, self.tribB = ctx.sb("trib", [128, 128], BF16)
        self.onesf, self.onesfB = ctx.sb("onesf", [1, TW], F32)
        self.onesrb, self.onesrbB = ctx.sb("onesrb", [1, 128], BF16)
        self.onescf, self.onescfB = ctx.sb("onescf", [128, 1], F32)
        self.onescb, self.onescbB = ctx.sb("onescb", [128, 1], BF16)
        self.zcolb, self.zcolbB = ctx.sb("zcolb", [128, 1], BF16)
        self.colp, self.colpB = ctx.sb("colp", [128, 64], F32)
        self.bnf = self.nc.dram_tensor("bounce_f", [24, T], F32, kind="Internal").ap()
        self.bnb = self.nc.dram_tensor("bounce_b", [24, T], BF16, kind="Internal").ap()
        self.bnfB, self.bnbB = Buf("bnf"), Buf("bnb")
        self.ctx.bufs += [self.bnfB, self.bnbB]
        self._slot = 0

        ctx.dma("qw", self.idf[:, :], self.idf_d, writes=[self.idfB])
        ctx.dma("qw", self.idb[:, :], self.idb_d, writes=[self.idbB])
        ctx.dma("qw", self.trib[:, :], self.trib_d, writes=[self.tribB])
        self.cdbt, self.cdbtB = ctx.sb("cdbt", [128, 16], F32)
        for i, v in enumerate((EPS, 2 * math.pi, -math.pi)):
            ctx.op("v", lambda E, i=i, v=v: E.memset(self.cdbt[:, i:i + 1], v),
                   writes=[self.cdbtB])
            self.nc.const_aps.register(F32, v, self.cdbt[:, i:i + 1])
        ctx.op("v", lambda E: E.memset(self.onesf[:, :], 1.0), writes=[self.onesfB])
        ctx.op("v", lambda E: E.memset(self.onesrb[:, :], 1.0), writes=[self.onesrbB])
        ctx.op("v", lambda E: E.memset(self.onescf[:, :], 1.0), writes=[self.onescfB])
        ctx.op("v", lambda E: E.memset(self.onescb[:, :], 1.0), writes=[self.onescbB])
        ctx.op("v", lambda E: E.memset(self.zcolb[:, :], 0.0), writes=[self.zcolbB])
        for dt in range(ND):
            ctx.dma(("qw", "qs", "qb")[dt % 3], self.x[:, dt * T:(dt + 1) * T],
                    self.xin[dt * 128:(dt + 1) * 128, :], writes=[self.xB])

    def finish(self):
        ctx = self.ctx
        qs_used = []
        for dt in range(ND):
            q = ("qx", "qs", "qb")[dt % 3]
            qs_used.append(q)
            ctx.dma(q, self.xout[dt * 128:(dt + 1) * 128, :],
                    self.x[:, dt * T:(dt + 1) * T], reads=[self.xB])
        for q in set(qs_used):
            ctx.wait("sync", (q, ctx.cnt[q]))

    def xs(self, dt, tt=None):
        if tt is None:
            return self.x[:, dt * T:(dt + 1) * T]
        return self.x[:, dt * T + tt * TW: dt * T + (tt + 1) * TW]

    def bcast(self, row_ap, row_buf, dst_ap, dst_buf, nrows=128, width=T, bf16=False):
        ctx = self.ctx
        ones = self.onesrb if bf16 else self.onesf
        onesB = self.onesrbB if bf16 else self.onesfB
        for c0 in range(0, width, 512):
            cw = min(512, width - c0)
            bj = ctx.bank()
            self.mm(bj, ctx.pb[bj][0:nrows, 0:cw], ones[:, 0:nrows], row_ap[:, c0:c0 + cw],
                    [onesB, row_buf], last=True)
            ctx.op("s", lambda E, bj=bj, c0=c0, cw=cw, nrows=nrows: E.copy(
                out=dst_ap[0:nrows, c0:c0 + cw], in_=ctx.pb[bj][0:nrows, 0:cw]),
                reads=[ctx.pbuf[bj]], writes=[dst_buf])

    def load_w(self, dram_ap, nk, dout, dst_h, dst_b, dst_off):
        ctx = self.ctx
        view = dram_ap.rearrange("(k p) d -> p k d", p=128)
        kchunk = max(1, 1024 // dout)
        k0 = 0
        ci = 0
        while k0 < nk:
            kn = min(kchunk, nk - k0)
            cw = kn * dout
            done = k0 * dout
            hs = ci % 2
            sl = self.stg[:, hs * 1024: hs * 1024 + cw]
            ctx.dma("qw" if hs == 0 else "qs", sl, view[:, k0:k0 + kn, :],
                    writes=[self.stgHB[hs]], allow_slow_non_contiguous=(dout < 16))
            ctx.op("s", lambda E, cw=cw, done=done, sl=sl, hs=hs: E.copy(
                out=dst_h[:, dst_off + done:dst_off + done + cw], in_=sl),
                reads=[self.stgHB[hs]], writes=[dst_b])
            k0 += kn
            ci += 1

    def emit_ln(self, xn, xnB):
        ctx = self.ctx
        sqv = self.bc1.ap().bitcast(BF16)
        sqvB = self.bc1B

        def xns(dt, tt=None):
            if tt is None:
                return xn[:, dt * T:(dt + 1) * T]
            return xn[:, dt * T + tt * TW: dt * T + (tt + 1) * TW]

        for dt in range(ND):
            ctx.op("s", lambda E, dt=dt: E.copy(out=xns(dt), in_=self.xs(dt)),
                   reads=[self.xB], writes=[xnB])
        for tt in range(NT):
            bs, bq = ctx.bank(), ctx.bank()
            for dt in range(ND):
                ctx.op("s", lambda E, dt=dt, tt=tt: E.activation(
                    out=sqv[:, dt * 512:(dt + 1) * 512], in_=self.xs(dt, tt), func=AF.Square),
                    reads=[self.xB], writes=[sqvB])
            for dt in range(ND):
                self.mm(bs, ctx.pb[bs][0:1, 0:512], self.onescb[:, :], xns(dt, tt),
                        [self.onescbB, xnB], last=(dt == ND - 1))
            for dt in range(ND):
                self.mm(bq, ctx.pb[bq][0:1, 0:512], self.onescb[:, :],
                        sqv[:, dt * 512:(dt + 1) * 512], [self.onescbB, sqvB],
                        last=(dt == ND - 1))
            ctx.op("v", lambda E, tt=tt, bs=bs: E.tensor_copy(
                out=self.rows[:, tt * TW:(tt + 1) * TW], in_=ctx.pb[bs][0:1, 0:512]),
                reads=[ctx.pbuf[bs]], writes=[self.rowsB])
            ctx.op("v", lambda E, tt=tt, bq=bq: E.tensor_copy(
                out=self.rows[:, 2048 + tt * TW:2048 + (tt + 1) * TW],
                in_=ctx.pb[bq][0:1, 0:512]), reads=[ctx.pbuf[bq]], writes=[self.rowsB])
        R0, R1 = 0, 2048
        ctx.op("v", lambda E: E.tensor_scalar(
            out=self.rows[:, R0:R0 + T], in0=self.rows[:, R0:R0 + T], scalar1=1.0 / D,
            scalar2=None, op0=ALU.mult), reads=[self.rowsB], writes=[self.rowsB])
        ctx.op("v", lambda E: E.tensor_scalar(
            out=self.rows[:, R1:R1 + T], in0=self.rows[:, R1:R1 + T], scalar1=1.0 / D,
            scalar2=None, op0=ALU.mult), reads=[self.rowsB], writes=[self.rowsB])
        ctx.op("v", lambda E: E.tensor_tensor(
            out=self.tmpf[0:1, 0:T], in0=self.rows[:, R0:R0 + T], in1=self.rows[:, R0:R0 + T],
            op=ALU.mult), reads=[self.rowsB], writes=[self.tmpfB])
        ctx.op("v", lambda E: E.tensor_tensor(
            out=self.rows[:, R1:R1 + T], in0=self.rows[:, R1:R1 + T], in1=self.tmpf[0:1, 0:T],
            op=ALU.subtract), reads=[self.rowsB, self.tmpfB], writes=[self.rowsB])
        ctx.op("s", lambda E: E.activation(
            out=self.rows[:, R1:R1 + T], in_=self.rows[:, R1:R1 + T], func=AF.Sqrt,
            bias=EPS, scale=1.0), reads=[self.rowsB], writes=[self.rowsB])
        ctx.op("v", lambda E: E.reciprocal(
            out=self.rows[:, R1:R1 + T], in_=self.rows[:, R1:R1 + T]),
            reads=[self.rowsB], writes=[self.rowsB])
        ctx.op("v", lambda E: E.tensor_tensor(
            out=self.rows[:, R0:R0 + T], in0=self.rows[:, R0:R0 + T],
            in1=self.rows[:, R1:R1 + T], op=ALU.mult), reads=[self.rowsB], writes=[self.rowsB])
        self.bcast(self.rows[:, R1:R1 + T], self.rowsB, self.bc1[:, :], self.bc1B)
        self.bcast(self.rows[:, R0:R0 + T], self.rowsB, self.bc2[:, :], self.bc2B)
        for dt in range(ND):
            ctx.op("v", lambda E, dt=dt: E.tensor_tensor(
                out=self.tmpf[:, :], in0=self.xs(dt), in1=self.bc1[:, :], op=ALU.mult),
                reads=[self.xB, self.bc1B], writes=[self.tmpfB])
            ctx.op("v", lambda E, dt=dt: E.tensor_tensor(
                out=xns(dt), in0=self.tmpf[:, :], in1=self.bc2[:, :], op=ALU.subtract),
                reads=[self.tmpfB, self.bc2B], writes=[xnB])

    def emit_l0_attn(self, scope):
        ctx = self.ctx
        w_s1 = self.dram("L0a_s_w1T", (D, 512))
        w_s2 = self.dram("L0a_s_w2T", (512, 128))
        w_B = self.dram("L0a_B_wT", (D, 128))
        w_C = self.dram("L0a_C_wT", (128, D))
        w_D = self.dram("L0a_D_wT", (D, D))
        w_g = self.dram("L0a_g_wT", (D, D))
        a_d = self.dram("L0a_A", (128, 128))
        biases = {}
        for nm, wd in (("s_b1", 512), ("s_b2", 128), ("B_b", 128), ("C_b", D),
                       ("D_b", D), ("g_b", D)):
            biases[nm] = self.dram("L0a_" + nm, (1, wd))

        xn, xnB = ctx.sb("xn0", [128, ND * T], BF16, scope)
        wl, wlB = ctx.sb("wl0a", [128, 17408], BF16, scope)
        O_S1, O_S2, O_B, O_C = 0, 4096, 4608, 16384
        O_D, O_G = 0, 8192
        s1t, s1tB = ctx.sb("s1t", [128, 2048], BF16, scope)
        sgb, sgbB = ctx.sb("sgb", [128, T], BF16, scope)
        ub, ubB = ctx.sb("ub", [128, T], BF16, scope)
        hsb, hsbB = ctx.sb("hsb", [128, T], BF16, scope)
        qstk, qstkB = ctx.sb("qstk", [128, 39 * 128], BF16, scope)
        esb, esbB = ctx.sb("esb", [128, SC_NCH], BF16, scope)
        abm, abmB = ctx.sb("abm", [128, 128], BF16, scope)
        rtm, rtmB = ctx.sb("rtm", [128, 128], BF16, scope)
        gt, gtB = ctx.sb("gt", [128, TW], BF16, scope)
        yt, ytB = ctx.sb("yt", [128, TW], BF16, scope)
        rowb, rowbB = ctx.sb("rowb0", [1, 2816], F32, scope)
        RO = {"s_b1": 0, "s_b2": 512, "B_b": 640, "C_b": 768, "g_b": 1792}

        self.load_w(w_s1, 8, 512, wl, wlB, O_S1)
        self.load_w(w_s2, 4, 128, wl, wlB, O_S2)
        self.load_w(w_B, 8, 128, wl, wlB, O_B)
        self.load_w(w_C, 1, D, wl, wlB, O_C)
        for nm in RO:
            w = {"s_b1": 512, "s_b2": 128, "B_b": 128, "C_b": D, "g_b": D}[nm]
            ctx.dma("qw", rowb[:, RO[nm]:RO[nm] + w], biases[nm], writes=[rowbB])
        ctx.dma("qw", self.tmpf[0:1, 0:D], biases["D_b"], writes=[self.tmpfB])
        ctx.op("v", lambda E: E.tensor_tensor(
            out=rowb[:, RO["C_b"]:RO["C_b"] + D], in0=rowb[:, RO["C_b"]:RO["C_b"] + D],
            in1=self.tmpf[0:1, 0:D], op=ALU.add), reads=[rowbB, self.tmpfB], writes=[rowbB])
        ctx.dma("qw", self.stg[:, 0:128], a_d, writes=[self.stgB])
        ctx.op("g", lambda E: E.tensor_copy(out=abm[:, :], in_=self.stg[:, 0:128]),
               reads=[self.stgB], writes=[abmB])

        self.emit_ln(xn, xnB)

        def xns(dt, tt):
            return xn[:, dt * T + tt * TW: dt * T + (tt + 1) * TW]

        for tt in range(NT):
            for ht in range(4):
                bi = ctx.bank()
                for kt in range(8):
                    self.mm(bi, ctx.pb[bi][:, 0:TW],
                            wl[:, O_S1 + kt * 512 + ht * 128: O_S1 + kt * 512 + ht * 128 + 128],
                            xns(kt, tt), [wlB, xnB])
                self.mm(bi, ctx.pb[bi][:, 0:TW], rowb[:, RO["s_b1"] + ht * 128:RO["s_b1"] + ht * 128 + 128],
                        self.onesf[:, :], [rowbB, self.onesfB], last=True)
                ctx.op("s", lambda E, ht=ht, bi=bi: E.activation(
                    out=s1t[:, ht * 512:(ht + 1) * 512], in_=ctx.pb[bi][:, 0:TW], func=AF.Silu),
                    reads=[ctx.pbuf[bi]], writes=[s1tB])
            bi = ctx.bank()
            for kt in range(4):
                self.mm(bi, ctx.pb[bi][:, 0:TW],
                        wl[:, O_S2 + kt * 128: O_S2 + (kt + 1) * 128],
                        s1t[:, kt * 512:(kt + 1) * 512], [wlB, s1tB])
            self.mm(bi, ctx.pb[bi][:, 0:TW], rowb[:, RO["s_b2"]:RO["s_b2"] + 128],
                    self.onesf[:, :], [rowbB, self.onesfB], last=True)
            ctx.op("s", lambda E, tt=tt, bi=bi: E.activation(
                out=sgb[:, tt * TW:(tt + 1) * TW], in_=ctx.pb[bi][:, 0:TW], func=AF.Sigmoid),
                reads=[ctx.pbuf[bi]], writes=[sgbB])
            bi = ctx.bank()
            for kt in range(8):
                self.mm(bi, ctx.pb[bi][:, 0:TW],
                        wl[:, O_B + kt * 128: O_B + (kt + 1) * 128], xns(kt, tt), [wlB, xnB])
            self.mm(bi, ctx.pb[bi][:, 0:TW], rowb[:, RO["B_b"]:RO["B_b"] + 128],
                    self.onesf[:, :], [rowbB, self.onesfB], last=True)
            ctx.op("v", lambda E, tt=tt, bi=bi: E.tensor_tensor(
                out=ub[:, tt * TW:(tt + 1) * TW], in0=ctx.pb[bi][:, 0:TW],
                in1=sgb[:, tt * TW:(tt + 1) * TW], op=ALU.mult),
                reads=[ctx.pbuf[bi], sgbB], writes=[ubB])

        self.load_w(w_D, 8, D, wl, wlB, O_D)
        self.load_w(w_g, 8, D, wl, wlB, O_G)

        self._emit_scan(abm, abmB, rtm, rtmB, qstk, qstkB, esb, esbB, ub, ubB, hsb, hsbB)
        if DBG:
            dq = self.dram("dbg_q", (128, 1024), kind="ExternalOutput")
            self.ctx.op("v", lambda E: E.tensor_copy(out=self.tmpf[:, 0:1024], in_=qstk[:, 0:1024]),
                        reads=[qstkB], writes=[self.tmpfB])
            self.ctx.dma("qx", dq, self.tmpf[:, 0:1024], reads=[self.tmpfB])
            de = self.dram("dbg_e", (128, 64), kind="ExternalOutput")
            self.ctx.op("v", lambda E: E.tensor_copy(out=self.tmpf[:, 1024:1088], in_=esb[:, :]),
                        reads=[esbB], writes=[self.tmpfB])
            self.ctx.dma("qx", de, self.tmpf[:, 1024:1088], reads=[self.tmpfB])
            for nm, th, tb in (("dbg_u", ub, ubB), ("dbg_hs", hsb, hsbB), ("dbg_s", sgb, sgbB)):
                dap = self.dram(nm, (128, T), kind="ExternalOutput")
                self.ctx.op("v", lambda E, th=th: E.tensor_copy(out=self.tmpf[:, :], in_=th[:, :]),
                            reads=[tb], writes=[self.tmpfB])
                self.ctx.dma("qx", dap, self.tmpf[:, :], reads=[self.tmpfB])

        for dt in range(ND):
            for tt in range(NT):
                bi = ctx.bank()
                self.mm(bi, ctx.pb[bi][:, 0:TW], wl[:, O_C + dt * 128: O_C + dt * 128 + 128],
                        hsb[:, tt * TW:(tt + 1) * TW], [wlB, hsbB])
                for kt in range(8):
                    self.mm(bi, ctx.pb[bi][:, 0:TW],
                            wl[:, O_D + kt * D + dt * 128: O_D + kt * D + dt * 128 + 128],
                            xns(kt, tt), [wlB, xnB])
                self.mm(bi, ctx.pb[bi][:, 0:TW], rowb[:, RO["C_b"] + dt * 128:RO["C_b"] + dt * 128 + 128],
                        self.onesf[:, :], [rowbB, self.onesfB], last=True)
                bg = ctx.bank()
                for kt in range(8):
                    self.mm(bg, ctx.pb[bg][:, 0:TW],
                            wl[:, O_G + kt * D + dt * 128: O_G + kt * D + dt * 128 + 128],
                            xns(kt, tt), [wlB, xnB])
                self.mm(bg, ctx.pb[bg][:, 0:TW], rowb[:, RO["g_b"] + dt * 128:RO["g_b"] + dt * 128 + 128],
                        self.onesf[:, :], [rowbB, self.onesfB], last=True)
                ctx.op("s", lambda E, bg=bg: E.activation(
                    out=gt[:, :], in_=ctx.pb[bg][:, 0:TW], func=AF.Sigmoid),
                    reads=[ctx.pbuf[bg]], writes=[gtB])
                ctx.op("v", lambda E, bi=bi: E.tensor_tensor(
                    out=yt[:, :], in0=ctx.pb[bi][:, 0:TW], in1=gt[:, :], op=ALU.mult),
                    reads=[ctx.pbuf[bi], gtB], writes=[ytB])
                ctx.op("v", lambda E, dt=dt, tt=tt: E.tensor_tensor(
                    out=self.xs(dt, tt), in0=self.xs(dt, tt), in1=yt[:, :], op=ALU.add),
                    reads=[ytB], writes=[self.xB])
        ctx.barrier()

    def _emit_scan(self, abm, abmB, rtm, rtmB, qstk, qstkB, esb, esbB, ub, ubB, hsb, hsbB):
        ctx = self.ctx

        def q(i):
            return qstk[:, i * 128:(i + 1) * 128]

        def pv(bi):
            return ctx.pb[bi].ap().bitcast(BF16)

        ctx.op("v", lambda E: E.tensor_copy(out=q(0), in_=self.idb[:, :]),
               reads=[self.idbB], writes=[qstkB])
        bi = ctx.bank()
        ctx.op("t", lambda E, bi=bi: E.transpose(pv(bi)[:, 0:128], abm[:, :], self.idb[:, :]),
               reads=[abmB, self.idbB], waitw=[ctx.pbuf[bi]], writes=[ctx.pbuf[bi]])
        ctx.op("v", lambda E, bi=bi: E.tensor_copy(out=q(1), in_=pv(bi)[:, 0:128]),
               reads=[ctx.pbuf[bi]], writes=[qstkB])

        def prod_block(src_i, rhs_lo, rhs_n, dst_i):
            bj = ctx.bank()
            ctx.op("t", lambda E, bj=bj: E.transpose(pv(bj)[:, 0:128], q(src_i), self.idb[:, :]),
                   reads=[qstkB, self.idbB], waitw=[ctx.pbuf[bj]], writes=[ctx.pbuf[bj]])
            ctx.op("v", lambda E, bj=bj: E.tensor_copy(out=rtm[:, :], in_=pv(bj)[:, 0:128]),
                   reads=[ctx.pbuf[bj]], writes=[rtmB])
            done = 0
            while done < rhs_n:
                cnt = min(4, rhs_n - done)
                bk = ctx.bank()
                self.mm(bk, ctx.pb[bk][:, 0:cnt * 128], rtm[:, :],
                        qstk[:, (rhs_lo + done) * 128:(rhs_lo + done + cnt) * 128],
                        [rtmB, qstkB], last=True)
                ctx.op("v", lambda E, bk=bk, done=done, cnt=cnt: E.tensor_copy(
                    out=qstk[:, (dst_i + done) * 128:(dst_i + done + cnt) * 128],
                    in_=ctx.pb[bk][:, 0:cnt * 128]), reads=[ctx.pbuf[bk]], writes=[qstkB])
                done += cnt

        for s in (1, 2, 4, 8, 16):
            prod_block(s, 1, min(s, SC_C - s), s + 1)
        prev = 32
        for k in range(1, 7):
            prod_block(prev, prev, 1, 32 + k)
            prev = 32 + k

        sb_banks = [ctx.bank() for _ in range(4)]
        ctx.reserved = set(sb_banks)
        for g, bi in enumerate(sb_banks):
            pb = ctx.pb[bi]
            for m in range(SC_C):
                w = SC_C - m
                out_ap = bass.AP(pb[:, :].tensor, m, [[512, 128], [SC_C, SC_PB], [1, w]])
                in_ap = bass.AP(ub[:, :].tensor, g * 512, [[T, 128], [SC_C, SC_PB], [1, w]])
                self.mm(bi, out_ap, q(m), in_ap, [qstkB, ubB], last=(m == SC_C - 1))
            e_ap = bass.AP(pb[:, :].tensor, SC_C - 1, [[512, 128], [SC_C, SC_PB]])
            ctx.op("v", lambda E, e_ap=e_ap, g=g: E.tensor_copy(
                out=esb[:, g * SC_PB:(g + 1) * SC_PB], in_=e_ap),
                reads=[ctx.pbuf[bi]], writes=[esbB])

        k = 0
        s = 1
        while s < SC_NCH:
            gi = 32 if k == 0 else 32 + k
            bj = ctx.bank()
            ctx.op("t", lambda E, bj=bj, gi=gi: E.transpose(
                pv(bj)[:, 0:128], q(gi), self.idb[:, :]),
                reads=[qstkB, self.idbB], waitw=[ctx.pbuf[bj]], writes=[ctx.pbuf[bj]])
            ctx.op("v", lambda E, bj=bj: E.tensor_copy(out=rtm[:, :], in_=pv(bj)[:, 0:128]),
                   reads=[ctx.pbuf[bj]], writes=[rtmB])
            bk = ctx.bank()
            self.mm(bk, ctx.pb[bk][:, 0:SC_NCH], self.idb[:, :], esb[:, 0:SC_NCH],
                    [self.idbB, esbB])
            self.mm(bk, ctx.pb[bk][:, s:SC_NCH], rtm[:, :], esb[:, 0:SC_NCH - s],
                    [rtmB, esbB], last=True)
            ctx.op("v", lambda E, bk=bk: E.tensor_copy(out=esb[:, :], in_=ctx.pb[bk][:, 0:SC_NCH]),
                   reads=[ctx.pbuf[bk]], writes=[esbB])
            s *= 2
            k += 1

        for g, bi in enumerate(sb_banks if not SCAN_NO_CARRY else []):
            pb = ctx.pb[bi]
            pbuf = ctx.pbuf[bi]
            firstmm = True
            for j in range(SC_C):
                if g == 0:
                    out_ap = bass.AP(pb[:, :].tensor, SC_C + j, [[512, 128], [SC_C, SC_PB - 1]])
                    rhs = esb[:, 0:SC_PB - 1]
                else:
                    out_ap = bass.AP(pb[:, :].tensor, j, [[512, 128], [SC_C, SC_PB]])
                    rhs = esb[:, g * SC_PB - 1: (g + 1) * SC_PB - 1]
                tok = ctx.op(
                    "t", lambda E, out_ap=out_ap, rhs=rhs, j=j: E.matmul(
                        out_ap, q(j + 1), rhs, start=False, stop=(j == SC_C - 1),
                        skip_group_check=True),
                    reads=[qstkB, esbB], writes=[pbuf] if j == SC_C - 1 else (),
                    waitw=[pbuf] if firstmm else (), tick=(j == SC_C - 1))
                firstmm = False
        for g, bi in enumerate(sb_banks):
            ctx.op("v", lambda E, g=g, bi=bi: E.tensor_copy(
                out=hsb[:, g * 512:(g + 1) * 512], in_=ctx.pb[bi][:, 0:512]),
                reads=[ctx.pbuf[bi]], writes=[hsbB])
        ctx.reserved = set()

    def emit_l0_ffn(self, scope):
        nc, ctx = self.nc, self.ctx
        w_rt = self.dram("L0f_WgT", (D, 16))
        w_eg = self.dram("L0f_We_gateT", (NEXP, D, MOE_HID))
        w_eu = self.dram("L0f_We_upT", (NEXP, D, MOE_HID))
        w_ed = self.dram("L0f_We_downT", (NEXP, MOE_HID, D))
        w_sg = self.dram("L0f_Ws_gateT", (D, SH_HID))
        w_su = self.dram("L0f_Ws_upT", (D, SH_HID))
        w_sd = self.dram("L0f_Ws_downT", (SH_HID, D))
        w_x = self.dram("L0f_WsgT", (128, 8))
        mask_d = nc.dram_tensor("moe_mask", [NEXP, T], BF16, kind="Internal").ap()
        mask_dB = Buf("mask_d")
        ctx.bufs.append(mask_dB)

        xn, xnB = ctx.sb("xn1", [128, ND * T], BF16, scope)
        wrtf, wrtfB = ctx.sb("wrtf", [128, 128], F32, scope)
        wxf, wxfB = ctx.sb("wxf", [128, 8], BF16, scope)
        wex, wexB = ctx.sb("wex", [128, 24576], BF16, scope)
        wexHB = [Buf("wexh0"), Buf("wexh1")]
        ctx.bufs += wexHB
        hb, hbB = ctx.sb("hb", [128, 8192], BF16, scope)
        hbTB = [Buf(f"hbt{i}") for i in range(NT)]
        ctx.bufs += hbTB
        ltok, ltokB = ctx.sb("ltok", [128, 16], F32, scope)
        mtok, mtokB = ctx.sb("mtok", [128, 16], BF16, scope)
        mfmx, mfmB = ctx.sb("mfm", [128, T], BF16, scope)
        mfm = mfmx[0:16, 0:T]
        maskb0, maskbB0 = ctx.sb("maskb", [128, T], BF16, scope)
        MASKB = [(maskb0, maskbB0), (mfmx, mfmB)]
        sgt, sgtB = ctx.sb("sgt", [128, TW], BF16, scope)
        hut, hutB = ctx.sb("hut", [128, TW], BF16, scope)
        lgf = self.tmpf[0:16, 0:T]
        lgfB = self.tmpfB

        ctx.dma("qw", wrtf[:, :], w_rt.rearrange("(k p) d -> p k d", p=128)[:, :, :],
                writes=[wrtfB])
        ctx.dma("qw", self.stg[:, 0:8], w_x, writes=[self.stgB])
        ctx.op("g", lambda E: E.tensor_copy(out=wxf[:, :], in_=self.stg[:, 0:8]),
               reads=[self.stgB], writes=[wxfB])

        self.emit_ln(xn, xnB)

        def xns(dt, tt):
            return xn[:, dt * T + tt * TW: dt * T + (tt + 1) * TW]

        bi = ctx.bank()
        for kt in range(8):
            self.mm(bi, ctx.pb[bi][0:16, 0:1], wrtf[:, kt * 16:(kt + 1) * 16], self.onescf[:, :],
                    [self.onescfB, wrtfB], last=(kt == 7))
        ctx.op("v", lambda E, bi=bi: E.tensor_scalar(
            out=self.colp[0:16, 0:1], in0=ctx.pb[bi][0:16, 0:1], scalar1=-1.0, scalar2=None,
            op0=ALU.mult), reads=[ctx.pbuf[bi]], writes=[self.colpB])

        for tt in range(NT):
            bi = ctx.bank()
            for kt in range(8):
                self.mm(bi, ctx.pb[bi][0:16, 0:TW], wrtf[:, kt * 16:(kt + 1) * 16],
                        self.xs(kt, tt), [wrtfB, self.xB], last=(kt == 7))
            ctx.op("v", lambda E, tt=tt, bi=bi: E.tensor_tensor(
                out=lgf[:, tt * TW:(tt + 1) * TW], in0=ctx.pb[bi][0:16, 0:TW],
                in1=self.bc1[0:16, tt * TW:(tt + 1) * TW], op=ALU.mult),
                reads=[ctx.pbuf[bi], self.bc1B], writes=[lgfB])
            ctx.op("v", lambda E, tt=tt: E.scalar_tensor_tensor(
                out=lgf[:, tt * TW:(tt + 1) * TW], in0=self.bc2[0:16, tt * TW:(tt + 1) * TW],
                scalar=self.colp[0:16, 0:1], in1=lgf[:, tt * TW:(tt + 1) * TW],
                op0=ALU.mult, op1=ALU.add),
                reads=[self.bc2B, self.colpB, lgfB], writes=[lgfB])

        for i in range(16):
            bi = ctx.bank()
            pb, pbuf = ctx.pb[bi], ctx.pbuf[bi]
            ctx.op("t", lambda E, i=i, pb=pb: E.transpose(
                pb[:, 0:16], lgf[:, i * 128:(i + 1) * 128], self.idf[0:16, 0:16]),
                reads=[lgfB, self.idfB], waitw=[pbuf], writes=[pbuf])
            ctx.op("v", lambda E, pb=pb: E.tensor_copy(
                out=ltok[:, 0:16], in_=pb[:, 0:16]),
                reads=[pbuf], writes=[ltokB])
            L = ltok[:, 0:16]
            ctx.op("v", lambda E, L=L: E.reduce_max(out=self.colp[:, 1:2], in_=L, axis=AX),
                   reads=[ltokB], writes=[self.colpB])
            ctx.op("v", lambda E, L=L: E.tensor_scalar(
                out=self.colp[:, 8:24], in0=L, scalar1=self.colp[:, 1:2], scalar2=None,
                op0=ALU.is_equal), reads=[ltokB, self.colpB], writes=[self.colpB])
            ctx.op("v", lambda E, L=L: E.scalar_tensor_tensor(
                out=self.colp[:, 24:40], in0=self.colp[:, 8:24], scalar=-1e30, in1=L,
                op0=ALU.mult, op1=ALU.add), reads=[self.colpB, ltokB], writes=[self.colpB])
            ctx.op("v", lambda E: E.reduce_max(out=self.colp[:, 2:3], in_=self.colp[:, 24:40],
                                               axis=AX), reads=[self.colpB], writes=[self.colpB])
            ctx.op("v", lambda E: E.tensor_scalar(
                out=self.colp[:, 40:56], in0=self.colp[:, 24:40], scalar1=self.colp[:, 2:3],
                scalar2=None, op0=ALU.is_equal), reads=[self.colpB], writes=[self.colpB])
            ctx.op("v", lambda E: E.tensor_tensor(
                out=self.colp[:, 5:6], in0=self.colp[:, 1:2], in1=self.colp[:, 2:3],
                op=ALU.subtract), reads=[self.colpB], writes=[self.colpB])
            ctx.op("s", lambda E: E.activation(out=self.colp[:, 3:4], in_=self.colp[:, 5:6],
                                               func=AF.Sigmoid),
                   reads=[self.colpB], writes=[self.colpB])
            ctx.op("v", lambda E: E.tensor_scalar(
                out=self.colp[:, 4:5], in0=self.colp[:, 3:4], scalar1=-1.0, scalar2=1.0,
                op0=ALU.mult, op1=ALU.add), reads=[self.colpB], writes=[self.colpB])
            ctx.op("v", lambda E: E.tensor_scalar(
                out=self.colp[:, 8:24], in0=self.colp[:, 8:24], scalar1=self.colp[:, 3:4],
                scalar2=None, op0=ALU.mult), reads=[self.colpB], writes=[self.colpB])
            ctx.op("v", lambda E: E.scalar_tensor_tensor(
                out=mtok[:, 0:16], in0=self.colp[:, 40:56], scalar=self.colp[:, 4:5],
                in1=self.colp[:, 8:24], op0=ALU.mult, op1=ALU.add),
                reads=[self.colpB], writes=[mtokB])
            bj = ctx.bank()
            pbj, pbjB = ctx.pb[bj], ctx.pbuf[bj]
            ctx.op("t", lambda E, pbj=pbj: E.transpose(
                pbj.ap().bitcast(BF16)[0:16, 0:128], mtok[:, 0:16], self.idb[:, :]),
                reads=[mtokB, self.idbB], waitw=[pbjB], writes=[pbjB])
            ctx.op("v", lambda E, i=i, pbj=pbj: E.tensor_copy(
                out=mfm[:, i * 128:(i + 1) * 128], in_=pbj.ap().bitcast(BF16)[0:16, 0:128]),
                reads=[pbjB], writes=[mfmB])
        ctx.dma("qb", mask_d, mfm[:, :], reads=[mfmB], writes=[mask_dB])
        if DBG:
            dm = self.dram("dbg_mask", (16, T), kind="ExternalOutput")
            self.ctx.op("v", lambda E: E.tensor_copy(out=self.tmpf[32:48, 0:T], in_=mfm[:, :]),
                        reads=[mfmB], writes=[self.tmpfB])
            self.ctx.dma("qx", dm, self.tmpf[32:48, 0:T], reads=[self.tmpfB])
            dl = self.dram("dbg_lg", (16, T), kind="ExternalOutput")
            self.ctx.dma("qx", dl, lgf, reads=[lgfB])

        for tt in range(NT):
            bi = ctx.bank()
            for kt in range(8):
                self.mm(bi, ctx.pb[bi][0:1, 0:TW], wxf[:, kt:kt + 1], xns(kt, tt),
                        [wxfB, xnB], last=(kt == 7))
            ctx.op("s", lambda E, tt=tt, bi=bi: E.activation(
                out=self.rows[:, tt * TW:(tt + 1) * TW], in_=ctx.pb[bi][0:1, 0:TW],
                func=AF.Sigmoid), reads=[ctx.pbuf[bi]], writes=[self.rowsB])

        def load_expert(e, half):
            off = half * 12288
            hbuf = wexHB[half]
            ci = 0
            slots = [(self.stg[:, 0:1024], self.stgHB[0], "qw"),
                     (self.stg[:, 1024:2048], self.stgHB[1], "qs"),
                     (self.tmpf[:, 0:1024], self.tmpfB, "qA")]
            for j, wsrc in enumerate((w_eg, w_eu, w_ed)):
                view = wsrc[e].rearrange("(k p) d -> p k d", p=128)
                kn = 2 if j < 2 else 1
                for c in range(4):
                    sl, slb, q = slots[ci % 3]
                    ctx.dma(q, sl, view[:, c * kn:(c + 1) * kn, :], writes=[slb])
                    ctx.op("s", lambda E, j=j, c=c, off=off, sl=sl, slb=slb: E.copy(
                        out=wex[:, off + j * 4096 + c * 1024: off + j * 4096 + (c + 1) * 1024],
                        in_=sl), reads=[slb], writes=[hbuf])
                    ci += 1

        def mrow_ap(e):
            return self.rows.ap().bitcast(BF16)[0:1, 4096 + (e % 2) * 2048: 4096 + (e % 2) * 2048 + T]

        mrowB = [Buf("mrow0"), Buf("mrow1")]
        ctx.bufs += mrowB
        _elist = ([] if MOE_NO_EXPERTS else ([MOE_ONE_EXPERT] if MOE_ONE_EXPERT is not None else list(range(MOE_NUM))))
        if _elist:
            ctx.dma("qb", mrow_ap(_elist[0]), mask_d[_elist[0]:_elist[0] + 1, :],
                    reads=[mask_dB], writes=[mrowB[_elist[0] % 2]])
        if _elist and not MOE_SERIAL:
            load_expert(_elist[0], _elist[0] % 2)
        for e in _elist:
            half = e % 2
            off = half * 12288
            hbuf = wexHB[half]
            maskb, maskbB = MASKB[half]
            if e + 1 < NEXP and len(_elist) > 1:
                ctx.dma("qb", mrow_ap(e + 1), mask_d[e + 1:e + 2, :], reads=[mask_dB],
                        writes=[mrowB[(e + 1) % 2]])
            self.bcast(mrow_ap(e), mrowB[e % 2], maskb[:, :], maskbB, bf16=True)
            if MOE_SERIAL:
                load_expert(e, half)
            elif e + 1 < NEXP and len(_elist) > 1:
                load_expert(e + 1, 1 - half)
            for tt in range(NT):
                for ht in range(4):
                    bg_, bu_ = ctx.bank(), ctx.bank()
                    for kt in range(8):
                        self.mm(bg_, ctx.pb[bg_][:, 0:TW],
                                wex[:, off + kt * MOE_HID + ht * 128: off + kt * MOE_HID + ht * 128 + 128],
                                xns(kt, tt), [hbuf, xnB], last=(kt == 7))
                    for kt in range(8):
                        self.mm(bu_, ctx.pb[bu_][:, 0:TW],
                                wex[:, off + 4096 + kt * MOE_HID + ht * 128: off + 4096 + kt * MOE_HID + ht * 128 + 128],
                                xns(kt, tt), [hbuf, xnB], last=(kt == 7))
                    ctx.op("s", lambda E, bg_=bg_: E.activation(
                        out=sgt[:, :], in_=ctx.pb[bg_][:, 0:TW], func=AF.Silu),
                        reads=[ctx.pbuf[bg_]], writes=[sgtB])
                    ctx.op("v", lambda E, tt=tt: E.tensor_tensor(
                        out=hut[:, :], in0=sgt[:, :],
                        in1=maskb[:, tt * TW:(tt + 1) * TW], op=ALU.mult),
                        reads=[sgtB, maskbB], writes=[hutB])
                    ctx.op("v", lambda E, tt=tt, ht=ht, bu_=bu_: E.tensor_tensor(
                        out=hb[:, ht * T + tt * TW: ht * T + (tt + 1) * TW],
                        in0=ctx.pb[bu_][:, 0:TW],
                        in1=hut[:, :], op=ALU.mult), reads=[ctx.pbuf[bu_], hutB],
                        writes=[hbTB[tt]])
                for dto in range(ND):
                    by = ctx.bank()
                    for kt in range(4):
                        self.mm(by, ctx.pb[by][:, 0:TW],
                                wex[:, off + 8192 + kt * D + dto * 128: off + 8192 + kt * D + dto * 128 + 128],
                                hb[:, kt * T + tt * TW: kt * T + (tt + 1) * TW],
                                [hbuf, hbTB[tt]], last=(kt == 3))
                    ctx.op("v", lambda E, dto=dto, tt=tt, by=by: E.tensor_tensor(
                        out=self.xs(dto, tt), in0=self.xs(dto, tt), in1=ctx.pb[by][:, 0:TW],
                        op=ALU.add), reads=[ctx.pbuf[by]], writes=[self.xB])

        self.bcast(self.rows[:, 0:T], self.rowsB, self.bc1[:, :], self.bc1B)
        def load_shared(qq, shalf):
            soff = shalf * 12288
            ci = 0
            slots = [(self.stg[:, 0:1024], self.stgHB[0], "qw", 0),
                     (self.stg[:, 1024:2048], self.stgHB[1], "qs", 1024),
                     (self.tmpf[:, 0:1024], self.tmpfB, "qA", -1)]
            for j, wsrc in enumerate((w_sg, w_su)):
                view = wsrc.rearrange("(k p) d -> p k d", p=128)
                for c in range(4):
                    sl, slb, q, _ = slots[ci % 3]
                    for kt in range(c * 2, c * 2 + 2):
                        ctx.dma(q, sl[:, (kt % 2) * 512:(kt % 2 + 1) * 512],
                                view[:, kt, qq * 512: qq * 512 + 512], writes=[slb])
                    ctx.op("s", lambda E, sl=sl, j=j, c=c, slb=slb, soff=soff: E.copy(
                        out=wex[:, soff + j * 4096 + c * 1024: soff + j * 4096 + (c + 1) * 1024],
                        in_=sl), reads=[slb], writes=[wexHB[shalf]])
                    ci += 1
            viewd = w_sd.rearrange("(k p) d -> p k d", p=128)
            for c in range(4):
                sl, slb, q, _ = slots[ci % 3]
                ctx.dma(q, sl, viewd[:, qq * 4 + c: qq * 4 + c + 1, :], writes=[slb])
                ctx.op("s", lambda E, c=c, sl=sl, slb=slb, soff=soff: E.copy(
                    out=wex[:, soff + 8192 + c * 1024: soff + 8192 + (c + 1) * 1024], in_=sl),
                    reads=[slb], writes=[wexHB[shalf]])
                ci += 1

        if not MOE_NO_SHARED:
            load_shared(0, 0)
        for qq in (range(4) if not MOE_NO_SHARED else []):
            shalf = qq % 2
            soff = shalf * 12288
            sbuf_w = wexHB[shalf]
            if qq + 1 < 4:
                load_shared(qq + 1, 1 - shalf)
            for tt in range(NT):
                for ht in range(4):
                    bg_, bu_ = ctx.bank(), ctx.bank()
                    for kt in range(8):
                        self.mm(bg_, ctx.pb[bg_][:, 0:TW],
                                wex[:, soff + kt * 512 + ht * 128: soff + kt * 512 + ht * 128 + 128],
                                xns(kt, tt), [sbuf_w, xnB], last=(kt == 7))
                    for kt in range(8):
                        self.mm(bu_, ctx.pb[bu_][:, 0:TW],
                                wex[:, soff + 4096 + kt * 512 + ht * 128: soff + 4096 + kt * 512 + ht * 128 + 128],
                                xns(kt, tt), [sbuf_w, xnB], last=(kt == 7))
                    ctx.op("s", lambda E, bg_=bg_: E.activation(
                        out=sgt[:, :], in_=ctx.pb[bg_][:, 0:TW], func=AF.Silu),
                        reads=[ctx.pbuf[bg_]], writes=[sgtB])
                    ctx.op("v", lambda E, tt=tt: E.tensor_tensor(
                        out=hut[:, :], in0=sgt[:, :],
                        in1=self.bc1[:, tt * TW:(tt + 1) * TW], op=ALU.mult),
                        reads=[sgtB, self.bc1B], writes=[hutB])
                    ctx.op("v", lambda E, tt=tt, ht=ht, bu_=bu_: E.tensor_tensor(
                        out=hb[:, ht * T + tt * TW: ht * T + (tt + 1) * TW],
                        in0=ctx.pb[bu_][:, 0:TW],
                        in1=hut[:, :], op=ALU.mult), reads=[ctx.pbuf[bu_], hutB],
                        writes=[hbTB[tt]])
                for dto in range(ND):
                    by = ctx.bank()
                    for kt in range(4):
                        self.mm(by, ctx.pb[by][:, 0:TW],
                                wex[:, soff + 8192 + kt * D + dto * 128: soff + 8192 + kt * D + dto * 128 + 128],
                                hb[:, kt * T + tt * TW: kt * T + (tt + 1) * TW],
                                [sbuf_w, hbTB[tt]], last=(kt == 3))
                    ctx.op("v", lambda E, dto=dto, tt=tt, by=by: E.tensor_tensor(
                        out=self.xs(dto, tt), in0=self.xs(dto, tt), in1=ctx.pb[by][:, 0:TW],
                        op=ALU.add), reads=[ctx.pbuf[by]], writes=[self.xB])
        ctx.barrier()

    def emit_l1_attn(self, scope):
        ctx = self.ctx
        w_ch = self.dram("L1a_W_chargeT", (128, 8))
        w_sh = self.dram("L1a_W_shellT", (D, 64))
        w_om = self.dram("L1a_omegaT", (64, 32))
        w_cmb = self.dram("L1a_W_combineT", (320, 64))
        w_v = self.dram("L1a_W_vT", (D, D))
        w_o = self.dram("L1a_W_outT", (D, D))
        bv_c = self.dram("L1a_b_v_col", (128, 9))
        r_bch = self.dram("L1a_b_charge", (1, 1))
        r_bsh = self.dram("L1a_b_shell", (1, 64))
        r_rff = self.dram("L1a_rff_bias", (1, 32))
        r_bo = self.dram("L1a_b_out", (1, D))
        r_lt = self.dram("L1a_log_temp", (1, 1))

        vsb, vsbB = ctx.sb("vsb", [128, 16 * 1024], BF16, scope)
        phq, phqB = ctx.sb("phq", [64, T], BF16, scope)
        phk, phkB = ctx.sb("phk", [64, T], BF16, scope)
        phis, phisB = ctx.sb("phis", [64, T], BF16, scope)
        rowb, rowbB = ctx.sb("rowb2", [1, 2176], F32, scope)
        RO_BSH, RO_RFF, RO_BO, RO_BIAS, RO_M = 0, 64, 96, 1120, 2144
        bvcol, bvcolB = ctx.sb("bvcol", [128, 8], BF16, scope)
        RQ, RS = 0, 2048

        with ExitStack() as p1:
            xn, xnB = ctx.sb("xn2", [128, ND * T], BF16, p1)
            wv, wvB = ctx.sb("wv", [128, 8192], BF16, p1)
            wsm, wsmB = ctx.sb("wsm", [128, 768], BF16, p1)
            O_CH, O_SH, O_OM, O_CMB = 0, 8, 520, 552
            shl, shlB = ctx.sb("shl", [64, T], BF16, p1)

            self.load_w(w_v, 8, D, wv, wvB, 0)
            ctx.dma("qw", self.stg[:, 0:8], w_ch, writes=[self.stgB])
            ctx.op("g", lambda E: E.tensor_copy(out=wsm[:, O_CH:O_CH + 8], in_=self.stg[:, 0:8]),
                   reads=[self.stgB], writes=[wsmB])
            self.load_w(w_sh, 8, 64, wsm, wsmB, O_SH)
            ctx.dma("qw", self.stg[0:64, 0:32], w_om, writes=[self.stgB])
            ctx.op("g", lambda E: E.tensor_copy(out=wsm[0:64, O_OM:O_OM + 32],
                                                in_=self.stg[0:64, 0:32]),
                   reads=[self.stgB], writes=[wsmB])
            for kt in range(3):
                kk = 128 if kt < 2 else 64
                ctx.dma("qw", self.stg[0:kk, 64:128], w_cmb[kt * 128:kt * 128 + kk, :],
                        writes=[self.stgB])
                ctx.op("g", lambda E, kt=kt, kk=kk: E.tensor_copy(
                    out=wsm[0:kk, O_CMB + kt * 64:O_CMB + (kt + 1) * 64],
                    in_=self.stg[0:kk, 64:128]), reads=[self.stgB], writes=[wsmB])
            ctx.dma("qw", self.stg[:, 0:9], bv_c, writes=[self.stgB])
            ctx.op("g", lambda E: E.tensor_copy(out=bvcol[:, :], in_=self.stg[:, 0:8]),
                   reads=[self.stgB], writes=[bvcolB])
            ctx.op("g", lambda E: E.tensor_copy(out=self.colp[0:32, 6:7], in_=self.stg[0:32, 8:9]),
                   reads=[self.stgB], writes=[self.colpB])
            for r_d, off, w in ((r_bsh, RO_BSH, 64), (r_rff, RO_RFF, 32), (r_bo, RO_BO, D),
                                (r_bch, RO_M, 1), (r_lt, RO_M + 1, 1)):
                ctx.dma("qw", rowb[:, off:off + w], r_d, writes=[rowbB])

            self.emit_ln(xn, xnB)

            def xns(dt, tt):
                return xn[:, dt * T + tt * TW: dt * T + (tt + 1) * TW]

            for i in range(16):
                for dh in range(2):
                    bi = ctx.bank()
                    for kt in range(8):
                        self.mm(bi, ctx.pb[bi][:, 0:TW],
                                xn[:, kt * T + i * 128: kt * T + (i + 1) * 128],
                                wv[:, kt * D + dh * TW: kt * D + (dh + 1) * TW],
                                [xnB, wvB], last=(kt == 7))
                    ctx.op("s", lambda E, i=i, dh=dh, bi=bi: E.copy(
                        out=vsb[:, i * 1024 + dh * TW: i * 1024 + (dh + 1) * TW],
                        in_=ctx.pb[bi][:, 0:TW]), reads=[ctx.pbuf[bi]], writes=[vsbB])

            for tt in range(NT):
                bi = ctx.bank()
                for kt in range(8):
                    self.mm(bi, ctx.pb[bi][0:1, 0:TW], wsm[:, O_CH + kt:O_CH + kt + 1],
                            xns(kt, tt), [wsmB, xnB])
                self.mm(bi, ctx.pb[bi][0:1, 0:TW], rowb[:, RO_M:RO_M + 1], self.onesf[:, :],
                        [rowbB, self.onesfB], last=True)
                ctx.op("s", lambda E, tt=tt, bi=bi: E.activation(
                    out=self.rows[:, RQ + tt * TW:RQ + (tt + 1) * TW], in_=ctx.pb[bi][0:1, 0:TW],
                    func=AF.Tanh), reads=[ctx.pbuf[bi]], writes=[self.rowsB])

            sqv = self.bc1.ap().bitcast(BF16)
            for tt in range(NT):
                bi = ctx.bank()
                for kt in range(8):
                    self.mm(bi, ctx.pb[bi][0:64, 0:TW], wsm[:, O_SH + kt * 64:O_SH + (kt + 1) * 64],
                            xns(kt, tt), [wsmB, xnB])
                self.mm(bi, ctx.pb[bi][0:64, 0:TW], rowb[:, RO_BSH:RO_BSH + 64], self.onesf[:, :],
                        [rowbB, self.onesfB], last=True)
                ctx.op("s", lambda E, tt=tt, bi=bi: E.copy(
                    out=shl[:, tt * TW:(tt + 1) * TW], in_=ctx.pb[bi][0:64, 0:TW]),
                    reads=[ctx.pbuf[bi]], writes=[shlB])
                ctx.op("s", lambda E, tt=tt, bi=bi: E.activation(
                    out=sqv[0:64, tt * TW:(tt + 1) * TW], in_=ctx.pb[bi][0:64, 0:TW],
                    func=AF.Square), reads=[ctx.pbuf[bi]], writes=[self.bc1B])
                bj = ctx.bank()
                self.mm(bj, ctx.pb[bj][0:1, 0:TW], self.onescb[0:64, :],
                        sqv[0:64, tt * TW:(tt + 1) * TW], [self.onescbB, self.bc1B], last=True)
                ctx.op("s", lambda E, tt=tt, bj=bj: E.activation(
                    out=self.rows[:, RS + tt * TW:RS + (tt + 1) * TW], in_=ctx.pb[bj][0:1, 0:TW],
                    func=AF.Sqrt), reads=[ctx.pbuf[bj]], writes=[self.rowsB])
            ctx.op("v", lambda E: E.tensor_scalar(
                out=self.rows[:, RS:RS + T], in0=self.rows[:, RS:RS + T], scalar1=1e-12,
                scalar2=None, op0=ALU.max), reads=[self.rowsB], writes=[self.rowsB])
            ctx.op("v", lambda E: E.reciprocal(
                out=self.rows[:, RS:RS + T], in_=self.rows[:, RS:RS + T]),
                reads=[self.rowsB], writes=[self.rowsB])
            self.bcast(self.rows[:, RS:RS + T], self.rowsB, self.bc1[0:32, 0:T], self.bc1B,
                       nrows=32)
            prj = self.tmpf[0:32, 0:T]
            for tt in range(NT):
                bi = ctx.bank()
                self.mm(bi, ctx.pb[bi][0:32, 0:TW], wsm[0:64, O_OM:O_OM + 32],
                        shl[:, tt * TW:(tt + 1) * TW], [wsmB, shlB], last=True)
                ctx.op("v", lambda E, tt=tt, bi=bi: E.tensor_tensor(
                    out=prj[:, tt * TW:(tt + 1) * TW], in0=ctx.pb[bi][0:32, 0:TW],
                    in1=self.bc1[0:32, tt * TW:(tt + 1) * TW], op=ALU.mult),
                    reads=[ctx.pbuf[bi], self.bc1B], writes=[self.tmpfB])
            ctx.op("v", lambda E: E.tensor_scalar(
                out=prj[:, :], in0=prj[:, :], scalar1=self.colp[0:32, 6:7], scalar2=None,
                op0=ALU.add), reads=[self.tmpfB, self.colpB], writes=[self.tmpfB])
            prji = self.bc2.ap().bitcast(I32)[32:64, 0:T]
            tred = self.tmpf[32:64, 0:T]
            inv2pi = 1.0 / (2 * math.pi)
            for rowlo, shift in ((32, 0.0), (0, 0.25)):
                ctx.op("v", lambda E, shift=shift, tred=tred: E.tensor_scalar(
                    out=tred, in0=prj[:, :], scalar1=inv2pi, scalar2=shift,
                    op0=ALU.mult, op1=ALU.add), reads=[self.tmpfB], writes=[self.tmpfB])
                ctx.op("v", lambda E, tred=tred: E.tensor_scalar(
                    out=prji[:, :], in0=tred, scalar1=-0.5, scalar2=None,
                    op0=ALU.add), reads=[self.tmpfB], writes=[self.bc2B])
                ctx.op("v", lambda E, tred=tred: E.tensor_tensor(
                    out=tred, in0=tred, in1=prji[:, :],
                    op=ALU.subtract), reads=[self.tmpfB, self.bc2B], writes=[self.tmpfB])
                ctx.op("s", lambda E, rowlo=rowlo, tred=tred: E.activation(
                    out=phis[rowlo:rowlo + 32, 0:T], in_=tred, func=AF.Sin,
                    scale=2 * math.pi, bias=-math.pi), reads=[self.tmpfB], writes=[phisB])

        ctx.barrier()
        with ExitStack() as p1b:
            tp1, tp1B = ctx.sb("tp1", [128, T], BF16, p1b)
            tp2, tp2B = ctx.sb("tp2", [128, T], BF16, p1b)
            tp3, tp3B = ctx.sb("tp3", [64, T], BF16, p1b)
            wcm, wcmB = ctx.sb("wcm", [128, 192], BF16, p1b)
            for kt in range(3):
                kk = 128 if kt < 2 else 64
                ctx.dma("qw", self.stg[0:kk, 0:64], w_cmb[kt * 128:kt * 128 + kk, :],
                        writes=[self.stgB])
                ctx.op("g", lambda E, kt=kt, kk=kk: E.tensor_copy(
                    out=wcm[0:kk, kt * 64:(kt + 1) * 64], in_=self.stg[0:kk, 0:64]),
                    reads=[self.stgB], writes=[wcmB])
            scl = [-RFF_SCALE * m for m in MAGS]
            targets = [(tp1, tp1B, 0), (tp1, tp1B, 64), (tp2, tp2B, 0), (tp2, tp2B, 64),
                       (tp3, tp3B, 0)]
            for m in range(5):
                tph, tphB, prow = targets[m]
                if m == 0:
                    ctx.op("v", lambda E: E.memset(self.rows[:, RS:RS + T], scl[0]),
                           writes=[self.rowsB])
                else:
                    if m == 1:
                        ctx.op("v", lambda E: E.tensor_copy(
                            out=self.tmpf[0:1, 0:T], in_=self.rows[:, RQ:RQ + T]),
                            reads=[self.rowsB], writes=[self.tmpfB])
                    else:
                        ctx.op("v", lambda E: E.tensor_tensor(
                            out=self.tmpf[0:1, 0:T], in0=self.tmpf[0:1, 0:T],
                            in1=self.rows[:, RQ:RQ + T], op=ALU.mult),
                            reads=[self.tmpfB, self.rowsB], writes=[self.tmpfB])
                    ctx.op("v", lambda E, m=m: E.tensor_scalar(
                        out=self.rows[:, RS:RS + T], in0=self.tmpf[0:1, 0:T], scalar1=scl[m],
                        scalar2=None, op0=ALU.mult), reads=[self.tmpfB], writes=[self.rowsB])
                self.bcast(self.rows[:, RS:RS + T], self.rowsB, self.bc2[0:64, 0:T],
                           self.bc2B, nrows=64)
                ctx.op("v", lambda E, tph=tph, prow=prow: E.tensor_tensor(
                    out=tph[prow:prow + 64, 0:T], in0=phis[:, :], in1=self.bc2[0:64, 0:T],
                    op=ALU.mult), reads=[phisB, self.bc2B], writes=[tphB])
            if DBG:
                dtp = self.dram("dbg_tp1", (128, T), BF16, kind="ExternalOutput")
                self.ctx.dma("qx", dtp, tp1[:, :], reads=[tp1B])
                dtp2 = self.dram("dbg_tp2", (128, T), BF16, kind="ExternalOutput")
                self.ctx.dma("qx", dtp2, tp2[:, :], reads=[tp2B])
                dtp3 = self.dram("dbg_tp3", (64, T), BF16, kind="ExternalOutput")
                self.ctx.dma("qx", dtp3, tp3[:, :], reads=[tp3B])
                dwcm = self.dram("dbg_wcm", (128, 192), BF16, kind="ExternalOutput")
                self.ctx.dma("qx", dwcm, wcm[:, :], reads=[wcmB])
            phiT = self.tmpf[64:128, 0:T]
            for tt in range(NT):
                bi = ctx.bank()
                srcs = [(tp1, tp1B, 128, 0), (tp2, tp2B, 128, 1), (tp3, tp3B, 64, 2)]
                for j, (tph, tphB, kk, kt) in enumerate(srcs):
                    self.mm(bi, ctx.pb[bi][0:64, 0:TW], wcm[0:kk, kt * 64:(kt + 1) * 64],
                            tph[0:kk, tt * TW:(tt + 1) * TW], [tphB, wcmB], last=(j == 2))
                ctx.op("v", lambda E, tt=tt, bi=bi: E.tensor_copy(
                    out=phiT[:, tt * TW:(tt + 1) * TW], in_=ctx.pb[bi][0:64, 0:TW]),
                    reads=[ctx.pbuf[bi]], writes=[self.tmpfB])

            M = RO_M
            ctx.op("s", lambda E: E.activation(
                out=rowb[:, M + 2:M + 3], in_=rowb[:, M + 1:M + 2], func=AF.Abs),
                reads=[rowbB], writes=[rowbB])
            ctx.op("s", lambda E: E.activation(
                out=rowb[:, M + 3:M + 4], in_=rowb[:, M + 2:M + 3], func=AF.Exp, scale=-1.0),
                reads=[rowbB], writes=[rowbB])
            ctx.op("s", lambda E: E.activation(
                out=rowb[:, M + 3:M + 4], in_=rowb[:, M + 3:M + 4], func=AF.Ln, bias=1.0,
                scale=1.0), reads=[rowbB], writes=[rowbB])
            ctx.op("v", lambda E: E.tensor_scalar(
                out=rowb[:, M + 4:M + 5], in0=rowb[:, M + 1:M + 2], scalar1=0.0, scalar2=None,
                op0=ALU.max), reads=[rowbB], writes=[rowbB])
            ctx.op("v", lambda E: E.tensor_tensor(
                out=rowb[:, M + 4:M + 5], in0=rowb[:, M + 4:M + 5], in1=rowb[:, M + 3:M + 4],
                op=ALU.add), reads=[rowbB], writes=[rowbB])
            ctx.op("v", lambda E: E.tensor_scalar(
                out=rowb[:, M + 4:M + 5], in0=rowb[:, M + 4:M + 5], scalar1=0.1, scalar2=None,
                op0=ALU.add), reads=[rowbB], writes=[rowbB])
            ctx.op("v", lambda E: E.reciprocal(out=rowb[:, M + 5:M + 6], in_=rowb[:, M + 4:M + 5]),
                   reads=[rowbB], writes=[rowbB])
            ctx.dma("qw", self.rows[:, RS:RS + T], self.ramps_d[0:1, :], writes=[self.rowsB])
            ctx.op("v", lambda E: E.tensor_scalar(
                out=self.rows[:, RS:RS + T], in0=self.rows[:, RS:RS + T],
                scalar1=rowb[:, M + 5:M + 6], scalar2=None, op0=ALU.mult),
                reads=[self.rowsB, rowbB], writes=[self.rowsB])
            self.bcast(self.rows[:, RS:RS + T], self.rowsB, self.bc1[64:128, 0:T], self.bc1B,
                       nrows=64)
            ctx.dma("qw", self.rows[:, RS:RS + T], self.ramps_d[1:2, :], writes=[self.rowsB])
            self.bcast(self.rows[:, RS:RS + T], self.rowsB, self.bc2[64:128, 0:T], self.bc2B,
                       nrows=64)
            ctx.op("v", lambda E: E.tensor_tensor(
                out=phq[:, :], in0=phiT[:, :], in1=self.bc1[64:128, 0:T], op=ALU.mult),
                reads=[self.tmpfB, self.bc1B], writes=[phqB])
            ctx.op("v", lambda E: E.tensor_tensor(
                out=phk[:, :], in0=phiT[:, :], in1=self.bc2[64:128, 0:T], op=ALU.mult),
                reads=[self.tmpfB, self.bc2B], writes=[phkB])

        if DBG:
            dprj = self.dram("dbg_prj", (32, T), kind="ExternalOutput")
            self.ctx.dma("qx", dprj, self.tmpf[0:32, 0:T], reads=[self.tmpfB])
            dphiT = self.dram("dbg_phiT", (64, T), kind="ExternalOutput")
            self.ctx.dma("qx", dphiT, self.tmpf[64:128, 0:T], reads=[self.tmpfB])
            dbc1 = self.dram("dbg_rampq", (64, T), kind="ExternalOutput")
            self.ctx.dma("qx", dbc1, self.bc1[64:128, 0:T], reads=[self.bc1B])
            for nm, th, tb, pp in (("dbg_phq", phq, phqB, 64), ("dbg_phk", phk, phkB, 64),
                                   ("dbg_phis", phis, phisB, 64)):
                dap = self.dram(nm, (pp, T), BF16, kind="ExternalOutput")
                self.ctx.dma("qx", dap, th[0:pp, 0:T], reads=[tb])
            dq = self.dram("dbg_qrow", (1, T), kind="ExternalOutput")
            self.ctx.dma("qx", dq, self.rows[:, RQ:RQ + T], reads=[self.rowsB])

        ctx.barrier()
        with ExitStack() as p2:
            wo, woB = ctx.sb("wo", [128, 8192], BF16, p2)
            expt, exptB = ctx.sb("expt", [128, 16 * 128], BF16, p2)
            otq, otqB = ctx.sb("otq", [128, 1024], BF16, p2)
            self.load_w(w_o, 8, D, wo, woB, 0)
            for half in range(2):
                bi = ctx.bank()
                for kt in range(8):
                    self.mm(bi, ctx.pb[bi][0:1, 0:TW], bvcol[:, kt:kt + 1],
                            wo[:, kt * D + half * TW: kt * D + (half + 1) * TW],
                            [bvcolB, woB], last=(kt == 7))
                ctx.op("v", lambda E, half=half, bi=bi: E.tensor_copy(
                    out=rowb[:, RO_BIAS + half * TW:RO_BIAS + (half + 1) * TW],
                    in_=ctx.pb[bi][0:1, 0:TW]), reads=[ctx.pbuf[bi]], writes=[rowbB])
            ctx.op("v", lambda E: E.tensor_tensor(
                out=rowb[:, RO_BIAS:RO_BIAS + D], in0=rowb[:, RO_BIAS:RO_BIAS + D],
                in1=rowb[:, RO_BO:RO_BO + D], op=ALU.add), reads=[rowbB], writes=[rowbB])

            for qt in range(16):
                for kt in range(qt + 1):
                    bl = ctx.bank()
                    self.mm(bl, ctx.pb[bl][:, 0:128], phk[:, kt * 128:(kt + 1) * 128],
                            phq[:, qt * 128:(qt + 1) * 128], [phkB, phqB], last=True)
                    ctx.op("s", lambda E, kt=kt, bl=bl: E.activation(
                        out=expt[:, kt * 128:(kt + 1) * 128], in_=ctx.pb[bl][:, 0:128],
                        func=AF.Exp), reads=[ctx.pbuf[bl]], writes=[exptB])
                    if kt == qt:
                        ctx.op("v", lambda E, kt=kt: E.tensor_tensor(
                            out=expt[:, kt * 128:(kt + 1) * 128],
                            in0=expt[:, kt * 128:(kt + 1) * 128], in1=self.trib[:, :],
                            op=ALU.mult), reads=[exptB, self.tribB], writes=[exptB])
                bs = ctx.bank()
                for kt in range(qt + 1):
                    self.mm(bs, ctx.pb[bs][0:1, 0:128], self.onescb[:, :],
                            expt[:, kt * 128:(kt + 1) * 128], [self.onescbB, exptB],
                            last=(kt == qt))
                ctx.op("v", lambda E, qt=qt, bs=bs: E.tensor_copy(
                    out=self.rows[:, RS + qt * 128:RS + (qt + 1) * 128],
                    in_=ctx.pb[bs][0:1, 0:128]), reads=[ctx.pbuf[bs]], writes=[self.rowsB])
                ctx.op("v", lambda E, qt=qt: E.reciprocal(
                    out=self.rows[:, RS + qt * 128:RS + (qt + 1) * 128],
                    in_=self.rows[:, RS + qt * 128:RS + (qt + 1) * 128]),
                    reads=[self.rowsB], writes=[self.rowsB])
                self.bcast(self.rows[:, RS + qt * 128:RS + (qt + 1) * 128], self.rowsB,
                           self.bc1[:, qt * 128:(qt + 1) * 128], self.bc1B, width=128)
                for bq in range(2):
                    bo = ctx.bank()
                    for kt in range(qt + 1):
                        for dv in range(4):
                            dvg = bq * 4 + dv
                            self.mm(bo, ctx.pb[bo][:, dv * 128:(dv + 1) * 128],
                                    vsb[:, kt * 1024 + dvg * 128: kt * 1024 + (dvg + 1) * 128],
                                    expt[:, kt * 128:(kt + 1) * 128], [vsbB, exptB],
                                    last=(kt == qt and dv == 3))
                    for dv in range(4):
                        dvg = bq * 4 + dv
                        ctx.op("v", lambda E, dv=dv, dvg=dvg, bo=bo, qt=qt: E.tensor_tensor(
                            out=otq[:, dvg * 128:(dvg + 1) * 128],
                            in0=ctx.pb[bo][:, dv * 128:(dv + 1) * 128],
                            in1=self.bc1[:, qt * 128:(qt + 1) * 128], op=ALU.mult),
                            reads=[ctx.pbuf[bo], self.bc1B], writes=[otqB])
                for bq in range(2):
                    bz = ctx.bank()
                    for dtl in range(4):
                        dt = bq * 4 + dtl
                        for kv in range(8):
                            self.mm(bz, ctx.pb[bz][:, dtl * 128:(dtl + 1) * 128],
                                    wo[:, kv * D + dt * 128: kv * D + dt * 128 + 128],
                                    otq[:, kv * 128:(kv + 1) * 128], [woB, otqB])
                        self.mm(bz, ctx.pb[bz][:, dtl * 128:(dtl + 1) * 128],
                                rowb[:, RO_BIAS + dt * 128:RO_BIAS + dt * 128 + 128],
                                self.onesf[:, 0:128], [rowbB, self.onesfB],
                                last=(dtl == 3))
                    for dtl in range(4):
                        dt = bq * 4 + dtl
                        ctx.op("v", lambda E, dt=dt, dtl=dtl, qt=qt, bz=bz: E.tensor_tensor(
                            out=self.x[:, dt * T + qt * 128: dt * T + (qt + 1) * 128],
                            in0=self.x[:, dt * T + qt * 128: dt * T + (qt + 1) * 128],
                            in1=ctx.pb[bz][:, dtl * 128:(dtl + 1) * 128], op=ALU.add),
                            reads=[ctx.pbuf[bz]], writes=[self.xB])
        ctx.barrier()

    def emit_l1_ffn(self, scope):
        ctx = self.ctx
        w_f = self.dram("L1f_weightT", (D, D))
        r_bf = self.dram("L1f_bias", (1, D))
        r_al = self.dram("L1f_alpha", (1, 1))

        xn, xnB = ctx.sb("xn3", [128, ND * T], BF16, scope)
        wstg, wstgB = ctx.sb("wf32", [128, 8192], F32, scope)
        wt, wtB = ctx.sb("wtern", [128, 8192], BF16, scope)
        sgn, sgnB = ctx.sb("sgn1f", [128, 8192], BF16, scope)
        rowb, rowbB = ctx.sb("rowb3", [1, 2048], F32, scope)

        view = w_f.rearrange("(k p) d -> p k d", p=128)
        for c in range(4):
            ctx.dma("qw", wstg[:, c * 2048:(c + 1) * 2048], view[:, c * 2:(c + 1) * 2, :],
                    writes=[wstgB])
        ctx.dma("qw", rowb[:, 0:D], r_bf, writes=[rowbB])
        ctx.dma("qw", rowb[:, D:D + 1], r_al, writes=[rowbB])

        def xns(dt, tt):
            return xn[:, dt * T + tt * TW: dt * T + (tt + 1) * TW]

        ctx.op("v", lambda E: E.tensor_reduce(
            out=self.colp[:, 0:1], in_=wstg[:, :], axis=AX, op=ALU.add,
            apply_absolute_value=True), reads=[wstgB], writes=[self.colpB])
        bi = ctx.bank()
        self.mm(bi, ctx.pb[bi][0:1, 0:1], self.colp[:, 0:1], self.onescf[:, :],
                [self.colpB, self.onescfB], last=True)
        ctx.op("v", lambda E, bi=bi: E.tensor_scalar(
            out=rowb[:, D + 2:D + 3], in0=ctx.pb[bi][0:1, 0:1], scalar1=0.7 / (D * D),
            scalar2=None, op0=ALU.mult), reads=[ctx.pbuf[bi]], writes=[rowbB])
        ctx.op("v", lambda E: E.reciprocal(out=rowb[:, D + 3:D + 4], in_=rowb[:, D:D + 1]),
               reads=[rowbB], writes=[rowbB])
        bj = ctx.bank()
        self.mm(bj, ctx.pb[bj][:, 0:1], self.onesf[:, 0:128], rowb[:, D + 2:D + 3],
                [self.onesfB, rowbB], last=True)
        ctx.op("v", lambda E, bj=bj: E.tensor_copy(out=self.colp[:, 1:2], in_=ctx.pb[bj][:, 0:1]),
               reads=[ctx.pbuf[bj]], writes=[self.colpB])
        bj = ctx.bank()
        self.mm(bj, ctx.pb[bj][:, 0:1], self.onesf[:, 0:128], rowb[:, D:D + 1],
                [self.onesfB, rowbB], last=True)
        ctx.op("v", lambda E, bj=bj: E.tensor_copy(out=self.colp[:, 2:3], in_=ctx.pb[bj][:, 0:1]),
               reads=[ctx.pbuf[bj]], writes=[self.colpB])
        ctx.op("s", lambda E: E.activation(out=sgn[:, :], in_=wstg[:, :], func=AF.Sign),
               reads=[wstgB], writes=[sgnB])
        ctx.op("s", lambda E: E.activation(out=wstg[:, :], in_=wstg[:, :], func=AF.Abs),
               reads=[wstgB], writes=[wstgB])
        ctx.op("v", lambda E: E.tensor_scalar(
            out=wstg[:, :], in0=wstg[:, :], scalar1=self.colp[:, 1:2], scalar2=None,
            op0=ALU.is_gt), reads=[wstgB, self.colpB], writes=[wstgB])
        ctx.op("v", lambda E: E.tensor_tensor(
            out=wt[:, :], in0=sgn[:, :], in1=wstg[:, :], op=ALU.mult),
            reads=[sgnB, wstgB], writes=[wtB])
        ctx.op("v", lambda E: E.tensor_scalar(
            out=rowb[:, 0:D], in0=rowb[:, 0:D], scalar1=rowb[:, D + 3:D + 4], scalar2=None,
            op0=ALU.mult), reads=[rowbB], writes=[rowbB])
        self.emit_ln(xn, xnB)
        for dt in range(ND):
            for tt in range(NT):
                bi = ctx.bank()
                for kt in range(8):
                    self.mm(bi, ctx.pb[bi][:, 0:TW],
                            wt[:, kt * D + dt * 128: kt * D + dt * 128 + 128], xns(kt, tt),
                            [wtB, xnB])
                self.mm(bi, ctx.pb[bi][:, 0:TW], rowb[:, dt * 128:dt * 128 + 128],
                        self.onesf[:, :], [rowbB, self.onesfB], last=True)
                ctx.op("v", lambda E, dt=dt, tt=tt, bi=bi: E.scalar_tensor_tensor(
                    out=self.xs(dt, tt), in0=ctx.pb[bi][:, 0:TW], scalar=self.colp[:, 2:3],
                    in1=self.xs(dt, tt), op0=ALU.mult, op1=ALU.add),
                    reads=[ctx.pbuf[bi], self.colpB], writes=[self.xB])
        ctx.barrier()


BLOCKS = {
    0: lambda m, sc: m.emit_l0_attn(sc),
    1: lambda m, sc: m.emit_l0_ffn(sc),
    2: lambda m, sc: m.emit_l1_attn(sc),
    3: lambda m, sc: m.emit_l1_ffn(sc),
}

LAUNCH_PLAN = [[0, 1, 2, 3]]
DBG = False
SCAN_NO_CARRY = False
MOE_NO_EXPERTS = False
MOE_ONE_EXPERT = None
MOE_SERIAL = False
MOE_NUM = NEXP
MOE_NO_SHARED = False


def build_program(block_ids):
    nc = bass.Bass("TRN2", target_bir_lowering=False, dynamic_dma_scratch_size=2048)
    m = Module(nc)
    with m.ctx.es:
        m.setup()
        m.ctx.barrier()
        for b in block_ids:
            with ExitStack() as scope:
                BLOCKS[b](m, scope)
        m.finish()
    return nc, m.declared


def _tr(a):
    return np.ascontiguousarray(np.asarray(a, np.float32).T)


def _row(a):
    return np.ascontiguousarray(np.asarray(a, np.float32).reshape(1, -1))


def prep_weights(inputs):
    g = {}
    g["L0a_s_w1T"] = _tr(inputs["L0_attn_s_w1"])
    g["L0a_s_w2T"] = _tr(inputs["L0_attn_s_w2"])
    g["L0a_B_wT"] = _tr(inputs["L0_attn_B_w"])
    g["L0a_C_wT"] = _tr(inputs["L0_attn_C_w"])
    g["L0a_D_wT"] = _tr(inputs["L0_attn_D_w"])
    g["L0a_g_wT"] = _tr(inputs["L0_attn_g_w"])
    g["L0a_A"] = np.ascontiguousarray(np.asarray(inputs["L0_attn_A"], np.float32))
    for nm in ("s_b1", "s_b2", "B_b", "C_b", "D_b", "g_b"):
        g["L0a_" + nm] = _row(inputs["L0_attn_" + nm])
    g["L0f_WgT"] = _tr(inputs["L0_ffn_Wg"])
    for nm, src in (("We_gateT", "We_gate"), ("We_upT", "We_up"), ("We_downT", "We_down")):
        g["L0f_" + nm] = np.ascontiguousarray(
            np.swapaxes(np.asarray(inputs["L0_ffn_" + src], np.float32), 1, 2))
    g["L0f_Ws_gateT"] = _tr(inputs["L0_ffn_Ws_gate"])
    g["L0f_Ws_upT"] = _tr(inputs["L0_ffn_Ws_up"])
    g["L0f_Ws_downT"] = _tr(inputs["L0_ffn_Ws_down"])
    g["L0f_WsgT"] = np.ascontiguousarray(
        np.asarray(inputs["L0_ffn_Wsg"], np.float32).reshape(8, 128).T)
    g["L1a_W_chargeT"] = np.ascontiguousarray(
        np.asarray(inputs["L1_attn_W_charge"], np.float32).reshape(8, 128).T)
    g["L1a_W_shellT"] = _tr(inputs["L1_attn_W_shell"])
    g["L1a_omegaT"] = _tr(inputs["L1_attn_omega"])
    g["L1a_W_combineT"] = _tr(inputs["L1_attn_W_combine"])
    g["L1a_W_vT"] = _tr(inputs["L1_attn_W_v"])
    g["L1a_W_outT"] = _tr(inputs["L1_attn_W_out"])
    bvc = np.zeros((128, 9), np.float32)
    bvc[:, 0:8] = np.asarray(inputs["L1_attn_b_v"], np.float32).reshape(8, 128).T
    bvc[0:32, 8] = np.asarray(inputs["L1_attn_rff_bias"], np.float32).ravel()
    g["L1a_b_v_col"] = np.ascontiguousarray(bvc)
    g["L1a_b_charge"] = _row(inputs["L1_attn_b_charge"])
    g["L1a_b_shell"] = _row(inputs["L1_attn_b_shell"])
    g["L1a_rff_bias"] = _row(inputs["L1_attn_rff_bias"])
    g["L1a_b_out"] = _row(inputs["L1_attn_b_out"])
    g["L1a_log_temp"] = _row(inputs["L1_attn_log_temp"])
    g["L1f_weightT"] = _tr(inputs["L1_ffn_weight"])
    g["L1f_bias"] = _row(inputs["L1_ffn_bias"])
    g["L1f_alpha"] = _row(inputs["L1_ffn_alpha"])
    g["c_idf"] = np.eye(16, dtype=np.float32)
    g["c_idb"] = np.eye(128, dtype=ml_dtypes.bfloat16)
    g["c_trib"] = (np.arange(128)[:, None] <= np.arange(128)[None, :]).astype(ml_dtypes.bfloat16)
    t_idx = np.arange(T, dtype=np.float64)
    c = 0.3 / 10.0
    g["c_ramps"] = np.stack([np.exp(-c * t_idx), np.exp(c * t_idx)]).astype(np.float32)
    return g


def kernel(**inputs):
    x = np.asarray(inputs["L0_attn_x"], np.float32)
    g = prep_weights(inputs)
    xcur = [np.ascontiguousarray(x[b].T) for b in range(B)]
    for launch in LAUNCH_PLAN:
        nc, declared = build_program(launch)
        in_maps = []
        for b in range(B):
            mp = {"xin": xcur[b]}
            for name, (shape, dtype, kind) in declared.items():
                if kind == "ExternalInput" and name != "xin":
                    mp[name] = g[name]
            in_maps.append(mp)
        res = run_bass_kernel_spmd(nc, in_maps, core_ids=list(range(B)))
        xcur = [res.results[b]["xout"] for b in range(B)]
    out = np.stack([xc.T for xc in xcur], axis=0).astype(np.float32)
    return np.ascontiguousarray(out)
